# version 02
# baseline (speedup 1.0000x reference)
import math
from contextlib import ExitStack

import numpy as np
import ml_dtypes

import concourse.bass as bass
import concourse.mybir as mybir
from concourse.bass_utils import run_bass_kernel_spmd

F32 = mybir.dt.float32
BF16 = mybir.dt.bfloat16
AF = mybir.ActivationFunctionType
ALU = mybir.AluOpType
AX = mybir.AxisListType
BF = ml_dtypes.bfloat16

NCORES = 8
T = 1024
TC = T // 128
D = 2048
DC = D // 128

RG_PAIRS = [[0, 1], [2, 3], [4, 5], [6, 7]]
RG_ALL = [[0, 1, 2, 3, 4, 5, 6, 7]]



def rhs_pack(w):
    K, N = w.shape
    assert K % 128 == 0 and N % 512 == 0
    return np.ascontiguousarray(
        w.reshape(K // 128, 128, N // 512, 512).transpose(2, 1, 0, 3))


def lhsT_pack(w):
    K, M = w.shape
    assert K % 128 == 0 and M % 128 == 0
    return np.ascontiguousarray(
        w.reshape(K // 128, 128, M // 128, 128).transpose(2, 1, 0, 3))


def col_pack(v):
    return np.ascontiguousarray(np.asarray(v, np.float32).reshape(-1, 128).T)



class _Noop:
    def then_inc(self, *a, **k):
        return self


NOOP = _Noop()


class Sem:
    __slots__ = ("h", "v")

    def __init__(self, h):
        self.h = h
        self.v = 0


_DMA_METHODS = {"dma_start", "indirect_dma_start", "dma_start_transpose"}
_NOAUTO_METHODS = {"wait_ge", "nop", "collective_compute"}


class Eng:

    __slots__ = ("real", "chain", "auto")

    def __init__(self, real, chain=None, auto=False):
        self.real = real
        self.chain = chain
        self.auto = auto

    def __getattr__(self, name):
        real = self.real
        chain = self.chain
        if chain is None or name in _NOAUTO_METHODS or name in _DMA_METHODS:
            if real is None:
                return lambda *a, **k: NOOP
            return getattr(real, name)
        auto = self.auto

        def call(*a, **k):
            prev = chain.v
            chain.v += 1
            if real is None:
                return NOOP
            if auto and prev > 0:
                real.wait_ge(chain.h, prev)
            inst = getattr(real, name)(*a, **k)
            inst.then_inc(chain.h, 1)
            return inst
        return call


class Ctx:

    def __init__(self, nc, which, eng, chains):
        self.nc = nc
        self.which = which
        self.chains = chains
        self.t = Eng(eng if which == "t" else None)
        self.v = Eng(eng if which == "v" else None, chains["v"], True)
        self.s = Eng(eng if which == "s" else None, chains["s"], True)
        self.g = Eng(eng if which == "g" else None, chains["g"], True)
        self.y = Eng(eng if which == "y" else None)

    def inc(self, inst, sem, n=1):
        sem.v += n
        inst.then_inc(sem.h, n)
        return sem.v

    def inc_cc(self, inst, sem):
        sem.v += 1
        inst.then_inc(sem.h)
        return sem.v

    def wait(self, eng, sem, v=None):
        eng.wait_ge(sem.h, sem.v if v is None else v)

    def dma(self, eng, sem, **kw):
        inst = eng.dma_start(**kw)
        sem.v += 16
        inst.then_inc(sem.h, 16)
        return (sem, sem.v)

    def mark(self, engine_char):
        c = self.chains[engine_char]
        return (c, c.v)

    def waitm(self, eng, marker):
        if marker is None:
            return
        sem, v = marker
        if v > 0:
            eng.wait_ge(sem.h, v)


ENG_ORDER = [("y", "sync"), ("g", "gpsimd"), ("t", "tensor"), ("s", "scalar"),
             ("v", "vector")]


def run_walk(nc, block, sems, walk, resettables=()):
    for which, attr in ENG_ORDER:
        for s in sems.values():
            s.v = 0
        for r in resettables:
            r.reset()

        def body(eng, which=which):
            walk(Ctx(nc, which, eng, {k: sems["ch_" + k] for k in "vsg"}))

        getattr(block, attr)(body)


def make_sems(nc, st):
    sems = {}
    for k in "vsg":
        sems["ch_" + k] = Sem(st.enter_context(nc.semaphore("ch_" + k)))
    return sems



CONST_VALS = [0.0, 1.0, -1.0, 128.0, -128.0, 1e-5, 1e-6,
              256.0, 512.0, 768.0]


class Common:

    def __init__(self, nc, st):
        self.nc = nc
        vals = list(dict.fromkeys(CONST_VALS))
        self.cmap = {v: i for i, v in enumerate(vals)}
        carr = np.zeros((128, len(vals)), np.float32)
        for v, i in self.cmap.items():
            carr[:, i] = v
        self.c_dram = nc.inline_tensor(carr, "const_cols")
        self.ident_dram = nc.inline_tensor(np.eye(128, dtype=BF), "ident_bf")
        self.identf_dram = nc.inline_tensor(np.eye(128, dtype=np.float32),
                                            "ident_f32")
        self.ccol = st.enter_context(
            nc.sbuf_tensor("const_cols_sb", [128, len(vals)], F32))
        self.ident = st.enter_context(
            nc.sbuf_tensor("ident_sb", [128, 128], BF16))
        self.identf = st.enter_context(
            nc.sbuf_tensor("identf_sb", [128, 128], F32))
        self.ones_row_dram = nc.inline_tensor(np.ones((1, 128), np.float32),
                                              "ones_row")
        self.ones_row = st.enter_context(
            nc.sbuf_tensor("ones_row_sb", [1, 128], F32))

    def cc(self, val):
        i = self.cmap[float(val)]
        return self.ccol[:, i:i + 1]

    def load(self, p, sem):
        p.dma(p.y, sem, out=self.ccol[:, :], in_=self.c_dram[:, :])
        p.dma(p.y, sem, out=self.ident[:, :], in_=self.ident_dram[:, :])
        p.dma(p.y, sem, out=self.identf[:, :], in_=self.identf_dram[:, :])
        m = p.dma(p.y, sem, out=self.ones_row[:, :], in_=self.ones_row_dram[:, :])
        return m


class PsumPool:

    def __init__(self, nc, st, sems, name, nslots, shape=(128, 512), dtype=F32):
        self.nslots = nslots
        self.tiles = [st.enter_context(
            nc.psum_tensor(f"{name}{i}", list(shape), dtype))
            for i in range(nslots)]
        self.wrote = []
        for i in range(nslots):
            s = Sem(st.enter_context(nc.semaphore(f"{name}{i}w")))
            sems[f"{name}{i}w"] = s
            self.wrote.append(s)
        self.freed = [None] * nslots
        self.idx = 0

    def reset(self):
        self.idx = 0
        self.freed = [None] * self.nslots

    def next(self):
        i = self.idx
        self.idx = (self.idx + 1) % self.nslots
        return i


def mm_group(p, pool, mats, evac, evac_eng="v", ncols=None):
    i = pool.next()
    pt = pool.tiles[i]
    w = pool.wrote[i]
    p.waitm(p.t, pool.freed[i])
    n = len(mats)
    out_ap = pt[:, :] if ncols is None else pt[:, 0:ncols]
    for j, (lt, rh) in enumerate(mats):
        inst = p.t.matmul(out_ap, lt, rh, start=(j == 0), stop=(j == n - 1))
        if j == n - 1:
            p.inc(inst, w)
    eng = p.v if evac_eng == "v" else p.s
    p.wait(eng, w, w.v)
    evac(eng, pt)
    pool.freed[i] = p.mark(evac_eng)
    return (w, w.v)


def transpose_tile(p, com, tpool, src_ap, dst_ap, evac_eng="v", fp32=False):
    i = tpool.next()
    pt = tpool.tiles[i]
    w = tpool.wrote[i]
    p.waitm(p.t, tpool.freed[i])
    ident = com.identf if fp32 else com.ident
    p.inc(p.t.transpose(pt[:, 0:128], src_ap, ident[:, :]), w)
    eng = p.v if evac_eng == "v" else p.s
    p.wait(eng, w, w.v)
    if evac_eng == "v":
        eng.tensor_copy(dst_ap, pt[:, 0:128])
    else:
        eng.copy(dst_ap, pt[:, 0:128])
    tpool.freed[i] = p.mark(evac_eng)


class WStream:

    def __init__(self, nc, st, sems, name, piece_cols, dtype, nbuf=2, queue="y"):
        self.name = name
        self.piece_cols = piece_cols
        self.bufs = [st.enter_context(
            nc.sbuf_tensor(f"{name}_b{i}", [128, piece_cols], dtype))
            for i in range(nbuf)]
        self.nbuf = nbuf
        self.queue = queue
        self.lsem = []
        for i in range(nbuf):
            s = Sem(st.enter_context(nc.semaphore(f"{name}{i}l")))
            sems[f"{name}{i}l"] = s
            self.lsem.append(s)
        self.freed = [None] * nbuf

    def reset(self):
        self.freed = [None] * self.nbuf

    def qeng(self, p):
        return {"y": p.y, "g": p.g, "s": p.s}[self.queue]

    def load(self, p, slot, src_ap, cols=None):
        eng = self.qeng(p)
        p.waitm(eng, self.freed[slot])
        dst = self.bufs[slot][:, :] if cols is None else self.bufs[slot][:, 0:cols]
        p.dma(eng, self.lsem[slot], out=dst, in_=src_ap)

    def wait_loaded(self, p, eng, slot):
        p.wait(eng, self.lsem[slot])

    def set_free(self, p, slot, marker):
        self.freed[slot] = marker



def emit_ln_chunk(p, com, xc, out_ap, scr, width=D, quant=False):
    inv_w = 1.0 / width
    eps = 1e-6 if quant else 1e-5
    m = scr["m"]; q = scr["q"]; r = scr["r"]; t1 = scr["t1"]
    sq = scr["sq"][:, 0:width]
    p.v.reduce_sum(m[:, :], xc, axis=AX.X)
    p.v.tensor_scalar_mul(m[:, :], m[:, :], inv_w)
    p.waitm(p.s, p.mark("v"))
    p.s.activation(out=sq, in_=xc, func=AF.Square)
    p.waitm(p.v, p.mark("s"))
    p.v.reduce_sum(q[:, :], sq, axis=AX.X)
    p.v.tensor_scalar_mul(q[:, :], q[:, :], inv_w)
    p.v.tensor_tensor(t1[:, :], m[:, :], m[:, :], ALU.mult)
    p.v.tensor_sub(q[:, :], q[:, :], t1[:, :])
    p.waitm(p.s, p.mark("v"))
    p.s.activation(out=t1[:, :], in_=q[:, :], func=AF.Sqrt, bias=com.cc(eps))
    p.waitm(p.v, p.mark("s"))
    p.v.reciprocal(r[:, :], t1[:, :])
    if not quant:
        p.v.tensor_scalar(out_ap, xc, m[:, :], r[:, :],
                          op0=ALU.subtract, op1=ALU.mult)
        return p.mark("v")
    p.v.tensor_scalar(sq, xc, m[:, :], r[:, :], op0=ALU.subtract, op1=ALU.mult)
    p.v.reduce_max(t1[:, :], sq, axis=AX.X, apply_absolute_value=True)
    p.v.reciprocal(m[:, :], t1[:, :])
    p.v.tensor_scalar_mul(m[:, :], m[:, :], 128.0)
    p.v.tensor_scalar_mul(t1[:, :], t1[:, :], 1.0 / 128.0)
    p.v.tensor_scalar(sq, sq, m[:, :], -128.0, op0=ALU.mult, op1=ALU.max)
    p.v.tensor_scalar(sq, sq, 128.0, t1[:, :], op0=ALU.min, op1=ALU.mult)
    p.v.tensor_copy(out_ap, sq)
    return p.mark("v")



def build_l0aa(detect_races=True, stop_after=None):
    nc = bass.Bass("TRN2", target_bir_lowering=False,
                   detect_race_conditions=detect_races)
    x_d = nc.dram_tensor("x", [T, D], F32, kind="ExternalInput").ap()
    wq_d = nc.dram_tensor("wq_p", [4, 128, 16, 512], F32, kind="ExternalInput").ap()
    wk_d = nc.dram_tensor("wk_p", [4, 128, 16, 512], F32, kind="ExternalInput").ap()
    wv_d = nc.dram_tensor("wv_p", [16, 128, 16, 128], F32, kind="ExternalInput").ap()
    wo_d = nc.dram_tensor("wo_p", [4, 128, 16, 512], F32, kind="ExternalInput").ap()
    xo_d = nc.dram_tensor("xo", [T, D], F32, kind="ExternalOutput").ap()

    s_in = nc.dram_tensor("s_in", [2048, 2048], F32, kind="Internal").ap()
    s_out = nc.dram_tensor("s_out", [2048, 2048], F32, kind="Internal").ap()

    with ExitStack() as st:
        sems = make_sems(nc, st)

        def sem(name):
            s = Sem(st.enter_context(nc.semaphore(name)))
            sems[name] = s
            return s

        com = Common(nc, st)
        sb = lambda n, sh, dt: st.enter_context(nc.sbuf_tensor(n, sh, dt))

        x_fm = sb("x_fm", [128, DC * T], BF16)
        qk_sb = sb("qk_sb", [128, TC * D * 2], BF16)
        v_fm = sb("v_fm", [128, DC * T], BF16)
        fA = sb("fA", [128, 2048], F32)
        fB = sb("fB", [128, 2048], F32)
        xn_scr = sb("xn_scr", [128, 2048], BF16)
        outst = [sb(f"outst{i}", [128, 512], F32) for i in range(2)]
        xres = [sb(f"xres{i}", [128, 512], F32) for i in range(2)]
        scr = dict(m=sb("scr_m", [128, 1], F32), q=sb("scr_q", [128, 1], F32),
                   r=sb("scr_r", [128, 1], F32), t1=sb("scr_t1", [128, 1], F32),
                   sq=fB)
        wt_sb = qk_sb
        o_fm = x_fm

        ccs = [sem(f"cc{j}") for j in range(4)]
        out_done = sem("out_done")
        ld = sem("ld"); xld = sem("xld"); sst = sem("sst"); sld = sem("sld")
        osf = [sem("osf0"), sem("osf1")]
        xrl = [sem("xrl0"), sem("xrl1")]

        ppool = PsumPool(nc, st, sems, "pp", 4)
        tpool = PsumPool(nc, st, sems, "tp", 2, shape=(128, 1024), dtype=BF16)

        wstr = WStream(nc, st, sems, "ws", 16 * 512, BF16, nbuf=2, queue="g")
        wstr_l = WStream(nc, st, sems, "wl", 16 * 128, BF16, nbuf=2, queue="g")

        block = st.enter_context(nc.Block())

        def walk(p):
            mld = com.load(p, ld)
            p.waitm(p.t, mld)
            p.waitm(p.v, mld)
            p.waitm(p.s, mld)

            for c in range(TC):
                if c > 0:
                    p.waitm(p.y, p.mark("v"))
                    p.waitm(p.y, p.mark("s"))
                mx = p.dma(p.y, xld, out=fA[:, :],
                           in_=x_d[c * 128:(c + 1) * 128, :])
                p.waitm(p.v, mx)
                p.waitm(p.s, mx)
                mk = emit_ln_chunk(p, com, fA[:, :], xn_scr[:, :], scr,
                                   width=D, quant=False)
                p.waitm(p.t, mk)
                for fc in range(DC):
                    transpose_tile(
                        p, com, tpool,
                        xn_scr[:, fc * 128:(fc + 1) * 128],
                        x_fm[:, fc * T + c * 128: fc * T + (c + 1) * 128])
            m_xfm = p.mark("v")
            if stop_after == 1:
                p.waitm(p.y, m_xfm)
                p.dma(p.y, osf[0], out=xo_d[0:128, :],
                      in_=fA[:, :])
                p.wait(p.y, osf[0])
                return

            p.waitm(p.t, m_xfm)
            for which, w_d in (("q", wq_d), ("k", wk_d)):
                off = 0 if which == "q" else TC * D
                for nb in range(4):
                    slot = nb % 2
                    wstr.load(p, slot, w_d[nb].rearrange("p k n -> p (k n)"))
                    wstr.wait_loaded(p, p.t, slot)
                    last = None
                    for c in range(TC):
                        mats = [(x_fm[:, k * T + c * 128: k * T + (c + 1) * 128],
                                 wstr.bufs[slot][:, k * 512:(k + 1) * 512])
                                for k in range(DC)]
                        dst = qk_sb[:, off + c * D + nb * 512:
                                    off + c * D + (nb + 1) * 512]
                        scale = 0.125 if which == "q" else None

                        def evac(eng, pt, dst=dst, scale=scale):
                            if scale is None:
                                eng.tensor_copy(dst, pt[:, :])
                            else:
                                eng.tensor_scalar_mul(dst, pt[:, :], scale)

                        last = mm_group(p, ppool, mats, evac)
                    wstr.set_free(p, slot, last)
            m_qk = p.mark("v")
            if stop_after == 2:
                p.waitm(p.y, m_qk)
                p.dma(p.y, osf[0], out=xo_d[0:128, :], in_=fA[:, :])
                p.wait(p.y, osf[0])
                return

            p.waitm(p.t, m_qk)
            stage = fA
            for et in range(16):
                for fb in range(4):
                    mats = [(qk_sb[:, c * D + et * 128: c * D + (et + 1) * 128],
                             qk_sb[:, TC * D + c * D + fb * 512:
                                   TC * D + c * D + (fb + 1) * 512])
                            for c in range(TC)]
                    dst = stage[:, fb * 512:(fb + 1) * 512]

                    def evac(eng, pt, dst=dst):
                        eng.tensor_copy(dst, pt[:, :])

                    mm_group(p, ppool, mats, evac)
                p.waitm(p.y, p.mark("v"))
                mst = p.dma(p.y, sst, out=s_in[et * 128:(et + 1) * 128, :],
                            in_=stage[:, :])
                if et < 15:
                    p.waitm(p.v, mst)

            if stop_after == 3:
                p.wait(p.y, sst)
                p.dma(p.y, osf[0], out=xo_d[0:128, :], in_=s_in[0:128, :])
                p.wait(p.y, osf[0])
                return

            p.wait(p.g, sst)
            for j in range(4):
                p.inc_cc(p.g.collective_compute(
                    "AllReduce", ALU.add, replica_groups=RG_PAIRS,
                    ins=[s_in[j * 512:(j + 1) * 512, :]],
                    outs=[s_out[j * 512:(j + 1) * 512, :]]), ccs[j])

            if stop_after == 4:
                for j in range(4):
                    p.wait(p.y, ccs[j], 1)
                p.dma(p.y, osf[0], out=xo_d[0:128, :], in_=s_out[0:128, :])
                p.wait(p.y, osf[0])
                return

            for mb in range(16):
                slot = mb % 2
                wstr_l.load(p, slot, wv_d[mb].rearrange("p k n -> p (k n)"))
                wstr_l.wait_loaded(p, p.t, slot)
                last = None
                for sbk in range(2):
                    mats = [(wstr_l.bufs[slot][:, k * 128:(k + 1) * 128],
                             x_fm[:, k * T + sbk * 512: k * T + sbk * 512 + 512])
                            for k in range(DC)]
                    dst = v_fm[:, mb * T + sbk * 512: mb * T + sbk * 512 + 512]

                    def evac(eng, pt, dst=dst):
                        eng.tensor_copy(dst, pt[:, :])

                    last = mm_group(p, ppool, mats, evac)
                wstr_l.set_free(p, slot, last)
            m_vfm = p.mark("v")
            if stop_after == 5:
                p.waitm(p.y, m_vfm)
                p.dma(p.y, osf[0], out=xo_d[0:128, :], in_=fA[:, :])
                p.wait(p.y, osf[0])
                return

            for et in range(16):
                j = et // 4
                p.wait(p.y, ccs[j], 1)
                p.waitm(p.y, p.mark("v"))
                msl = p.dma(p.y, sld, out=stage[:, :],
                            in_=s_out[et * 128:(et + 1) * 128, :])
                p.waitm(p.v, msl)
                p.v.reduce_max(scr["m"][:, :], stage[:, :], axis=AX.X,
                               negate=True)
                p.waitm(p.s, p.mark("v"))
                p.s.activation(out=stage[:, :], in_=stage[:, :], func=AF.Exp,
                               bias=scr["m"][:, :])
                p.waitm(p.v, p.mark("s"))
                p.v.reduce_sum(scr["q"][:, :], stage[:, :], axis=AX.X)
                p.v.reciprocal(scr["r"][:, :], scr["q"][:, :])
                wrow = xn_scr[:, :]
                p.v.tensor_scalar_mul(wrow, stage[:, :], scr["r"][:, :])
                p.waitm(p.t, p.mark("v"))
                for fc in range(16):
                    transpose_tile(
                        p, com, tpool,
                        wrow[:, fc * 128:(fc + 1) * 128],
                        wt_sb[:, fc * D + et * 128: fc * D + (et + 1) * 128],
                        evac_eng="s")

            if stop_after == 6:
                p.waitm(p.y, p.mark("s"))
                p.dma(p.y, osf[0], out=xo_d[0:128, :], in_=fA[:, :])
                p.wait(p.y, osf[0])
                return

            p.waitm(p.t, p.mark("s"))
            p.waitm(p.t, m_vfm)
            for et in range(16):
                for sbk in range(2):
                    mats = [(wt_sb[:, fc * D + et * 128: fc * D + (et + 1) * 128],
                             v_fm[:, fc * T + sbk * 512: fc * T + sbk * 512 + 512])
                            for fc in range(16)]
                    dst = o_fm[:, et * T + sbk * 512: et * T + sbk * 512 + 512]

                    def evac(eng, pt, dst=dst):
                        eng.tensor_copy(dst, pt[:, :])

                    mm_group(p, ppool, mats, evac)
            m_ofm = p.mark("v")
            if stop_after == 7:
                p.waitm(p.y, m_ofm)
                p.dma(p.y, osf[0], out=xo_d[0:128, :], in_=fA[:, :])
                p.wait(p.y, osf[0])
                return

            p.waitm(p.t, m_ofm)
            free_markers = [None, None]
            xres_free = [None, None]
            for nb in range(4):
                slot = 0
                wstr.load(p, slot, wo_d[nb].rearrange("p k n -> p (k n)"))
                wstr.wait_loaded(p, p.t, slot)
                last = None
                for c in range(TC):
                    mats = [(o_fm[:, k * T + c * 128: k * T + (c + 1) * 128],
                             wstr.bufs[slot][:, k * 512:(k + 1) * 512])
                            for k in range(DC)]
                    ob = (nb * TC + c) % 2
                    p.waitm(p.y, xres_free[ob])
                    mxr = p.dma(
                        p.y, xrl[ob], out=xres[ob][:, :],
                        in_=x_d[c * 128:(c + 1) * 128, nb * 512:(nb + 1) * 512])

                    def evac(eng, pt, ob=ob, mxr=mxr):
                        p.waitm(eng, free_markers[ob])
                        p.waitm(eng, mxr)
                        eng.tensor_add(outst[ob][:, :], xres[ob][:, :], pt[:, :])

                    last = mm_group(p, ppool, mats, evac)
                    xres_free[ob] = p.mark("v")
                    p.waitm(p.y, xres_free[ob])
                    free_markers[ob] = p.dma(
                        p.y, osf[ob],
                        out=xo_d[c * 128:(c + 1) * 128, nb * 512:(nb + 1) * 512],
                        in_=outst[ob][:, :])
                wstr.set_free(p, slot, last)
            p.wait(p.y, osf[0])
            p.wait(p.y, osf[1])

        run_walk(nc, block, sems, walk, resettables=[ppool, tpool, wstr, wstr_l])
    return nc





PREP_JOBS = [
    ("wfc", 8, 2048, "tern"),
    ("wpr", 2, 8192, "tern"),
    ("win", 9, 2048, "cast"),
    ("wou", 2, 4096, "castscale"),
    ("wga", 8, 2048, "cast"),
    ("wup", 8, 2048, "cast"),
    ("wdo", 2, 8192, "cast"),
]


def build_prep(detect_races=True):
    nc = bass.Bass("TRN2", target_bir_lowering=False,
                   detect_race_conditions=detect_races)
    ins = {}
    outs = {}
    for name, nt, K, kind in PREP_JOBS:
        ins[name] = nc.dram_tensor(name, [nt * 128, K], F32,
                                   kind="ExternalInput").ap()
        outs[name] = nc.dram_tensor(name + "T", [K // 128, 128, nt * 128],
                                    BF16, kind="ExternalOutput").ap()
    normw_d = nc.dram_tensor("normw_p", [128, 32], F32,
                             kind="ExternalInput").ap()
    gam_d = nc.dram_tensor("gam", [1, 2], F32, kind="ExternalOutput").ap()
    cc_in = nc.dram_tensor("cc_in", [1, 512], F32, kind="Internal").ap()
    cc_out = nc.dram_tensor("cc_out", [1, 512], F32, kind="Internal").ap()

    NFC = 8192.0 * 2048.0
    NPR = 2048.0 * 8192.0

    with ExitStack() as st:
        sems = make_sems(nc, st)

        def sem(name):
            s = Sem(st.enter_context(nc.semaphore(name)))
            sems[name] = s
            return s

        com = Common(nc, st)
        sb = lambda n, sh, dt: st.enter_context(nc.sbuf_tensor(n, sh, dt))

        sliceA = sb("sliceA", [128, 9 * 2048], F32)
        sliceB = sb("sliceB", [128, 2 * 8192], F32)
        tbf = sb("tbf", [128, 9 * 2048], BF16)
        sgn = sb("sgn", [128, 4096], F32)
        outrow = sb("outrow", [128, 9 * 128], BF16)
        normw = sb("normw", [128, 32], F32)
        acc = sb("acc", [128, 1], F32)
        tmp1 = sb("tmp1", [128, 1], F32)
        gcols = sb("gcols", [128, 2], F32)
        grow = sb("grow", [1, 512], F32)

        ld = sem("ld"); wld = sem("wld"); od = sem("od")
        ccm = sem("ccm"); gld = sem("gld"); cc = sem("cc_g")
        nwl = sem("nwl")
        ppool = PsumPool(nc, st, sems, "pp", 2, shape=(128, 512))
        tpool = PsumPool(nc, st, sems, "tp", 2, shape=(128, 1024), dtype=BF16)

        block = st.enter_context(nc.Block())

        def abs_sum(p, slice_sb, nt, K, col_out):
            for t_ in range(nt):
                tile = slice_sb[:, t_ * K:(t_ + 1) * K]
                p.v.reduce_sum(tmp1[:, :], tile, axis=AX.X,
                               apply_absolute_value=True)
                if t_ == 0:
                    p.v.tensor_copy(acc[:, :], tmp1[:, :])
                else:
                    p.v.tensor_add(acc[:, :], acc[:, :], tmp1[:, :])
            p.waitm(p.t, p.mark("v"))
            i = ppool.next()
            pt = ppool.tiles[i]
            p.waitm(p.t, ppool.freed[i])
            p.inc(p.t.matmul(pt[0:1, 0:1], acc[:, :], com.cc(1.0),
                             start=True, stop=True), ppool.wrote[i])
            p.wait(p.v, ppool.wrote[i], ppool.wrote[i].v)
            p.v.tensor_copy(col_out, pt[0:1, 0:1])
            ppool.freed[i] = p.mark("v")

        def transpose_out(p, src_bf, nt, K, out_d, scale_col_of=None):
            for kc in range(K // 128):
                for rt in range(nt):
                    src = src_bf[:, rt * K + kc * 128: rt * K + kc * 128 + 128]
                    dst = outrow[:, rt * 128:(rt + 1) * 128]
                    i = tpool.next()
                    pt = tpool.tiles[i]
                    w = tpool.wrote[i]
                    p.waitm(p.t, tpool.freed[i])
                    p.inc(p.t.transpose(pt[:, 0:128], src, com.ident[:, :]), w)
                    p.wait(p.v, w, w.v)
                    if scale_col_of is None:
                        p.v.tensor_copy(dst, pt[:, 0:128])
                    else:
                        p.v.tensor_scalar_mul(dst, pt[:, 0:128],
                                              scale_col_of(kc))
                    tpool.freed[i] = p.mark("v")
                p.waitm(p.y, p.mark("v"))
                mo = p.dma(p.y, od, out=out_d[kc, :, 0:nt * 128],
                           in_=outrow[:, 0:nt * 128])
                p.waitm(p.v, mo)

        def load_slice(p, dst_sb, name, nt, K):
            m = p.dma(p.y, wld,
                      out=dst_sb[:, 0:nt * K].rearrange("p (t k) -> p t k", t=nt),
                      in_=ins[name].rearrange("(t p) k -> p t k", p=128))
            return m

        def ternarize(p, slice_sb, nt, K, gi):
            W = min(K, 4096)
            for t_ in range(nt):
                for j in range(K // W):
                    tile = slice_sb[:, t_ * K + j * W: t_ * K + (j + 1) * W]
                    sg = sgn[:, 0:W]
                    p.waitm(p.s, p.mark("v"))
                    p.s.activation(out=sg, in_=tile, func=AF.Sign)
                    p.s.activation(out=tile, in_=tile, func=AF.Abs)
                    p.waitm(p.v, p.mark("s"))
                    p.v.tensor_scalar(tile, tile, gcols[:, gi:gi + 1], 0.5,
                                      op0=ALU.mult, op1=ALU.is_gt)
                    p.v.tensor_tensor(
                        tbf[:, t_ * K + j * W: t_ * K + (j + 1) * W],
                        sg, tile, ALU.mult)

        def walk(p):
            mld = com.load(p, ld)
            p.waitm(p.v, mld)
            p.waitm(p.s, mld)
            p.waitm(p.t, mld)
            mnw = p.dma(p.y, nwl, out=normw[:, :], in_=normw_d)

            m1 = load_slice(p, sliceA, "wfc", 8, 2048)
            m2 = load_slice(p, sliceB, "wpr", 2, 8192)
            p.waitm(p.v, m2)
            abs_sum(p, sliceA, 8, 2048, gcols[0:1, 0:1])
            abs_sum(p, sliceB, 2, 8192, gcols[0:1, 1:2])
            p.v.memset(grow[:, :], 0.0)
            p.v.tensor_copy(grow[0:1, 0:2], gcols[0:1, 0:2])
            p.waitm(p.y, p.mark("v"))
            p.dma(p.y, ccm, out=cc_in, in_=grow[:, :])
            p.wait(p.g, ccm)
            p.inc_cc(p.g.collective_compute(
                "AllReduce", ALU.add, replica_groups=RG_ALL,
                ins=[cc_in], outs=[cc_out]), cc)
            p.wait(p.y, cc, 1)
            mg = p.dma(p.y, gld, out=grow[:, :], in_=cc_out)
            p.waitm(p.v, mg)
            p.v.tensor_scalar_mul(grow[0:1, 0:1], grow[0:1, 0:1], 1.0 / NFC)
            p.v.tensor_scalar_mul(grow[0:1, 1:2], grow[0:1, 1:2], 1.0 / NPR)
            p.waitm(p.y, p.mark("v"))
            mgam = p.dma(p.y, od, out=gam_d, in_=grow[0:1, 0:2])
            p.waitm(p.v, mgam)
            p.v.tensor_scalar_add(grow[0:1, 0:2], grow[0:1, 0:2], 1e-6)
            p.v.reciprocal(grow[0:1, 0:2], grow[0:1, 0:2])
            p.waitm(p.t, p.mark("v"))
            i = ppool.next()
            pt = ppool.tiles[i]
            p.waitm(p.t, ppool.freed[i])
            p.inc(p.t.matmul(pt[:, 0:2], com.ones_row[:, :], grow[0:1, 0:2],
                             start=True, stop=True), ppool.wrote[i])
            p.wait(p.v, ppool.wrote[i], ppool.wrote[i].v)
            p.v.tensor_copy(gcols[:, :], pt[:, 0:2])
            ppool.freed[i] = p.mark("v")

            ternarize(p, sliceA, 8, 2048, 0)
            p.waitm(p.t, p.mark("v"))
            transpose_out(p, tbf, 8, 2048, outs["wfc"])
            ternarize(p, sliceB, 2, 8192, 1)
            p.waitm(p.t, p.mark("v"))
            transpose_out(p, tbf, 2, 8192, outs["wpr"])

            def cast_job(name, nt, K, scale_col_of=None):
                dst_sb = sliceA if nt * K > 16384 else sliceB
                m = load_slice(p, dst_sb, name, nt, K)
                p.waitm(p.v, m)
                for t_ in range(nt):
                    p.v.tensor_copy(tbf[:, t_ * K:(t_ + 1) * K],
                                    dst_sb[:, t_ * K:(t_ + 1) * K])
                p.waitm(p.t, p.mark("v"))
                transpose_out(p, tbf, nt, K, outs[name], scale_col_of)

            cast_job("win", 9, 2048)
            p.waitm(p.v, mnw)
            cast_job("wou", 2, 4096,
                     scale_col_of=lambda kc: normw[:, kc:kc + 1])
            cast_job("wga", 8, 2048)
            cast_job("wup", 8, 2048)
            cast_job("wdo", 2, 8192)
            p.wait(p.y, od)

        run_walk(nc, block, sems, walk, resettables=[ppool, tpool])
    return nc


def run_prep(inputs):
    nc = build_prep()
    w_fc = np.ascontiguousarray(np.asarray(inputs["L0_ffn_w_fc"], np.float32))
    w_pr = np.ascontiguousarray(np.asarray(inputs["L0_ffn_w_proj"], np.float32))
    w_in = np.asarray(inputs["L0_attn_b_W_in"], np.float32)
    w_in_pad = np.zeros((9216, 2048), np.float32)
    w_in_pad[:8352] = w_in
    w_ou = np.ascontiguousarray(np.asarray(inputs["L0_attn_b_W_out"], np.float32))
    w_ga = np.ascontiguousarray(np.asarray(inputs["L1_ffn_w_gate"], np.float32))
    w_up = np.ascontiguousarray(np.asarray(inputs["L1_ffn_w_up"], np.float32))
    w_do = np.ascontiguousarray(np.asarray(inputs["L1_ffn_w_down"], np.float32))
    full = dict(wfc=w_fc, wpr=w_pr, win=w_in_pad, wou=w_ou, wga=w_ga,
                wup=w_up, wdo=w_do)
    normw_p = col_pack(inputs["L0_attn_b_norm_w"])
    in_maps = []
    for c in range(NCORES):
        m = {"normw_p": normw_p}
        for name, nt, K, kind in PREP_JOBS:
            m[name] = np.ascontiguousarray(
                full[name][c * nt * 128:(c + 1) * nt * 128, :])
        in_maps.append(m)
    res = run_bass_kernel_spmd(nc, in_maps, core_ids=list(range(NCORES)))
    out = {}
    for name, nt, K, kind in PREP_JOBS:
        parts = [np.asarray(res.results[c][name + "T"]).reshape(K, nt * 128)
                 for c in range(NCORES)]
        out[name + "T"] = np.concatenate(parts, axis=1)
    out["winT"] = out["winT"][:, :8352]
    gam = np.asarray(res.results[0]["gam"])
    out["g_fc"] = float(gam[0, 0])
    out["g_pr"] = float(gam[0, 1])
    out["gam"] = gam
    return out



def tproj_pack(tT):
    return np.ascontiguousarray(
        tT.reshape(4, 16, 128, 4, 512).transpose(3, 0, 2, 1, 4))


def build_l0ffn(detect_races=True):
    nc = bass.Bass("TRN2", target_bir_lowering=False,
                   detect_race_conditions=detect_races)
    x_d = nc.dram_tensor("x", [T, D], F32, kind="ExternalInput").ap()
    tfc_d = nc.dram_tensor("tfc_p", [16, 128, 16, 512], BF16,
                           kind="ExternalInput").ap()
    tpr_d = nc.dram_tensor("tpr_p", [4, 4, 128, 16, 512], BF16,
                           kind="ExternalInput").ap()
    gam_d = nc.dram_tensor("gam", [1, 2], F32, kind="ExternalInput").ap()
    xo_d = nc.dram_tensor("xo", [T, D], F32, kind="ExternalOutput").ap()
    hg_d = nc.dram_tensor("hg", [T, 8192], BF16, kind="Internal").ap()

    with ExitStack() as st:
        sems = make_sems(nc, st)

        def sem(name):
            s = Sem(st.enter_context(nc.semaphore(name)))
            sems[name] = s
            return s

        com = Common(nc, st)
        sb = lambda n, sh, dt: st.enter_context(nc.sbuf_tensor(n, sh, dt))

        yacc = sb("yacc", [128, TC * D], F32)
        xh_fm = sb("xh_fm", [128, DC * T], BF16)
        fA = sb("fA", [128, 2048], F32)
        fB = sb("fB", [128, 8192], F32)
        xn_scr = sb("xn_scr", [128, 2048], BF16)
        hgrow = sb("hgrow", [128, 8192], BF16)
        hst = [sb(f"hst{i}", [128, 512], BF16) for i in range(2)]
        scr = dict(m=sb("scr_m", [128, 1], F32), q=sb("scr_q", [128, 1], F32),
                   r=sb("scr_r", [128, 1], F32), t1=sb("scr_t1", [128, 1], F32),
                   sq=fB)
        sumac = sb("sumac", [128, TC], F32)
        sqac = sb("sqac", [128, TC], F32)
        m2c = sb("m2c", [128, TC], F32)
        c1c = sb("c1c", [128, TC], F32)
        c2c = sb("c2c", [128, TC], F32)
        gcol = sb("gcol", [128, 2], F32)
        grow = sb("grow", [1, 2], F32)
        xh2_fm = xh_fm

        ld = sem("ld"); xld = sem("xld"); gld = sem("gld")
        hst_d = [sem("hst_d0"), sem("hst_d1")]
        hld = sem("hld"); osf = sem("osf")

        ppool = PsumPool(nc, st, sems, "pp", 4)
        tpool = PsumPool(nc, st, sems, "tp", 2, shape=(128, 1024), dtype=BF16)
        wstr = WStream(nc, st, sems, "ws", 16 * 512, BF16, nbuf=2, queue="g")

        block = st.enter_context(nc.Block())

        def walk(p):
            mld = com.load(p, ld)
            p.waitm(p.v, mld)
            p.waitm(p.s, mld)
            p.waitm(p.t, mld)
            mgam = p.dma(p.y, gld, out=grow[:, :], in_=gam_d)
            p.waitm(p.t, mgam)
            i = ppool.next()
            pt = ppool.tiles[i]
            p.inc(p.t.matmul(pt[:, 0:2], com.ones_row[:, :], grow[:, :],
                             start=True, stop=True), ppool.wrote[i])
            p.wait(p.v, ppool.wrote[i], ppool.wrote[i].v)
            p.v.tensor_copy(gcol[:, :], pt[:, 0:2])
            ppool.freed[i] = p.mark("v")

            for c in range(TC):
                if c > 0:
                    p.waitm(p.y, p.mark("v"))
                    p.waitm(p.y, p.mark("s"))
                mx = p.dma(p.y, xld, out=fA[:, :],
                           in_=x_d[c * 128:(c + 1) * 128, :])
                p.waitm(p.v, mx)
                p.waitm(p.s, mx)
                p.v.tensor_copy(yacc[:, c * D:(c + 1) * D], fA[:, :])
                mk = emit_ln_chunk(p, com, fA[:, :], xn_scr[:, :], scr,
                                   width=D, quant=True)
                p.waitm(p.t, mk)
                for fc in range(DC):
                    transpose_tile(
                        p, com, tpool,
                        xn_scr[:, fc * 128:(fc + 1) * 128],
                        xh_fm[:, fc * T + c * 128: fc * T + (c + 1) * 128])
            m_xh = p.mark("v")

            p.waitm(p.t, m_xh)
            p.waitm(p.v, p.mark("s"))
            p.v.memset(sumac[:, :], 0.0)
            p.v.memset(sqac[:, :], 0.0)
            hfree = [None, None]
            for ob in range(16):
                slot = ob % 2
                wstr.load(p, slot, tfc_d[ob].rearrange("p k n -> p (k n)"))
                wstr.wait_loaded(p, p.t, slot)
                last = None
                for c in range(TC):
                    mats = [(xh_fm[:, k * T + c * 128: k * T + (c + 1) * 128],
                             wstr.bufs[slot][:, k * 512:(k + 1) * 512])
                            for k in range(DC)]
                    hb = (ob * TC + c) % 2

                    def evac(eng, pt, hb=hb, c=c):
                        p.waitm(eng, hfree[hb])
                        eng.activation(out=hst[hb][:, :], in_=pt[:, :],
                                       func=AF.Gelu, scale=gcol[:, 0:1])

                    last = mm_group(p, ppool, mats, evac, evac_eng="s")
                    mh = p.mark("s")
                    p.waitm(p.v, mh)
                    p.v.reduce_sum(scr["m"][:, :], hst[hb][:, :], axis=AX.X)
                    p.v.tensor_add(sumac[:, c:c + 1], sumac[:, c:c + 1],
                                   scr["m"][:, :])
                    p.waitm(p.s, p.mark("v"))
                    p.s.activation(out=fA[:, 0:512], in_=hst[hb][:, :],
                                   func=AF.Square)
                    p.waitm(p.v, p.mark("s"))
                    p.v.reduce_sum(scr["q"][:, :], fA[:, 0:512], axis=AX.X)
                    p.v.tensor_add(sqac[:, c:c + 1], sqac[:, c:c + 1],
                                   scr["q"][:, :])
                    p.waitm(p.y, mh)
                    hfree[hb] = p.dma(
                        p.y, hst_d[hb],
                        out=hg_d[c * 128:(c + 1) * 128, ob * 512:(ob + 1) * 512],
                        in_=hst[hb][:, :])
                wstr.set_free(p, slot, last)

            p.waitm(p.v, p.mark("s"))
            p.v.tensor_scalar_mul(m2c[:, :], sumac[:, :], 1.0 / 8192.0)
            p.v.tensor_scalar_mul(sqac[:, :], sqac[:, :], 1.0 / 8192.0)
            p.v.tensor_tensor(c1c[:, :], m2c[:, :], m2c[:, :], ALU.mult)
            p.v.tensor_sub(sqac[:, :], sqac[:, :], c1c[:, :])
            p.waitm(p.s, p.mark("v"))
            p.s.activation(out=sqac[:, :], in_=sqac[:, :], func=AF.Sqrt,
                           bias=com.cc(1e-6))
            p.waitm(p.v, p.mark("s"))
            p.v.reciprocal(sqac[:, :], sqac[:, :])
            p.wait(p.y, hst_d[0])
            p.wait(p.y, hst_d[1])
            for c in range(TC):
                if c > 0:
                    p.waitm(p.y, p.mark("v"))
                mh = p.dma(p.y, hld, out=hgrow[:, :],
                           in_=hg_d[c * 128:(c + 1) * 128, :])
                p.waitm(p.v, mh)
                p.v.tensor_scalar(fB[:, :], hgrow[:, :], m2c[:, c:c + 1], None,
                                  op0=ALU.subtract)
                p.v.reduce_max(scr["m"][:, :], fB[:, :], axis=AX.X,
                               apply_absolute_value=True)
                p.v.reciprocal(scr["q"][:, :], scr["m"][:, :])
                p.v.tensor_scalar_mul(c1c[:, c:c + 1], scr["q"][:, :], 128.0)
                p.v.tensor_tensor(scr["t1"][:, :], scr["m"][:, :],
                                  sqac[:, c:c + 1], ALU.mult)
                p.v.tensor_scalar_mul(c2c[:, c:c + 1], scr["t1"][:, :],
                                      1.0 / 128.0)

            for os_ in range(4):
                for c in range(TC):
                    p.waitm(p.y, p.mark("v"))
                    mh = p.dma(p.y, hld, out=hgrow[:, 0:2048],
                               in_=hg_d[c * 128:(c + 1) * 128,
                                        os_ * 2048:(os_ + 1) * 2048])
                    p.waitm(p.v, mh)
                    p.v.tensor_scalar(fB[:, 0:2048], hgrow[:, 0:2048],
                                      m2c[:, c:c + 1], c1c[:, c:c + 1],
                                      op0=ALU.subtract, op1=ALU.mult)
                    p.v.tensor_scalar(fB[:, 0:2048], fB[:, 0:2048],
                                      -128.0, 128.0, op0=ALU.max, op1=ALU.min)
                    p.v.tensor_scalar_mul(xn_scr[:, :], fB[:, 0:2048],
                                          c2c[:, c:c + 1])
                    p.waitm(p.t, p.mark("v"))
                    for oc in range(16):
                        transpose_tile(
                            p, com, tpool,
                            xn_scr[:, oc * 128:(oc + 1) * 128],
                            xh2_fm[:, oc * T + c * 128: oc * T + (c + 1) * 128])
                m_x2 = p.mark("v")
                p.waitm(p.t, m_x2)
                for db in range(4):
                    slot = db % 2
                    wstr.load(p, slot, tpr_d[db, os_].rearrange("p k n -> p (k n)"))
                    wstr.wait_loaded(p, p.t, slot)
                    last = None
                    for c in range(TC):
                        mats = [(xh2_fm[:, oc * T + c * 128: oc * T + (c + 1) * 128],
                                 wstr.bufs[slot][:, oc * 512:(oc + 1) * 512])
                                for oc in range(16)]
                        dst = yacc[:, c * D + db * 512: c * D + (db + 1) * 512]

                        def evac(eng, pt, dst=dst):
                            eng.scalar_tensor_tensor(
                                dst, pt[:, :], gcol[:, 1:2], dst,
                                op0=ALU.mult, op1=ALU.add)

                        last = mm_group(p, ppool, mats, evac)
                    wstr.set_free(p, slot, last)

            p.waitm(p.y, p.mark("v"))
            xo_r = xo_d.rearrange("(c p) d -> p c d", p=128)
            p.dma(p.y, osf, out=xo_r,
                  in_=yacc[:, :].rearrange("p (c d) -> p c d", c=TC))
            p.wait(p.y, osf)

        run_walk(nc, block, sems, walk, resettables=[ppool, tpool, wstr])
    return nc


def run_l0ffn(x_shards, tfcT, tprT, gam):
    nc = build_l0ffn()
    tfc_p = rhs_pack(tfcT.astype(BF))
    tpr_p = tproj_pack(tprT.astype(BF))
    in_maps = [dict(x=x_shards[i], tfc_p=tfc_p, tpr_p=tpr_p, gam=gam)
               for i in range(NCORES)]
    return _run(nc, in_maps, "xo")





def build_l1ffn(detect_races=True):
    nc = bass.Bass("TRN2", target_bir_lowering=False,
                   detect_race_conditions=detect_races)
    x_d = nc.dram_tensor("x", [T, D], F32, kind="ExternalInput").ap()
    wga_d = nc.dram_tensor("wga_p", [16, 128, 16, 512], BF16,
                           kind="ExternalInput").ap()
    wup_d = nc.dram_tensor("wup_p", [16, 128, 16, 512], BF16,
                           kind="ExternalInput").ap()
    wdo_d = nc.dram_tensor("wdo_p", [4, 4, 128, 16, 512], BF16,
                           kind="ExternalInput").ap()
    xo_d = nc.dram_tensor("xo", [T, D], F32, kind="ExternalOutput").ap()

    with ExitStack() as st:
        sems = make_sems(nc, st)

        def sem(name):
            s = Sem(st.enter_context(nc.semaphore(name)))
            sems[name] = s
            return s

        com = Common(nc, st)
        sb = lambda n, sh, dt: st.enter_context(nc.sbuf_tensor(n, sh, dt))

        yacc = sb("yacc", [128, TC * D], F32)
        xn_fm = sb("xn_fm", [128, DC * T], BF16)
        gatedT = sb("gatedT", [128, 16 * T], BF16)
        fA = sb("fA", [128, 2048], F32)
        fB = sb("fB", [128, 2048], F32)
        xn_scr = sb("xn_scr", [128, 2048], BF16)
        sil = sb("sil", [128, 512], BF16)
        gated_st = sb("gated_st", [128, 512], BF16)
        scr = dict(m=sb("scr_m", [128, 1], F32), q=sb("scr_q", [128, 1], F32),
                   r=sb("scr_r", [128, 1], F32), t1=sb("scr_t1", [128, 1], F32),
                   sq=fB)

        ld = sem("ld"); xld = sem("xld"); osf = sem("osf")

        ppool = PsumPool(nc, st, sems, "pp", 4)
        tpool = PsumPool(nc, st, sems, "tp", 2, shape=(128, 1024), dtype=BF16)
        wsg = WStream(nc, st, sems, "wsg", 16 * 512, BF16, nbuf=1, queue="g")
        wsu = WStream(nc, st, sems, "wsu", 16 * 512, BF16, nbuf=1, queue="g")
        wsd = WStream(nc, st, sems, "wsd", 16 * 512, BF16, nbuf=1, queue="y")

        block = st.enter_context(nc.Block())

        def walk(p):
            mld = com.load(p, ld)
            p.waitm(p.v, mld)
            p.waitm(p.s, mld)
            p.waitm(p.t, mld)

            for c in range(TC):
                if c > 0:
                    p.waitm(p.y, p.mark("v"))
                    p.waitm(p.y, p.mark("s"))
                mx = p.dma(p.y, xld, out=fA[:, :],
                           in_=x_d[c * 128:(c + 1) * 128, :])
                p.waitm(p.v, mx)
                p.waitm(p.s, mx)
                p.v.tensor_copy(yacc[:, c * D:(c + 1) * D], fA[:, :])
                mk = emit_ln_chunk(p, com, fA[:, :], xn_scr[:, :], scr,
                                   width=D, quant=False)
                p.waitm(p.t, mk)
                for fc in range(DC):
                    transpose_tile(
                        p, com, tpool,
                        xn_scr[:, fc * 128:(fc + 1) * 128],
                        xn_fm[:, fc * T + c * 128: fc * T + (c + 1) * 128])
            p.waitm(p.t, p.mark("v"))

            for hs in range(4):
                for hb in range(4):
                    ob = hs * 4 + hb
                    wsg.load(p, 0, wga_d[ob].rearrange("p k n -> p (k n)"))
                    wsu.load(p, 0, wup_d[ob].rearrange("p k n -> p (k n)"))
                    wsg.wait_loaded(p, p.t, 0)
                    wsu.wait_loaded(p, p.t, 0)
                    lg = lu = None
                    for c in range(TC):
                        mats_g = [(xn_fm[:, k * T + c * 128: k * T + (c + 1) * 128],
                                   wsg.bufs[0][:, k * 512:(k + 1) * 512])
                                  for k in range(DC)]
                        mats_u = [(xn_fm[:, k * T + c * 128: k * T + (c + 1) * 128],
                                   wsu.bufs[0][:, k * 512:(k + 1) * 512])
                                  for k in range(DC)]

                        def evac_g(eng, pt):
                            p.waitm(eng, p.mark("v"))
                            eng.activation(out=sil[:, :], in_=pt[:, :],
                                           func=AF.Silu)

                        lg = mm_group(p, ppool, mats_g, evac_g, evac_eng="s")

                        def evac_u(eng, pt):
                            p.waitm(eng, p.mark("s"))
                            eng.tensor_tensor(gated_st[:, :], sil[:, :],
                                              pt[:, :], ALU.mult)

                        lu = mm_group(p, ppool, mats_u, evac_u)
                        p.waitm(p.t, p.mark("v"))
                        for j in range(4):
                            hc = hb * 4 + j
                            transpose_tile(
                                p, com, tpool,
                                gated_st[:, j * 128:(j + 1) * 128],
                                gatedT[:, hc * T + c * 128: hc * T + (c + 1) * 128])
                    wsg.set_free(p, 0, lg)
                    wsu.set_free(p, 0, lu)
                p.waitm(p.t, p.mark("v"))
                for db in range(4):
                    wsd.load(p, 0, wdo_d[db, hs].rearrange("p k n -> p (k n)"))
                    wsd.wait_loaded(p, p.t, 0)
                    last = None
                    for c in range(TC):
                        mats = [(gatedT[:, oc * T + c * 128: oc * T + (c + 1) * 128],
                                 wsd.bufs[0][:, oc * 512:(oc + 1) * 512])
                                for oc in range(16)]
                        dst = yacc[:, c * D + db * 512: c * D + (db + 1) * 512]

                        def evac(eng, pt, dst=dst):
                            eng.tensor_add(dst, dst, pt[:, :])

                        last = mm_group(p, ppool, mats, evac)
                    wsd.set_free(p, 0, last)

            p.waitm(p.y, p.mark("v"))
            xo_r = xo_d.rearrange("(c p) d -> p c d", p=128)
            p.dma(p.y, osf, out=xo_r,
                  in_=yacc[:, :].rearrange("p (c d) -> p c d", c=TC))
            p.wait(p.y, osf)

        run_walk(nc, block, sems, walk, resettables=[ppool, tpool, wsg, wsu, wsd])
    return nc


def run_l1ffn(x_shards, wgaT, wupT, wdoT):
    nc = build_l1ffn()
    wga_p = rhs_pack(wgaT.astype(BF))
    wup_p = rhs_pack(wupT.astype(BF))
    wdo_p = tproj_pack(wdoT.astype(BF))
    in_maps = [dict(x=x_shards[i], wga_p=wga_p, wup_p=wup_p, wdo_p=wdo_p)
               for i in range(NCORES)]
    return _run(nc, in_maps, "xo")





def build_l1ab(detect_races=True):
    nc = bass.Bass("TRN2", target_bir_lowering=False,
                   detect_race_conditions=detect_races)
    x_d = nc.dram_tensor("x", [T, D], F32, kind="ExternalInput").ap()
    wq_d = nc.dram_tensor("wq_l", [16, 128, 16, 128], F32, kind="ExternalInput").ap()
    wk_d = nc.dram_tensor("wk_l", [16, 128, 16, 128], F32, kind="ExternalInput").ap()
    wv_d = nc.dram_tensor("wv_p", [4, 128, 16, 512], F32, kind="ExternalInput").ap()
    wo_d = nc.dram_tensor("wo_p", [4, 128, 16, 512], F32, kind="ExternalInput").ap()
    mask_d = nc.dram_tensor("mask", [T, 2048], BF16, kind="ExternalInput").ap()
    flags_d = nc.dram_tensor("flags", [128, 2], F32, kind="ExternalInput").ap()
    xo_d = nc.dram_tensor("xo", [T, D], F32, kind="ExternalOutput").ap()

    ck_in = nc.dram_tensor("ck_in", [16, 128, 2048], BF16, kind="Internal").ap()
    ck_out = nc.dram_tensor("ck_out", [16, 128, 2048], BF16, kind="Internal").ap()
    cv_in = nc.dram_tensor("cv_in", [2, 2048, 1024], BF16, kind="Internal").ap()
    cv_out = nc.dram_tensor("cv_out", [2, 2048, 1024], BF16, kind="Internal").ap()

    SCALE = 1.0 / math.sqrt(128.0)

    with ExitStack() as st:
        sems = make_sems(nc, st)

        def sem(name):
            s = Sem(st.enter_context(nc.semaphore(name)))
            sems[name] = s
            return s

        com = Common(nc, st)
        sb = lambda n, sh, dt: st.enter_context(nc.sbuf_tensor(n, sh, dt))

        xn_fm = sb("xn_fm", [128, DC * T], BF16)
        q_fm = sb("q_fm", [128, 16 * T], BF16)
        kg = sb("kg", [128, 4 * 2048], BF16)
        vg = sb("vg", [128, 16 * 512], BF16)
        o_tok = sb("o_tok", [128, TC * D], BF16)
        P_sb = sb("P_sb", [128, 2048], BF16)
        PT_sb = sb("PT_sb", [128, 16 * 128], BF16)
        fA = sb("fA", [128, 2048], F32)
        fB = sb("fB", [128, 2048], F32)
        xn_scr = sb("xn_scr", [128, 2048], BF16)
        S_sb = fA
        mask_sb = xn_scr
        st2 = [sb(f"st2{i}", [128, 512], BF16) for i in range(2)]
        outst = [sb(f"outst{i}", [128, 512], F32) for i in range(2)]
        xres = [sb(f"xres{i}", [128, 512], F32) for i in range(2)]
        flags = sb("flags_sb", [128, 2], F32)
        scr = dict(m=sb("scr_m", [128, 1], F32), q=sb("scr_q", [128, 1], F32),
                   r=sb("scr_r", [128, 1], F32), t1=sb("scr_t1", [128, 1], F32),
                   sq=fB)
        o_fm = xn_fm

        ld = sem("ld"); xld = sem("xld"); mld_s = sem("mld"); fld = sem("fld")
        kst = [sem("kst0"), sem("kst1")]
        kld = sem("kld"); vld = sem("vld")
        osf = [sem("osf0"), sem("osf1")]
        xrl = [sem("xrl0"), sem("xrl1")]
        ccs = [sem(f"ccab{j}") for j in range(6)]

        ppool = PsumPool(nc, st, sems, "pp", 2)
        tpool = PsumPool(nc, st, sems, "tp", 2, shape=(128, 1024), dtype=BF16)
        spool = PsumPool(nc, st, sems, "sp", 1, shape=(128, 2048))
        wstr = WStream(nc, st, sems, "ws", 16 * 512, BF16, nbuf=1, queue="g")
        wstr_l = WStream(nc, st, sems, "wl", 16 * 128, BF16, nbuf=2, queue="g")

        block = st.enter_context(nc.Block())

        def walk(p):
            mld = com.load(p, ld)
            p.waitm(p.v, mld)
            p.waitm(p.s, mld)
            p.waitm(p.t, mld)
            mf = p.dma(p.y, fld, out=flags[:, :], in_=flags_d)
            p.waitm(p.v, mf)

            for c in range(TC):
                if c > 0:
                    p.waitm(p.y, p.mark("v"))
                    p.waitm(p.y, p.mark("s"))
                mx = p.dma(p.y, xld, out=fA[:, :],
                           in_=x_d[c * 128:(c + 1) * 128, :])
                p.waitm(p.v, mx)
                p.waitm(p.s, mx)
                mk = emit_ln_chunk(p, com, fA[:, :], xn_scr[:, :], scr,
                                   width=D, quant=False)
                p.waitm(p.t, mk)
                for fc in range(DC):
                    transpose_tile(
                        p, com, tpool,
                        xn_scr[:, fc * 128:(fc + 1) * 128],
                        xn_fm[:, fc * T + c * 128: fc * T + (c + 1) * 128])
            p.waitm(p.t, p.mark("v"))

            for which, w_d in (("q", wq_d), ("k", wk_d)):
                for h in range(16):
                    slot = h % 2
                    wstr_l.load(p, slot, w_d[h].rearrange("p k n -> p (k n)"))
                    wstr_l.wait_loaded(p, p.t, slot)
                    last = None
                    for sbk in range(2):
                        mats = [(wstr_l.bufs[slot][:, k * 128:(k + 1) * 128],
                                 xn_fm[:, k * T + sbk * 512: k * T + sbk * 512 + 512])
                                for k in range(DC)]
                        if which == "q":
                            dst = q_fm[:, h * T + sbk * 512: h * T + sbk * 512 + 512]

                            def evac(eng, pt, dst=dst):
                                eng.tensor_scalar_mul(dst, pt[:, :], SCALE)

                            last = mm_group(p, ppool, mats, evac)
                        else:
                            def evac(eng, pt, sbk=sbk):
                                p.waitm(eng, (kst[0], kst[0].v))
                                p.waitm(eng, (kst[1], kst[1].v))
                                eng.tensor_scalar_mul(st2[0][:, :], pt[:, :],
                                                      flags[:, 0:1])
                                eng.tensor_scalar_mul(st2[1][:, :], pt[:, :],
                                                      flags[:, 1:2])

                            last = mm_group(p, ppool, mats, evac)
                            p.waitm(p.y, p.mark("v"))
                            p.dma(p.y, kst[0],
                                  out=ck_in[h, :, sbk * 512: sbk * 512 + 512],
                                  in_=st2[0][:, :])
                            p.dma(p.y, kst[1],
                                  out=ck_in[h, :, 1024 + sbk * 512: 1024 + sbk * 512 + 512],
                                  in_=st2[1][:, :])
                    wstr_l.set_free(p, slot, last)

            for nb in range(4):
                slot = 0
                wstr.load(p, slot, wv_d[nb].rearrange("p k n -> p (k n)"))
                wstr.wait_loaded(p, p.t, slot)
                last = None
                for c in range(TC):
                    mats = [(xn_fm[:, k * T + c * 128: k * T + (c + 1) * 128],
                             wstr.bufs[slot][:, k * 512:(k + 1) * 512])
                            for k in range(DC)]

                    def evac(eng, pt):
                        p.waitm(eng, (kst[0], kst[0].v))
                        p.waitm(eng, (kst[1], kst[1].v))
                        eng.tensor_scalar_mul(st2[0][:, :], pt[:, :],
                                              flags[:, 0:1])
                        eng.tensor_scalar_mul(st2[1][:, :], pt[:, :],
                                              flags[:, 1:2])

                    last = mm_group(p, ppool, mats, evac)
                    p.waitm(p.y, p.mark("v"))
                    eh = nb // 2
                    ec = (nb % 2) * 512
                    p.dma(p.y, kst[0],
                          out=cv_in[eh, c * 128:(c + 1) * 128, ec:ec + 512],
                          in_=st2[0][:, :])
                    p.dma(p.y, kst[1],
                          out=cv_in[eh, 1024 + c * 128: 1024 + (c + 1) * 128,
                                    ec:ec + 512],
                          in_=st2[1][:, :])
                wstr.set_free(p, slot, last)

            p.wait(p.g, kst[0])
            p.wait(p.g, kst[1])
            for j in range(4):
                p.inc_cc(p.g.collective_compute(
                    "AllReduce", ALU.add, replica_groups=RG_PAIRS,
                    ins=[ck_in[j * 4:(j + 1) * 4].rearrange("h p s -> (h p) s")],
                    outs=[ck_out[j * 4:(j + 1) * 4].rearrange("h p s -> (h p) s")]),
                    ccs[j])
            for j in range(2):
                p.inc_cc(p.g.collective_compute(
                    "AllReduce", ALU.add, replica_groups=RG_PAIRS,
                    ins=[cv_in[j]], outs=[cv_out[j]]), ccs[4 + j])

            for grp in range(4):
                p.wait(p.y, ccs[grp], 1)
                p.wait(p.y, ccs[4 + grp // 2], 1)
                if grp > 0:
                    for w_ in spool.wrote + ppool.wrote:
                        if w_.v > 0:
                            p.wait(p.y, w_, w_.v)
                mkld = p.dma(p.y, kld,
                    out=kg[:, :].rearrange("p (h s) -> p h s", h=4),
                    in_=ck_out[grp * 4:(grp + 1) * 4].rearrange(
                        "h p s -> p h s"))
                mvld = p.dma(p.y, vld,
                    out=vg[:, :].rearrange("p (kc e) -> p kc e", kc=16),
                    in_=cv_out[grp // 2][:, (grp % 2) * 512:
                                         (grp % 2) * 512 + 512].rearrange(
                        "(kc p) e -> p kc e", p=128))
                p.waitm(p.t, mkld)
                p.waitm(p.t, mvld)
                for c in range(TC):
                    p.waitm(p.y, p.mark("v"))
                    mm = p.dma(p.y, mld_s, out=mask_sb[:, :],
                               in_=mask_d[c * 128:(c + 1) * 128, :])
                    p.waitm(p.v, mm)
                    for h8 in range(4):
                        h = grp * 4 + h8
                        i = spool.next()
                        pt = spool.tiles[i]
                        w = spool.wrote[i]
                        p.waitm(p.t, spool.freed[i])
                        for kb in range(4):
                            p.inc(p.t.matmul(
                                pt[:, kb * 512:(kb + 1) * 512],
                                q_fm[:, h * T + c * 128: h * T + (c + 1) * 128],
                                kg[:, h8 * 2048 + kb * 512: h8 * 2048 + (kb + 1) * 512],
                                start=True, stop=True),
                                w) if kb == 3 else p.t.matmul(
                                pt[:, kb * 512:(kb + 1) * 512],
                                q_fm[:, h * T + c * 128: h * T + (c + 1) * 128],
                                kg[:, h8 * 2048 + kb * 512: h8 * 2048 + (kb + 1) * 512],
                                start=True, stop=True)
                        p.wait(p.v, w, w.v)
                        p.v.tensor_tensor(S_sb[:, :], pt[:, :], mask_sb[:, :],
                                          ALU.add)
                        spool.freed[i] = p.mark("v")
                        p.v.reduce_max(scr["m"][:, :], S_sb[:, :], axis=AX.X,
                                       negate=True)
                        p.waitm(p.s, p.mark("v"))
                        p.s.activation(out=P_sb[:, :], in_=S_sb[:, :],
                                       func=AF.Exp, bias=scr["m"][:, :])
                        p.waitm(p.v, p.mark("s"))
                        p.v.reduce_sum(scr["q"][:, :], P_sb[:, :], axis=AX.X)
                        p.v.reciprocal(scr["r"][:, :], scr["q"][:, :])
                        mkp = p.mark("v")
                        p.waitm(p.t, mkp)
                        for kc in range(16):
                            transpose_tile(
                                p, com, tpool,
                                P_sb[:, kc * 128:(kc + 1) * 128],
                                PT_sb[:, kc * 128:(kc + 1) * 128])
                        p.waitm(p.t, p.mark("v"))
                        mats = [(PT_sb[:, kc * 128:(kc + 1) * 128],
                                 vg[:, kc * 512 + h8 * 128: kc * 512 + (h8 + 1) * 128])
                                for kc in range(16)]
                        dst = o_tok[:, c * D + h * 128: c * D + (h + 1) * 128]

                        def evac(eng, pt2, dst=dst):
                            eng.tensor_scalar_mul(dst, pt2[:, 0:128],
                                                  scr["r"][:, :])

                        mm_group(p, ppool, mats, evac, ncols=128)

            m_o = p.mark("v")
            p.waitm(p.t, m_o)
            for c in range(TC):
                for fc in range(DC):
                    transpose_tile(
                        p, com, tpool,
                        o_tok[:, c * D + fc * 128: c * D + (fc + 1) * 128],
                        o_fm[:, fc * T + c * 128: fc * T + (c + 1) * 128])
            p.waitm(p.t, p.mark("v"))
            free_markers = [None, None]
            xres_free = [None, None]
            for nb in range(4):
                slot = 0
                wstr.load(p, slot, wo_d[nb].rearrange("p k n -> p (k n)"))
                wstr.wait_loaded(p, p.t, slot)
                last = None
                for c in range(TC):
                    mats = [(o_fm[:, k * T + c * 128: k * T + (c + 1) * 128],
                             wstr.bufs[slot][:, k * 512:(k + 1) * 512])
                            for k in range(DC)]
                    ob = (nb * TC + c) % 2
                    p.waitm(p.y, xres_free[ob])
                    mxr = p.dma(
                        p.y, xrl[ob], out=xres[ob][:, :],
                        in_=x_d[c * 128:(c + 1) * 128, nb * 512:(nb + 1) * 512])

                    def evac(eng, pt, ob=ob, mxr=mxr):
                        p.waitm(eng, free_markers[ob])
                        p.waitm(eng, mxr)
                        eng.tensor_add(outst[ob][:, :], xres[ob][:, :], pt[:, :])

                    last = mm_group(p, ppool, mats, evac)
                    xres_free[ob] = p.mark("v")
                    p.waitm(p.y, xres_free[ob])
                    free_markers[ob] = p.dma(
                        p.y, osf[ob],
                        out=xo_d[c * 128:(c + 1) * 128, nb * 512:(nb + 1) * 512],
                        in_=outst[ob][:, :])
                wstr.set_free(p, slot, last)
            p.wait(p.y, osf[0])
            p.wait(p.y, osf[1])

        run_walk(nc, block, sems, walk,
                 resettables=[ppool, tpool, spool, wstr, wstr_l])
    return nc


def make_l1ab_mask(parity):
    q = np.arange(T)[:, None] + parity * T
    k = np.arange(2048)[None, :]
    return np.where(k <= q, np.float32(0.0), np.float32(-1e9)).astype(BF)


def make_flags(parity):
    f = np.zeros((128, 2), np.float32)
    f[:, 0] = 1.0 if parity == 0 else 0.0
    f[:, 1] = 0.0 if parity == 0 else 1.0
    return f


def run_l1ab(x_shards, Wq, Wk, Wv, Wo):
    nc = build_l1ab()
    wq_l = lhsT_pack(Wq)
    wk_l = lhsT_pack(Wk)
    wv_p = rhs_pack(Wv)
    wo_p = rhs_pack(Wo)
    masks = [make_l1ab_mask(i % 2) for i in range(2)]
    flags = [make_flags(i % 2) for i in range(2)]
    in_maps = [dict(x=x_shards[i], wq_l=wq_l, wk_l=wk_l, wv_p=wv_p, wo_p=wo_p,
                    mask=masks[i % 2], flags=flags[i % 2])
               for i in range(NCORES)]
    return _run(nc, in_maps, "xo")





def build_l1aa(detect_races=True):
    nc = bass.Bass("TRN2", target_bir_lowering=False,
                   detect_race_conditions=detect_races)
    x_d = nc.dram_tensor("x", [T, D], F32, kind="ExternalInput").ap()
    wqkv_d = nc.dram_tensor("wqkv_p", [12, 128, 16, 512], F32,
                            kind="ExternalInput").ap()
    wg_d = nc.dram_tensor("wg_p", [4, 128, 16, 512], F32,
                          kind="ExternalInput").ap()
    wo_d = nc.dram_tensor("wo_p", [4, 128, 16, 512], F32,
                          kind="ExternalInput").ap()
    slope_d = nc.dram_tensor("slope_r", [1, 16], F32, kind="ExternalInput").ap()
    rms_d = nc.dram_tensor("rms_p", [128, 16], F32, kind="ExternalInput").ap()
    flags_d = nc.dram_tensor("flags", [128, 2], F32, kind="ExternalInput").ap()
    iotaq_d = nc.dram_tensor("iotaq", [128, 2], F32, kind="ExternalInput").ap()
    iotak_d = nc.dram_tensor("iotak", [128, 2], F32, kind="ExternalInput").ap()
    lmm_d = nc.dram_tensor("lmm", [128, 512], BF16, kind="ExternalInput").ap()
    trim_d = nc.dram_tensor("trim", [128, 512], BF16, kind="ExternalInput").ap()
    xo_d = nc.dram_tensor("xo", [T, D], F32, kind="ExternalOutput").ap()

    kv_in = nc.dram_tensor("kv_in", [128, 2048], F32, kind="Internal").ap()
    kv_out = nc.dram_tensor("kv_out", [128, 2048], F32, kind="Internal").ap()

    with ExitStack() as st:
        sems = make_sems(nc, st)

        def sem(name):
            s = Sem(st.enter_context(nc.semaphore(name)))
            sems[name] = s
            return s

        com = Common(nc, st)
        sb = lambda n, sh, dt: st.enter_context(nc.sbuf_tensor(n, sh, dt))

        xn_fm = sb("xn_fm", [128, DC * T], BF16)
        qkv_h = sb("qkv_h", [128, 3 * 8 * 1024], BF16)
        o_tok = sb("o_tok", [128, TC * D], BF16)
        qs_fm = sb("qs_fm", [128, 16 * T], BF16)
        kv_run = sb("kv_run", [128, 2048], F32)
        fA = sb("fA", [128, 2048], F32)
        fB = sb("fB", [128, 2048], F32)
        xn_scr = sb("xn_scr", [128, 2048], BF16)
        kvs_bf = xn_scr
        qf = sb("qf", [128, 256], BF16)
        kf = sb("kf", [128, 256], BF16)
        qs_st = sb("qs_st", [128, 256], BF16)
        ks_st = sb("ks_st", [128, 256], BF16)
        kvb = sb("kvb", [128, 128], BF16)
        M_sb = sb("M_sb", [128, 512], BF16)
        ddTh = sb("ddTh", [128, 512], BF16)
        sil = sb("sil", [128, 512], BF16)
        outst = [sb(f"outst{i}", [128, 512], F32) for i in range(2)]
        xres = [sb(f"xres{i}", [128, 512], F32) for i in range(2)]
        flags = sb("flags_sb", [128, 2], F32)
        srow = sb("srow", [1, 16], F32)
        negS = sb("negS", [128, 16], F32)
        qd_all = sb("qd_all", [128, 32], F32)
        kd_all = sb("kd_all", [128, 32], F32)
        bd_all = sb("bd_all", [128, 16], F32)
        bdjs = sb("bdjs", [128, 64], F32)
        rstdc = sb("rstdc", [128, TC], F32)
        iotaq = sb("iotaq_sb", [128, 2], F32)
        iotak = sb("iotak_sb", [128, 2], F32)
        lmm = sb("lmm_sb", [128, 512], BF16)
        trim = sb("trim_sb", [128, 512], BF16)
        rmsw = sb("rmsw_sb", [128, 16], F32)
        scr = dict(m=sb("scr_m", [128, 1], F32), q=sb("scr_q", [128, 1], F32),
                   r=sb("scr_r", [128, 1], F32), t1=sb("scr_t1", [128, 1], F32),
                   sq=fB)
        og_tok = qs_fm
        og_fm = qkv_h

        ld = sem("ld"); xld = sem("xld"); cld = sem("cld")
        kvst = sem("kvst"); kvld = sem("kvld")
        osf = [sem("osf0"), sem("osf1")]
        xrl = [sem("xrl0"), sem("xrl1")]
        cckv = sem("cckv")

        ppool = PsumPool(nc, st, sems, "pp", 4)
        tpool = PsumPool(nc, st, sems, "tp", 2, shape=(128, 1024), dtype=BF16)
        wstr = WStream(nc, st, sems, "ws", 16 * 512, BF16, nbuf=1, queue="g")

        block = st.enter_context(nc.Block())

        def walk(p):
            mld = com.load(p, ld)
            p.waitm(p.v, mld)
            p.waitm(p.s, mld)
            p.waitm(p.t, mld)
            p.dma(p.y, cld, out=flags[:, :], in_=flags_d)
            p.dma(p.y, cld, out=srow[:, :], in_=slope_d)
            p.dma(p.y, cld, out=iotaq[:, :], in_=iotaq_d)
            p.dma(p.y, cld, out=iotak[:, :], in_=iotak_d)
            p.dma(p.y, cld, out=lmm[:, :], in_=lmm_d)
            p.dma(p.y, cld, out=trim[:, :], in_=trim_d)
            mc = p.dma(p.y, cld, out=rmsw[:, :], in_=rms_d)
            p.waitm(p.v, mc)
            p.waitm(p.s, mc)
            p.waitm(p.t, mc)

            i = ppool.next()
            pt = ppool.tiles[i]
            p.inc(p.t.matmul(pt[:, 0:16], com.ones_row[:, :], srow[:, :],
                             start=True, stop=True), ppool.wrote[i])
            p.wait(p.v, ppool.wrote[i], ppool.wrote[i].v)
            p.v.tensor_scalar_mul(negS[:, :], pt[:, 0:16], -1.0)
            ppool.freed[i] = p.mark("v")
            p.waitm(p.s, p.mark("v"))
            for h in range(16):
                nsc = negS[:, h:h + 1]
                for j in range(2):
                    p.s.activation(out=qd_all[:, h * 2 + j: h * 2 + j + 1],
                                   in_=iotaq[:, j:j + 1], func=AF.Exp, scale=nsc)
                    p.s.activation(out=kd_all[:, h * 2 + j: h * 2 + j + 1],
                                   in_=iotak[:, j:j + 1], func=AF.Exp, scale=nsc)
                p.s.activation(out=bd_all[:, h:h + 1], in_=com.cc(256.0),
                               func=AF.Exp, scale=nsc)
            for j in range(4):
                p.s.activation(out=bdjs[:, j * 16:(j + 1) * 16], in_=negS[:, :],
                               func=AF.Exp, scale=com.cc(256.0 * j))
            p.waitm(p.v, p.mark("s"))
            for j in range(4):
                p.v.tensor_scalar_mul(bdjs[:, j * 16:(j + 1) * 16],
                                      bdjs[:, j * 16:(j + 1) * 16],
                                      flags[:, 1:2])
            p.v.memset(kv_run[:, :], 0.0)

            for c in range(TC):
                if c > 0:
                    p.waitm(p.y, p.mark("v"))
                    p.waitm(p.y, p.mark("s"))
                mx = p.dma(p.y, xld, out=fA[:, :],
                           in_=x_d[c * 128:(c + 1) * 128, :])
                p.waitm(p.v, mx)
                p.waitm(p.s, mx)
                mk = emit_ln_chunk(p, com, fA[:, :], xn_scr[:, :], scr,
                                   width=D, quant=False)
                p.waitm(p.t, mk)
                for fc in range(DC):
                    transpose_tile(
                        p, com, tpool,
                        xn_scr[:, fc * 128:(fc + 1) * 128],
                        xn_fm[:, fc * T + c * 128: fc * T + (c + 1) * 128])
            p.waitm(p.t, p.mark("v"))

            for half in range(2):
                for nb in range(12):
                    wstr.load(p, 0, wqkv_d[nb].rearrange("p k n -> p (k n)"))
                    wstr.wait_loaded(p, p.t, 0)
                    last = None
                    which = nb // 4
                    col = (nb % 4) * 512
                    for cl in range(4):
                        c = half * 4 + cl
                        mats = [(xn_fm[:, k * T + c * 128: k * T + (c + 1) * 128],
                                 wstr.bufs[0][:, k * 512:(k + 1) * 512])
                                for k in range(DC)]
                        dst = qkv_h[:, which * 8192 + cl * D + col:
                                    which * 8192 + cl * D + col + 512]

                        def evac(eng, pt, dst=dst):
                            eng.activation(out=dst, in_=pt[:, :], func=AF.Silu)

                        last = mm_group(p, ppool, mats, evac, evac_eng="s")
                    wstr.set_free(p, 0, last)
                m_qkv = p.mark("s")
                p.waitm(p.v, m_qkv)
                p.waitm(p.t, m_qkv)

                for h in range(16):
                    hcol = h * 128
                    nsc = negS[:, h:h + 1]
                    for mcx in range(2):
                        base = lmm[:, mcx * 256:(mcx + 1) * 256]
                        p.waitm(p.v, p.mark("s"))
                        p.v.tensor_scalar(fA[:, 0:256], base, nsc, 0.0,
                                          op0=ALU.mult, op1=ALU.min)
                        p.waitm(p.s, p.mark("v"))
                        p.s.activation(out=fA[:, 256:512], in_=fA[:, 0:256],
                                       func=AF.Exp)
                        p.waitm(p.v, p.mark("s"))
                        p.v.tensor_tensor(
                            ddTh[:, mcx * 256:(mcx + 1) * 256],
                            fA[:, 256:512], trim[:, mcx * 256:(mcx + 1) * 256],
                            ALU.mult)
                    for blh in range(2):
                        bl = half * 2 + blh
                        for j in range(2):
                            cl = blh * 2 + j
                            c = half * 4 + cl
                            src_q = qkv_h[:, cl * D + hcol: cl * D + hcol + 128]
                            src_k = qkv_h[:, 8192 + cl * D + hcol:
                                          8192 + cl * D + hcol + 128]
                            p.v.tensor_scalar_mul(
                                qs_st[:, j * 128:(j + 1) * 128], src_q,
                                qd_all[:, h * 2 + j: h * 2 + j + 1])
                            p.v.tensor_scalar_mul(
                                ks_st[:, j * 128:(j + 1) * 128], src_k,
                                kd_all[:, h * 2 + j: h * 2 + j + 1])
                            mkv = p.mark("v")
                            p.waitm(p.t, mkv)
                            transpose_tile(p, com, tpool, src_q,
                                           qf[:, j * 128:(j + 1) * 128])
                            transpose_tile(p, com, tpool, src_k,
                                           kf[:, j * 128:(j + 1) * 128])
                            transpose_tile(
                                p, com, tpool,
                                qs_st[:, j * 128:(j + 1) * 128],
                                qs_fm[:, h * T + c * 128: c * 128 + h * T + 128])
                        p.v.tensor_copy(kvb[:, :], kv_run[:, hcol:hcol + 128])
                        m_pre = p.mark("v")
                        p.waitm(p.t, m_pre)
                        for mcx in range(2):
                            def evacm(eng, pt, mcx=mcx):
                                eng.tensor_tensor(
                                    M_sb[:, mcx * 256:(mcx + 1) * 256],
                                    pt[:, 0:256],
                                    ddTh[:, mcx * 256:(mcx + 1) * 256],
                                    ALU.mult)

                            mm_group(p, ppool,
                                     [(kf[:, mcx * 128:(mcx + 1) * 128],
                                       qf[:, 0:256])],
                                     evacm, ncols=256)
                        m_M = p.mark("v")
                        p.waitm(p.t, m_M)
                        for lc in range(2):
                            c = bl * 2 + lc
                            cl = blh * 2 + lc
                            mats = [(M_sb[:, mcx * 256 + lc * 128:
                                          mcx * 256 + lc * 128 + 128],
                                     qkv_h[:, 16384 + (blh * 2 + mcx) * D + hcol:
                                           16384 + (blh * 2 + mcx) * D + hcol + 128])
                                    for mcx in range(2)]
                            mats.append((qs_fm[:, h * T + c * 128:
                                               h * T + c * 128 + 128],
                                         kvb[:, :]))
                            dst = o_tok[:, c * D + hcol: c * D + hcol + 128]

                            def evaco(eng, pt, dst=dst):
                                eng.tensor_copy(dst, pt[:, 0:128])

                            mm_group(p, ppool, mats, evaco, ncols=128)
                        mats = [(ks_st[:, mcx * 128:(mcx + 1) * 128],
                                 qkv_h[:, 16384 + (blh * 2 + mcx) * D + hcol:
                                       16384 + (blh * 2 + mcx) * D + hcol + 128])
                                for mcx in range(2)]

                        def evacu(eng, pt, hcol=hcol, h=h):
                            eng.scalar_tensor_tensor(
                                kv_run[:, hcol:hcol + 128],
                                kv_run[:, hcol:hcol + 128],
                                bd_all[:, h:h + 1], pt[:, 0:128],
                                op0=ALU.mult, op1=ALU.add)

                        mm_group(p, ppool, mats, evacu, ncols=128)

            p.v.tensor_scalar_mul(fB[:, :], kv_run[:, :], flags[:, 0:1])
            p.waitm(p.y, p.mark("v"))
            p.dma(p.y, kvst, out=kv_in, in_=fB[:, :])
            p.wait(p.g, kvst)
            p.inc_cc(p.g.collective_compute(
                "AllReduce", ALU.add, replica_groups=RG_PAIRS,
                ins=[kv_in], outs=[kv_out]), cckv)
            p.wait(p.y, cckv, 1)
            mkv2 = p.dma(p.y, kvld, out=fB[:, :], in_=kv_out)
            p.waitm(p.v, mkv2)
            p.v.tensor_copy(kvs_bf[:, :], fB[:, :])
            m_seed = p.mark("v")
            p.waitm(p.t, m_seed)

            for bl in range(4):
                for h in range(16):
                    hcol = h * 128
                    for lc in range(2):
                        c = bl * 2 + lc
                        dst = o_tok[:, c * D + hcol: c * D + hcol + 128]

                        def evacc(eng, pt, dst=dst, bl=bl, h=h):
                            eng.scalar_tensor_tensor(
                                dst, pt[:, 0:128],
                                bdjs[:, bl * 16 + h: bl * 16 + h + 1], dst,
                                op0=ALU.mult, op1=ALU.add)

                        mm_group(p, ppool,
                                 [(qs_fm[:, h * T + c * 128: h * T + c * 128 + 128],
                                   kvs_bf[:, hcol:hcol + 128])],
                                 evacc, ncols=128)

            for c in range(TC):
                p.waitm(p.s, p.mark("v"))
                p.s.activation(out=fB[:, :], in_=o_tok[:, c * D:(c + 1) * D],
                               func=AF.Square)
                p.waitm(p.v, p.mark("s"))
                p.v.reduce_sum(scr["m"][:, :], fB[:, :], axis=AX.X)
                p.v.tensor_scalar_mul(scr["m"][:, :], scr["m"][:, :],
                                      1.0 / 2048.0)
                p.waitm(p.s, p.mark("v"))
                p.s.activation(out=scr["t1"][:, :], in_=scr["m"][:, :],
                               func=AF.Sqrt, bias=com.cc(1e-6))
                p.waitm(p.v, p.mark("s"))
                p.v.reciprocal(rstdc[:, c:c + 1], scr["t1"][:, :])

            for nb in range(4):
                wstr.load(p, 0, wg_d[nb].rearrange("p k n -> p (k n)"))
                wstr.wait_loaded(p, p.t, 0)
                last = None
                for c in range(TC):
                    mats = [(xn_fm[:, k * T + c * 128: k * T + (c + 1) * 128],
                             wstr.bufs[0][:, k * 512:(k + 1) * 512])
                            for k in range(DC)]

                    def evacg(eng, pt):
                        p.waitm(eng, p.mark("v"))
                        eng.activation(out=sil[:, :], in_=pt[:, :],
                                       func=AF.Sigmoid)

                    last = mm_group(p, ppool, mats, evacg, evac_eng="s")
                    p.waitm(p.v, p.mark("s"))
                    dst = og_tok[:, c * D + nb * 512: c * D + (nb + 1) * 512]
                    p.v.scalar_tensor_tensor(
                        dst, o_tok[:, c * D + nb * 512: c * D + (nb + 1) * 512],
                        rstdc[:, c:c + 1], sil[:, :],
                        op0=ALU.mult, op1=ALU.mult)
                wstr.set_free(p, 0, last)

            p.waitm(p.t, p.mark("v"))
            for c in range(TC):
                for fc in range(DC):
                    i2 = tpool.next()
                    pt2 = tpool.tiles[i2]
                    w2 = tpool.wrote[i2]
                    p.waitm(p.t, tpool.freed[i2])
                    p.inc(p.t.transpose(
                        pt2[:, 0:128],
                        og_tok[:, c * D + fc * 128: c * D + (fc + 1) * 128],
                        com.ident[:, :]), w2)
                    p.wait(p.v, w2, w2.v)
                    p.v.tensor_scalar_mul(
                        og_fm[:, fc * T + c * 128: fc * T + (c + 1) * 128],
                        pt2[:, 0:128], rmsw[:, fc:fc + 1])
                    tpool.freed[i2] = p.mark("v")
            p.waitm(p.t, p.mark("v"))

            free_markers = [None, None]
            xres_free = [None, None]
            for nb in range(4):
                wstr.load(p, 0, wo_d[nb].rearrange("p k n -> p (k n)"))
                wstr.wait_loaded(p, p.t, 0)
                last = None
                for c in range(TC):
                    mats = [(og_fm[:, k * T + c * 128: k * T + (c + 1) * 128],
                             wstr.bufs[0][:, k * 512:(k + 1) * 512])
                            for k in range(DC)]
                    ob = (nb * TC + c) % 2
                    p.waitm(p.y, xres_free[ob])
                    mxr = p.dma(
                        p.y, xrl[ob], out=xres[ob][:, :],
                        in_=x_d[c * 128:(c + 1) * 128, nb * 512:(nb + 1) * 512])

                    def evacf(eng, pt, ob=ob, mxr=mxr):
                        p.waitm(eng, free_markers[ob])
                        p.waitm(eng, mxr)
                        eng.tensor_add(outst[ob][:, :], xres[ob][:, :], pt[:, :])

                    last = mm_group(p, ppool, mats, evacf)
                    xres_free[ob] = p.mark("v")
                    p.waitm(p.y, xres_free[ob])
                    free_markers[ob] = p.dma(
                        p.y, osf[ob],
                        out=xo_d[c * 128:(c + 1) * 128, nb * 512:(nb + 1) * 512],
                        in_=outst[ob][:, :])
                wstr.set_free(p, 0, last)
            p.wait(p.y, osf[0])
            p.wait(p.y, osf[1])

        run_walk(nc, block, sems, walk, resettables=[ppool, tpool, wstr])
    return nc


def l1aa_consts():
    i = np.arange(128, dtype=np.float32)
    iotaq = np.stack([i + 1, 128 + i + 1], axis=1)
    iotak = np.stack([255 - i, 255 - 128 - i], axis=1)
    l = np.arange(256, dtype=np.float32)[None, :]
    lmm = np.concatenate([l - i[:, None], l - (128 + i[:, None])], axis=1)
    trim = (lmm >= 0).astype(BF)
    return (np.ascontiguousarray(iotaq), np.ascontiguousarray(iotak),
            np.ascontiguousarray(lmm.astype(BF)), np.ascontiguousarray(trim))


def run_l1aa(x_shards, slope_rate, w_qkv, w_gate, w_out, rms_weight):
    nc = build_l1aa()
    wqkv_p = rhs_pack(w_qkv)
    wg_p = rhs_pack(w_gate)
    wo_p = rhs_pack(w_out)
    slope_r = np.ascontiguousarray(
        np.asarray(slope_rate, np.float32).reshape(1, 16))
    rms_p = col_pack(rms_weight)
    iotaq, iotak, lmm, trim = l1aa_consts()
    in_maps = [dict(x=x_shards[i], wqkv_p=wqkv_p, wg_p=wg_p, wo_p=wo_p,
                    slope_r=slope_r, rms_p=rms_p, flags=make_flags(i % 2),
                    iotaq=iotaq, iotak=iotak, lmm=lmm, trim=trim)
               for i in range(NCORES)]
    return _run(nc, in_maps, "xo")





def build_l0ab(detect_races=True):
    nc = bass.Bass("TRN2", target_bir_lowering=False,
                   detect_race_conditions=detect_races)
    x_d = nc.dram_tensor("x", [T, D], F32, kind="ExternalInput").ap()
    winz_d = nc.dram_tensor("winz_p", [8, 128, 16, 512], BF16,
                            kind="ExternalInput").ap()
    winx_d = nc.dram_tensor("winx_l", [33, 128, 16, 128], BF16,
                            kind="ExternalInput").ap()
    windt_d = nc.dram_tensor("windt_p", [16, 128, 32], BF16,
                             kind="ExternalInput").ap()
    wout_d = nc.dram_tensor("wout_p", [4, 4, 128, 8, 512], BF16,
                            kind="ExternalInput").ap()
    wconv_d = nc.dram_tensor("wconv_p", [128, 132], F32, kind="ExternalInput").ap()
    bconv_d = nc.dram_tensor("bconv_p", [128, 33], F32, kind="ExternalInput").ap()
    dtb_d = nc.dram_tensor("dtb_r", [1, 32], F32, kind="ExternalInput").ap()
    alog_d = nc.dram_tensor("alog_r", [1, 32], F32, kind="ExternalInput").ap()
    dco_d = nc.dram_tensor("dco_r", [1, 32], F32, kind="ExternalInput").ap()
    ucum_d = nc.dram_tensor("ucum", [128, 512], F32, kind="ExternalInput").ap()
    trim_d = nc.dram_tensor("trim", [128, 512], BF16, kind="ExternalInput").ap()
    flags_d = nc.dram_tensor("flags", [128, 2], F32, kind="ExternalInput").ap()
    xo_d = nc.dram_tensor("xo", [T, D], F32, kind="ExternalOutput").ap()

    z_dram = nc.dram_tensor("z_sc", [T, 4096], F32, kind="Internal").ap()
    xbc_dram = nc.dram_tensor("xbc_sc", [33, 128, T], F32, kind="Internal").ap()
    xpost_dram = nc.dram_tensor("xpost_sc", [32, 128, T], F32, kind="Internal").ap()
    y_dram = nc.dram_tensor("y_sc", [T, 4096], F32, kind="Internal").ap()
    acum_dram = nc.dram_tensor("acum_sc", [4, 32, 256], F32, kind="Internal").ap()
    bnd_in = nc.dram_tensor("bnd_in", [128, 99], F32, kind="Internal").ap()
    bnd_out = nc.dram_tensor("bnd_out", [128, 99], F32, kind="Internal").ap()
    st_in = nc.dram_tensor("st_in", [64, 4096], F32, kind="Internal").ap()
    st_out = nc.dram_tensor("st_out", [64, 4096], F32, kind="Internal").ap()
    cbc_dram = nc.dram_tensor("cbc_sc", [128, 1024], F32, kind="Internal").ap()

    with ExitStack() as st:
        sems = make_sems(nc, st)

        def sem(name):
            s = Sem(st.enter_context(nc.semaphore(name)))
            sems[name] = s
            return s

        com = Common(nc, st)
        sb = lambda n, sh, dt: st.enter_context(nc.sbuf_tensor(n, sh, dt))

        u_fm = sb("u_fm", [128, DC * T], BF16)
        fA = sb("fA", [128, 2048], F32)
        fB = sb("fB", [128, 2048], F32)
        xn_scr = sb("xn_scr", [128, 2048], BF16)
        zst = [sb(f"zst{i}", [128, 512], F32) for i in range(2)]
        dt_raw = sb("dt_raw", [128, 256], F32)
        adt_tok = dt_raw
        dt_tok = sb("dt_tok", [128, 256], F32)
        ws_cv = sb("ws_cv", [128, 1027], F32)
        convf = sb("convf", [128, 1024], F32)
        bnd_my = sb("bnd_my", [128, 99], F32)
        bnd_eff = sb("bnd_eff", [128, 99], F32)
        Bc_fm = sb("Bc_fm", [128, 1024], BF16)
        C_fm = sb("C_fm", [64, 1024], BF16)
        acum_sc = sb("acum_sc_sb", [32, 256], F32)
        sd_sc = sb("sd_sc", [32, 256], F32)
        nacum_tok = sb("nacum_tok", [128, 256], F32)
        sdo_tok = sb("sdo_tok", [128, 256], F32)
        decay_tok = sb("decay_tok", [128, 256], F32)
        pisdo_tok = sb("pisdo_tok", [128, 256], F32)
        GT_sb = sb("GT_sb", [128, 512], F32)
        M_bf = sb("M_bf", [128, 512], BF16)
        rb_sb = sb("rb_sb", [128, 256], F32)
        xload = [sb(f"xload{i}", [128, 256], F32) for i in range(2)]
        X_bf = sb("X_bf", [128, 256], BF16)
        xs_st = sb("xs_st", [128, 256], BF16)
        xd_st = sb("xd_st", [128, 256], F32)
        bd_st = sb("bd_st", [128, 128], BF16)
        yst = [sb(f"yst{i}", [128, 128], F32) for i in range(2)]
        arow = [sb(f"arow{i}", [1, 256], F32) for i in range(2)]
        S_run = sb("S_run", [64, 4096], F32)
        S_bf = sb("S_bfh", [64, 128], BF16)
        seed_bf = sb("seed_bf", [64, 4096], BF16)
        cd64 = sb("cd64", [64, 32], F32)
        crow = sb("crow", [1, 32], F32)
        cd_row = sb("cd_row", [1, 32], F32)
        pi_row = sb("pi_row", [1, 32], F32)
        pif_row = sb("pif_row", [1, 32], F32)
        dtb_bc = sb("dtb_bc", [128, 32], F32)
        A_bc = sb("A_bc", [128, 32], F32)
        D_bc = sb("D_bc", [128, 32], F32)
        pibc = sb("pibc", [128, 32], F32)
        rowbuf = sb("rowbuf", [1, 32], F32)
        wdt_sb = sb("wdt_sb", [128, 512], BF16)
        B_tok = wdt_sb
        ucum = sb("ucum_sb", [128, 512], F32)
        trim = sb("trim_sb", [128, 512], BF16)
        flags = sb("flags_sb", [128, 2], F32)
        wconv = sb("wconv_sb", [128, 132], F32)
        bconv = sb("bconv_sb", [128, 33], F32)
        yrow = sb("yrow", [128, 4096], F32)
        zrow = sb("zrow", [128, 4096], F32)
        bfx = sb("bfx", [128, 4096], BF16)
        gnT_c = sb("gnT_c", [128, 4096], BF16)
        yacc_c = sb("yacc_c", [128, 2048], F32)
        scr = dict(m=sb("scr_m", [128, 1], F32), q=sb("scr_q", [128, 1], F32),
                   r=sb("scr_r", [128, 1], F32), t1=sb("scr_t1", [128, 1], F32),
                   sq=fB)

        ld = sem("ld"); xld = sem("xld"); cld = sem("cld")
        zsf = [sem("zsf0"), sem("zsf1")]
        xbs = sem("xbs"); bndm = sem("bndm"); bndl = sem("bndl")
        cvl = sem("cvl"); xps = sem("xps")
        acs = sem("acs"); arl = [sem("arl0"), sem("arl1")]
        xll = [sem("xll0"), sem("xll1")]
        ysf = [sem("ysf0"), sem("ysf1")]
        stx = sem("stx"); stl = sem("stl")
        yrl = sem("yrl"); zrl = sem("zrl"); gns = sem("gns"); gnl = sem("gnl")
        osf = sem("osf"); xrl2 = sem("xrl2")
        ccb = sem("ccb"); ccst = sem("ccst"); wdt = sem("wdt")
        cfl = sem("cfl"); wofr = sem("wofr")

        ppool = PsumPool(nc, st, sems, "pp", 4)
        tpool = PsumPool(nc, st, sems, "tp", 2, shape=(128, 1024), dtype=BF16)
        tpoolf = PsumPool(nc, st, sems, "tf", 2, shape=(128, 512), dtype=F32)
        wstr = WStream(nc, st, sems, "ws", 16 * 512, BF16, nbuf=1, queue="g")
        wstx = WStream(nc, st, sems, "wx", 16 * 128, BF16, nbuf=2, queue="g")
        wsto = WStream(nc, st, sems, "wo", 8 * 512, BF16, nbuf=1, queue="g")

        block = st.enter_context(nc.Block())

        def bcast_row(p, src_row, ncol, dst, nrow=128):
            i = ppool.next()
            pt = ppool.tiles[i]
            p.waitm(p.t, ppool.freed[i])
            p.inc(p.t.matmul(pt[0:nrow, 0:ncol], com.ones_row[:, 0:nrow],
                             src_row, start=True, stop=True), ppool.wrote[i])
            p.wait(p.v, ppool.wrote[i], ppool.wrote[i].v)
            p.v.tensor_copy(dst, pt[0:nrow, 0:ncol])
            ppool.freed[i] = p.mark("v")

        def walk(p):
            mld = com.load(p, ld)
            p.waitm(p.v, mld)
            p.waitm(p.s, mld)
            p.waitm(p.t, mld)
            p.dma(p.y, cld, out=flags[:, :], in_=flags_d)
            p.dma(p.y, cld, out=ucum[:, :], in_=ucum_d)
            p.dma(p.y, cld, out=trim[:, :], in_=trim_d)
            p.dma(p.y, cld, out=wconv[:, :], in_=wconv_d)
            p.dma(p.y, cld, out=bconv[:, :], in_=bconv_d)
            p.dma(p.y, cld, out=rowbuf[:, :], in_=dtb_d)
            mc = p.dma(p.y, cld, out=crow[:, :], in_=alog_d)
            p.waitm(p.v, mc)
            p.waitm(p.s, mc)
            p.waitm(p.t, mc)
            bcast_row(p, rowbuf[:, :], 32, dtb_bc[:, :])
            bcast_row(p, crow[:, :], 32, A_bc[:, :])
            p.waitm(p.s, p.mark("v"))
            p.s.activation(out=A_bc[:, :], in_=A_bc[:, :], func=AF.Exp)
            p.waitm(p.v, p.mark("s"))
            p.v.tensor_scalar_mul(A_bc[:, :], A_bc[:, :], -1.0)
            for w_ in ppool.wrote:
                if w_.v > 0:
                    p.wait(p.y, w_, w_.v)
            mcd = p.dma(p.y, cld, out=rowbuf[:, :], in_=dco_d)
            p.waitm(p.v, mcd)
            p.waitm(p.t, mcd)
            bcast_row(p, rowbuf[:, :], 32, D_bc[:, :])
            p.v.memset(pi_row[:, :], 1.0)
            p.v.memset(S_run[:, :], 0.0)

            for c in range(TC):
                if c > 0:
                    p.waitm(p.y, p.mark("v"))
                    p.waitm(p.y, p.mark("s"))
                mx = p.dma(p.y, xld, out=fA[:, :],
                           in_=x_d[c * 128:(c + 1) * 128, :])
                p.waitm(p.v, mx)
                p.waitm(p.s, mx)
                mk = emit_ln_chunk(p, com, fA[:, :], xn_scr[:, :], scr,
                                   width=D, quant=False)
                p.waitm(p.t, mk)
                for fc in range(DC):
                    transpose_tile(
                        p, com, tpool,
                        xn_scr[:, fc * 128:(fc + 1) * 128],
                        u_fm[:, fc * T + c * 128: fc * T + (c + 1) * 128])
            p.waitm(p.t, p.mark("v"))

            zfree = [None, None]
            for nb in range(8):
                wstr.load(p, 0, winz_d[nb].rearrange("p k n -> p (k n)"))
                wstr.wait_loaded(p, p.t, 0)
                last = None
                for c in range(TC):
                    mats = [(u_fm[:, k * T + c * 128: k * T + (c + 1) * 128],
                             wstr.bufs[0][:, k * 512:(k + 1) * 512])
                            for k in range(DC)]
                    zb = (nb * TC + c) % 2

                    def evac(eng, pt, zb=zb):
                        p.waitm(eng, zfree[zb])
                        eng.tensor_copy(zst[zb][:, :], pt[:, :])

                    last = mm_group(p, ppool, mats, evac)
                    p.waitm(p.y, p.mark("v"))
                    zfree[zb] = p.dma(
                        p.y, zsf[zb],
                        out=z_dram[c * 128:(c + 1) * 128, nb * 512:(nb + 1) * 512],
                        in_=zst[zb][:, :])
                wstr.set_free(p, 0, last)

            mwd = p.dma(p.y, wdt,
                        out=wdt_sb[:, :].rearrange("p (k n) -> p k n", k=16),
                        in_=windt_d.rearrange("k p n -> p k n"))
            p.waitm(p.t, mwd)
            for c in range(TC):
                mats = [(u_fm[:, k * T + c * 128: k * T + (c + 1) * 128],
                         wdt_sb[:, k * 32:(k + 1) * 32])
                        for k in range(DC)]
                dst = dt_raw[:, c * 32:(c + 1) * 32]

                def evac(eng, pt, dst=dst):
                    eng.tensor_copy(dst, pt[:, 0:32])

                mm_group(p, ppool, mats, evac, ncols=32)

            for mb in range(33):
                slot = mb % 2
                wstx.load(p, slot, winx_d[mb].rearrange("p k n -> p (k n)"))
                wstx.wait_loaded(p, p.t, slot)
                last = None
                for sbk in range(2):
                    mats = [(wstx.bufs[slot][:, k * 128:(k + 1) * 128],
                             u_fm[:, k * T + sbk * 512: k * T + sbk * 512 + 512])
                            for k in range(DC)]
                    zb = sbk % 2

                    def evac(eng, pt, zb=zb, sbk=sbk, mb=mb):
                        p.waitm(eng, zfree[zb])
                        eng.tensor_copy(zst[zb][:, :], pt[:, :])
                        if sbk == 1:
                            eng.tensor_copy(bnd_my[:, mb * 3:(mb + 1) * 3],
                                            zst[zb][:, 509:512])

                    last = mm_group(p, ppool, mats, evac)
                    p.waitm(p.y, p.mark("v"))
                    zfree[zb] = p.dma(
                        p.y, zsf[zb],
                        out=xbc_dram[mb, :, sbk * 512: sbk * 512 + 512],
                        in_=zst[zb][:, :])
                wstx.set_free(p, slot, last)

            p.v.tensor_scalar_mul(bnd_my[:, :], bnd_my[:, :], flags[:, 0:1])
            p.waitm(p.y, p.mark("v"))
            p.dma(p.y, bndm, out=bnd_in, in_=bnd_my[:, :])
            p.wait(p.g, bndm)
            p.inc_cc(p.g.collective_compute(
                "AllReduce", ALU.add, replica_groups=RG_PAIRS,
                ins=[bnd_in], outs=[bnd_out]), ccb)
            p.wait(p.y, ccb, 1)
            mb_ = p.dma(p.y, bndl, out=bnd_eff[:, :], in_=bnd_out)
            p.waitm(p.v, mb_)
            p.v.tensor_scalar_mul(bnd_eff[:, :], bnd_eff[:, :], flags[:, 1:2])

            for c in range(TC):
                sl = slice(c * 32, (c + 1) * 32)
                p.v.tensor_tensor(dt_raw[:, sl], dt_raw[:, sl], dtb_bc[:, :],
                                  ALU.add)
            p.waitm(p.s, p.mark("v"))
            p.s.activation(out=fA[:, 0:256], in_=dt_raw[:, :], func=AF.Abs)
            p.waitm(p.v, p.mark("s"))
            p.v.tensor_scalar_mul(fA[:, 256:512], fA[:, 0:256], -1.0)
            p.waitm(p.s, p.mark("v"))
            p.s.activation(out=fA[:, 0:256], in_=fA[:, 256:512], func=AF.Exp)
            p.s.activation(out=fA[:, 256:512], in_=fA[:, 0:256], func=AF.Ln,
                           bias=1.0)
            p.waitm(p.v, p.mark("s"))
            p.v.tensor_scalar_max(dt_tok[:, :], dt_raw[:, :], 0.0)
            p.v.tensor_tensor(dt_tok[:, :], dt_tok[:, :], fA[:, 256:512],
                              ALU.add)
            for c in range(TC):
                sl = slice(c * 32, (c + 1) * 32)
                p.v.tensor_tensor(adt_tok[:, sl], dt_tok[:, sl], A_bc[:, :],
                                  ALU.mult)

            p.waitm(p.v, p.mark("s"))
            p.wait(p.y, zsf[0])
            p.wait(p.y, zsf[1])
            cfree = None
            for mb in range(33):
                p.waitm(p.y, p.mark("v"))
                mw = p.dma(p.y, cvl, out=ws_cv[:, 3:1027],
                           in_=xbc_dram[mb, :, :])
                p.waitm(p.v, mw)
                p.v.tensor_copy(ws_cv[:, 0:3], bnd_eff[:, mb * 3:(mb + 1) * 3])
                p.waitm(p.v, cfree)
                p.v.tensor_scalar_mul(convf[:, :], ws_cv[:, 0:1024],
                                      wconv[:, mb * 4: mb * 4 + 1])
                for j in range(1, 4):
                    p.v.scalar_tensor_tensor(
                        convf[:, :], ws_cv[:, j:j + 1024],
                        wconv[:, mb * 4 + j: mb * 4 + j + 1], convf[:, :],
                        op0=ALU.mult, op1=ALU.add)
                p.waitm(p.s, p.mark("v"))
                p.s.activation(out=convf[:, :], in_=convf[:, :], func=AF.Silu,
                               bias=bconv[:, mb:mb + 1])
                msl = p.mark("s")
                if mb < 32:
                    p.waitm(p.y, msl)
                    cfree = p.dma(p.y, xps, out=xpost_dram[mb, :, :],
                                  in_=convf[:, :])
                else:
                    p.waitm(p.v, msl)
                    p.v.tensor_copy(Bc_fm[:, :], convf[:, :])
                    p.waitm(p.y, msl)
                    mcb = p.dma(p.y, xps, out=cbc_dram, in_=convf[:, :])
                    p.wait(p.g, xps)
                    mcf = p.dma(p.g, cfl, out=C_fm[:, :],
                                in_=cbc_dram[64:128, :])
                    p.waitm(p.t, mcf)
                    p.waitm(p.v, mcf)
                    cfree = None
            p.waitm(p.t, p.mark("v"))
            for c in range(TC):
                i2 = tpool.next()
                pt2 = tpool.tiles[i2]
                w2 = tpool.wrote[i2]
                p.waitm(p.t, tpool.freed[i2])
                p.inc(p.t.transpose(pt2[:, 0:64],
                                    Bc_fm[0:64, c * 128:(c + 1) * 128],
                                    com.ident[0:64, 0:64]), w2)
                p.wait(p.v, w2, w2.v)
                p.v.tensor_copy(B_tok[:, c * 64:(c + 1) * 64], pt2[:, 0:64])
                tpool.freed[i2] = p.mark("v")

            ylast = [None, None]
            for sc in range(4):
                p.waitm(p.t, p.mark("v"))
                i2 = tpoolf.next()
                pt2 = tpoolf.tiles[i2]
                w2 = tpoolf.wrote[i2]
                p.waitm(p.t, tpoolf.freed[i2])
                for j in range(2):
                    c = sc * 2 + j
                    p.inc(p.t.matmul(pt2[0:32, 0:256],
                                     adt_tok[:, c * 32:(c + 1) * 32],
                                     ucum[:, j * 256:(j + 1) * 256],
                                     start=(j == 0), stop=(j == 1)),
                          w2) if j == 1 else p.t.matmul(
                        pt2[0:32, 0:256], adt_tok[:, c * 32:(c + 1) * 32],
                        ucum[:, j * 256:(j + 1) * 256],
                        start=(j == 0), stop=(j == 1))
                p.wait(p.v, w2, w2.v)
                p.v.tensor_copy(acum_sc[:, :], pt2[0:32, 0:256])
                tpoolf.freed[i2] = p.mark("v")
                p.waitm(p.y, p.mark("v"))
                p.dma(p.y, acs, out=acum_dram[sc], in_=acum_sc[:, :])
                p.waitm(p.s, p.mark("v"))
                p.s.activation(out=sd_sc[:, :], in_=acum_sc[:, :], func=AF.Exp)
                msd = p.mark("s")
                for j in range(2):
                    i2 = tpoolf.next()
                    pt2 = tpoolf.tiles[i2]
                    w2 = tpoolf.wrote[i2]
                    p.waitm(p.t, tpoolf.freed[i2])
                    p.waitm(p.t, msd)
                    p.inc(p.t.transpose(pt2[:, 0:32],
                                        sd_sc[0:32, j * 128:(j + 1) * 128],
                                        com.identf[0:32, 0:32]), w2)
                    p.wait(p.v, w2, w2.v)
                    p.v.tensor_copy(sdo_tok[:, j * 32:(j + 1) * 32],
                                    pt2[:, 0:32])
                    tpoolf.freed[i2] = p.mark("v")
                p.waitm(p.s, p.mark("v"))
                p.s.activation(out=sd_sc[:, :], in_=acum_sc[:, :], func=AF.Exp,
                               bias=acum_sc[:, 255:256],
                               scale=com.cc(-1.0)[0:32, :])
                msd = p.mark("s")
                for j in range(2):
                    i2 = tpoolf.next()
                    pt2 = tpoolf.tiles[i2]
                    w2 = tpoolf.wrote[i2]
                    p.waitm(p.t, tpoolf.freed[i2])
                    p.waitm(p.t, msd)
                    p.inc(p.t.transpose(pt2[:, 0:32],
                                        sd_sc[0:32, j * 128:(j + 1) * 128],
                                        com.identf[0:32, 0:32]), w2)
                    p.wait(p.v, w2, w2.v)
                    p.v.tensor_copy(decay_tok[:, j * 32:(j + 1) * 32],
                                    pt2[:, 0:32])
                    tpoolf.freed[i2] = p.mark("v")
                for j in range(2):
                    i2 = tpoolf.next()
                    pt2 = tpoolf.tiles[i2]
                    w2 = tpoolf.wrote[i2]
                    p.waitm(p.t, tpoolf.freed[i2])
                    p.inc(p.t.transpose(pt2[:, 0:32],
                                        acum_sc[0:32, j * 128:(j + 1) * 128],
                                        com.identf[0:32, 0:32]), w2)
                    p.wait(p.v, w2, w2.v)
                    p.v.tensor_scalar_mul(nacum_tok[:, j * 32:(j + 1) * 32],
                                          pt2[:, 0:32], -1.0)
                    tpoolf.freed[i2] = p.mark("v")
                i2 = tpoolf.next()
                pt2 = tpoolf.tiles[i2]
                w2 = tpoolf.wrote[i2]
                p.waitm(p.t, tpoolf.freed[i2])
                p.inc(p.t.transpose(pt2[0:1, 0:32], acum_sc[0:32, 255:256],
                                    com.identf[0:32, 0:32]), w2)
                p.wait(p.s, w2, w2.v)
                p.s.activation(out=cd_row[:, :], in_=pt2[0:1, 0:32], func=AF.Exp)
                tpoolf.freed[i2] = p.mark("s")
                p.waitm(p.v, p.mark("s"))
                p.v.tensor_scalar_mul(pif_row[:, :], pi_row[:, :],
                                      flags[0:1, 1:2])
                p.waitm(p.t, p.mark("v"))
                bcast_row(p, pif_row[:, :], 32, pibc[:, :])
                for j in range(2):
                    p.v.tensor_tensor(pisdo_tok[:, j * 32:(j + 1) * 32],
                                      sdo_tok[:, j * 32:(j + 1) * 32],
                                      pibc[:, :], ALU.mult)
                p.v.tensor_tensor(pi_row[:, :], pi_row[:, :], cd_row[:, :],
                                  ALU.mult)
                bcast_row(p, cd_row[:, :], 32, cd64[:, :], nrow=64)
                for j in range(2):
                    c = sc * 2 + j

                    def evg(eng, pt3, j=j):
                        eng.tensor_tensor(GT_sb[:, j * 256:(j + 1) * 256],
                                          pt3[:, 0:256],
                                          trim[:, j * 256:(j + 1) * 256],
                                          ALU.mult)

                    mm_group(p, ppool,
                             [(Bc_fm[0:64, c * 128:(c + 1) * 128],
                               C_fm[:, sc * 256:(sc + 1) * 256])],
                             evg, ncols=256)

                for h in range(32):
                    hcol = h * 128
                    ab = h % 2
                    p.wait(p.y, acs)
                    if h < 2:
                        p.wait(p.y, xps)
                    mar = p.dma(p.y, arl[ab], out=arow[ab][:, :],
                                in_=acum_dram[sc, h:h + 1, :])
                    mxl = p.dma(p.y, xll[ab], out=xload[ab][:, :],
                                in_=xpost_dram[h, :, sc * 256:(sc + 1) * 256])
                    p.waitm(p.t, mar)
                    i3 = ppool.next()
                    pt3 = ppool.tiles[i3]
                    p.waitm(p.t, ppool.freed[i3])
                    p.inc(p.t.matmul(pt3[:, 0:256], com.ones_row[:, :],
                                     arow[ab][:, :], start=True, stop=True),
                          ppool.wrote[i3])
                    p.wait(p.v, ppool.wrote[i3], ppool.wrote[i3].v)
                    p.v.tensor_copy(rb_sb[:, :], pt3[:, 0:256])
                    ppool.freed[i3] = p.mark("v")
                    p.waitm(p.t, mxl)
                    for j in range(2):
                        i2 = tpoolf.next()
                        pt2 = tpoolf.tiles[i2]
                        w2 = tpoolf.wrote[i2]
                        p.waitm(p.t, tpoolf.freed[i2])
                        p.inc(p.t.transpose(pt2[:, 0:128],
                                            xload[ab][:, j * 128:(j + 1) * 128],
                                            com.identf[:, :]), w2)
                        p.wait(p.v, w2, w2.v)
                        p.v.tensor_copy(X_bf[:, j * 128:(j + 1) * 128],
                                        pt2[:, 0:128])
                        tpoolf.freed[i2] = p.mark("v")
                    for j in range(2):
                        c = sc * 2 + j
                        p.v.tensor_scalar_mul(xs_st[:, j * 128:(j + 1) * 128],
                                              X_bf[:, j * 128:(j + 1) * 128],
                                              dt_tok[:, c * 32 + h: c * 32 + h + 1])
                        p.v.tensor_scalar_mul(xd_st[:, j * 128:(j + 1) * 128],
                                              X_bf[:, j * 128:(j + 1) * 128],
                                              D_bc[:, h:h + 1])
                    for j in range(2):
                        p.v.tensor_scalar(fA[:, 0:256], rb_sb[:, :],
                                          nacum_tok[:, j * 32 + h: j * 32 + h + 1],
                                          0.0, op0=ALU.add, op1=ALU.min)
                        p.waitm(p.s, p.mark("v"))
                        p.s.activation(out=fA[:, 256:512], in_=fA[:, 0:256],
                                       func=AF.Exp)
                        p.waitm(p.v, p.mark("s"))
                        p.v.tensor_tensor(M_bf[:, j * 256:(j + 1) * 256],
                                          fA[:, 256:512],
                                          GT_sb[:, j * 256:(j + 1) * 256],
                                          ALU.mult)
                    p.v.tensor_copy(S_bf[:, :], S_run[:, hcol:hcol + 128])
                    m_pre = p.mark("v")
                    p.waitm(p.t, m_pre)
                    for lc in range(2):
                        c = sc * 2 + lc
                        id_ = ppool.next()
                        ptd = ppool.tiles[id_]
                        wd_ = ppool.wrote[id_]
                        p.waitm(p.t, ppool.freed[id_])
                        for j in range(2):
                            p.inc(p.t.matmul(
                                ptd[:, 0:128],
                                M_bf[:, j * 256 + lc * 128: j * 256 + lc * 128 + 128],
                                xs_st[:, j * 128:(j + 1) * 128],
                                start=(j == 0), stop=(j == 1)), wd_) \
                                if j == 1 else p.t.matmul(
                                ptd[:, 0:128],
                                M_bf[:, j * 256 + lc * 128: j * 256 + lc * 128 + 128],
                                xs_st[:, j * 128:(j + 1) * 128],
                                start=(j == 0), stop=(j == 1))
                        io_ = ppool.next()
                        pto = ppool.tiles[io_]
                        wo_ = ppool.wrote[io_]
                        p.waitm(p.t, ppool.freed[io_])
                        p.inc(p.t.matmul(
                            pto[:, 0:128],
                            C_fm[:, c * 128:(c + 1) * 128],
                            S_bf[:, :], start=True, stop=True), wo_)
                        yb = lc % 2
                        p.wait(p.v, wo_, wo_.v)
                        p.wait(p.v, wd_, wd_.v)
                        p.waitm(p.v, ylast[yb])
                        p.v.scalar_tensor_tensor(
                            yst[yb][:, :], pto[:, 0:128],
                            sdo_tok[:, lc * 32 + h: lc * 32 + h + 1],
                            xd_st[:, lc * 128:(lc + 1) * 128],
                            op0=ALU.mult, op1=ALU.add)
                        p.v.tensor_tensor(yst[yb][:, :], yst[yb][:, :],
                                          ptd[:, 0:128], ALU.add)
                        ppool.freed[id_] = p.mark("v")
                        ppool.freed[io_] = ppool.freed[id_]
                        p.waitm(p.y, ppool.freed[id_])
                        ylast[yb] = p.dma(
                            p.y, ysf[yb],
                            out=y_dram[c * 128:(c + 1) * 128, hcol:hcol + 128],
                            in_=yst[yb][:, :])
                    for j in range(2):
                        c = sc * 2 + j
                        p.v.tensor_scalar_mul(
                            bd_st[:, j * 64:(j + 1) * 64],
                            B_tok[:, c * 64:(c + 1) * 64],
                            decay_tok[:, j * 32 + h: j * 32 + h + 1])
                    m_bd = p.mark("v")
                    p.waitm(p.t, m_bd)
                    iu = ppool.next()
                    ptu = ppool.tiles[iu]
                    wu_ = ppool.wrote[iu]
                    p.waitm(p.t, ppool.freed[iu])
                    for j in range(2):
                        p.inc(p.t.matmul(
                            ptu[0:64, 0:128], bd_st[:, j * 64:(j + 1) * 64],
                            xs_st[:, j * 128:(j + 1) * 128],
                            start=(j == 0), stop=(j == 1)), wu_) \
                            if j == 1 else p.t.matmul(
                            ptu[0:64, 0:128], bd_st[:, j * 64:(j + 1) * 64],
                            xs_st[:, j * 128:(j + 1) * 128],
                            start=(j == 0), stop=(j == 1))
                    p.wait(p.v, wu_, wu_.v)
                    p.v.scalar_tensor_tensor(
                        S_run[:, hcol:hcol + 128], S_run[:, hcol:hcol + 128],
                        cd64[:, h:h + 1], ptu[0:64, 0:128],
                        op0=ALU.mult, op1=ALU.add)
                    ppool.freed[iu] = p.mark("v")
                for j in range(2):
                    c = sc * 2 + j
                    p.v.tensor_copy(fB[:, c * 32:(c + 1) * 32],
                                    pisdo_tok[:, j * 32:(j + 1) * 32])

            p.v.tensor_scalar_mul(S_run[:, :], S_run[:, :], flags[0:64, 0:1])
            p.waitm(p.y, p.mark("v"))
            p.dma(p.y, stx, out=st_in, in_=S_run[:, :])
            p.wait(p.g, stx)
            p.inc_cc(p.g.collective_compute(
                "AllReduce", ALU.add, replica_groups=RG_PAIRS,
                ins=[st_in], outs=[st_out]), ccst)
            p.wait(p.y, ccst, 1)
            msl2 = p.dma(p.y, stl, out=S_run[:, :], in_=st_out)
            p.waitm(p.v, msl2)
            p.v.tensor_copy(seed_bf[:, :], S_run[:, :])
            m_seed = p.mark("v")
            p.waitm(p.t, m_seed)

            p.wait(p.y, ysf[0])
            p.wait(p.y, ysf[1])
            for c in range(TC):
                sc = c // 2
                p.waitm(p.y, p.mark("v"))
                myr = p.dma(p.y, yrl, out=yrow[:, :],
                            in_=y_dram[c * 128:(c + 1) * 128, :])
                mzr = p.dma(p.y, zrl, out=zrow[:, :],
                            in_=z_dram[c * 128:(c + 1) * 128, :])
                p.waitm(p.v, myr)
                for h in range(32):
                    ic = ppool.next()
                    ptc = ppool.tiles[ic]
                    wc_ = ppool.wrote[ic]
                    p.waitm(p.t, ppool.freed[ic])
                    p.inc(p.t.matmul(ptc[:, 0:128],
                                     C_fm[:, c * 128:(c + 1) * 128],
                                     seed_bf[:, h * 128:(h + 1) * 128],
                                     start=True, stop=True), wc_)
                    p.wait(p.v, wc_, wc_.v)
                    p.v.scalar_tensor_tensor(
                        yrow[:, h * 128:(h + 1) * 128], ptc[:, 0:128],
                        fB[:, c * 32 + h: c * 32 + h + 1],
                        yrow[:, h * 128:(h + 1) * 128],
                        op0=ALU.mult, op1=ALU.add)
                    ppool.freed[ic] = p.mark("v")
                p.waitm(p.s, mzr)
                p.waitm(p.s, p.mark("v"))
                p.s.activation(out=bfx[:, :], in_=zrow[:, :], func=AF.Silu)
                p.waitm(p.v, p.mark("s"))
                p.v.tensor_tensor(yrow[:, :], yrow[:, :], bfx[:, :], ALU.mult)
                p.waitm(p.s, p.mark("v"))
                p.s.activation(out=zrow[:, :], in_=yrow[:, :], func=AF.Square)
                p.waitm(p.v, p.mark("s"))
                p.v.reduce_sum(scr["m"][:, :], zrow[:, :], axis=AX.X)
                p.v.tensor_scalar_mul(scr["m"][:, :], scr["m"][:, :],
                                      1.0 / 4096.0)
                p.waitm(p.s, p.mark("v"))
                p.s.activation(out=scr["t1"][:, :], in_=scr["m"][:, :],
                               func=AF.Sqrt, bias=com.cc(1e-5))
                p.waitm(p.v, p.mark("s"))
                p.v.reciprocal(scr["r"][:, :], scr["t1"][:, :])
                p.v.tensor_scalar_mul(bfx[:, :], yrow[:, :], scr["r"][:, :])
                p.waitm(p.t, p.mark("v"))
                for fc in range(32):
                    transpose_tile(p, com, tpool,
                                   bfx[:, fc * 128:(fc + 1) * 128],
                                   gnT_c[:, fc * 128:(fc + 1) * 128])
                if c > 0:
                    p.wait(p.y, osf)
                mxr2 = p.dma(p.y, xrl2, out=yacc_c[:, :],
                             in_=x_d[c * 128:(c + 1) * 128, :])
                p.waitm(p.v, mxr2)
                p.waitm(p.t, p.mark("v"))
                for db in range(4):
                    ib = ppool.next()
                    ptb = ppool.tiles[ib]
                    wb_ = ppool.wrote[ib]
                    p.waitm(p.t, ppool.freed[ib])
                    for os_ in range(4):
                        slot = 0
                        wsto.load(p, slot,
                                  wout_d[db, os_].rearrange("p k n -> p (k n)"))
                        wsto.wait_loaded(p, p.t, slot)
                        for k8 in range(8):
                            fc = os_ * 8 + k8
                            lastmm = (os_ == 3 and k8 == 7)
                            inst = p.t.matmul(
                                ptb[:, :],
                                gnT_c[:, fc * 128:(fc + 1) * 128],
                                wsto.bufs[slot][:, k8 * 512:(k8 + 1) * 512],
                                start=(os_ == 0 and k8 == 0), stop=lastmm)
                            if lastmm:
                                p.inc(inst, wb_)
                            elif k8 == 7:
                                p.inc(inst, wofr)
                        if os_ < 3:
                            wsto.set_free(p, 0, (wofr, wofr.v))
                    wsto.set_free(p, 0, (wb_, wb_.v))
                    p.wait(p.v, wb_, wb_.v)
                    p.v.tensor_add(yacc_c[:, db * 512:(db + 1) * 512],
                                   yacc_c[:, db * 512:(db + 1) * 512],
                                   ptb[:, :])
                    ppool.freed[ib] = p.mark("v")
                p.waitm(p.y, p.mark("v"))
                p.dma(p.y, osf, out=xo_d[c * 128:(c + 1) * 128, :],
                      in_=yacc_c[:, :])
            p.wait(p.y, osf)

        run_walk(nc, block, sems, walk,
                 resettables=[ppool, tpool, tpoolf, wstr, wstx, wsto])
    return nc


def l0ab_consts():
    i = np.arange(128, dtype=np.float32)
    n = np.arange(256, dtype=np.float32)[None, :]
    u0 = (i[:, None] <= n).astype(np.float32)
    u1 = ((128 + i[:, None]) <= n).astype(np.float32)
    return np.ascontiguousarray(np.concatenate([u0, u1], axis=1))


def run_l0ab(x_shards, winT, woutT, W_conv, b_conv, dt_bias, A_log, Dv):
    nc = build_l0ab()
    winz_p = rhs_pack(winT[:, 0:4096].astype(BF))
    winx_l = lhsT_pack(winT[:, 4096:8320].astype(BF))
    windt_p = np.ascontiguousarray(
        winT[:, 8320:8352].astype(BF).reshape(16, 128, 32))
    wout_p = np.ascontiguousarray(
        woutT.astype(BF).reshape(4, 8, 128, 4, 512).transpose(3, 0, 2, 1, 4))
    wconv_p = np.ascontiguousarray(
        np.asarray(W_conv, np.float32)[:, 0, :].reshape(33, 128, 4)
        .transpose(1, 0, 2).reshape(128, 132))
    bconv_p = np.ascontiguousarray(
        np.asarray(b_conv, np.float32).reshape(33, 128).T)
    dtb_r = np.ascontiguousarray(np.asarray(dt_bias, np.float32).reshape(1, 32))
    alog_r = np.ascontiguousarray(np.asarray(A_log, np.float32).reshape(1, 32))
    dco_r = np.ascontiguousarray(np.asarray(Dv, np.float32).reshape(1, 32))
    ucum = l0ab_consts()
    _, _, _, trim = l1aa_consts()
    in_maps = [dict(x=x_shards[i], winz_p=winz_p, winx_l=winx_l,
                    windt_p=windt_p, wout_p=wout_p, wconv_p=wconv_p,
                    bconv_p=bconv_p, dtb_r=dtb_r, alog_r=alog_r, dco_r=dco_r,
                    ucum=ucum, trim=trim, flags=make_flags(i % 2))
               for i in range(NCORES)]
    return _run(nc, in_maps, "xo")



def _shard_x(x):
    out = []
    for b in range(4):
        for h in range(2):
            out.append(np.ascontiguousarray(x[b, h * T:(h + 1) * T, :]))
    return out


def _unshard_x(shards):
    x = np.empty((4, 2048, 2048), np.float32)
    for b in range(4):
        for h in range(2):
            x[b, h * T:(h + 1) * T, :] = shards[2 * b + h]
    return x


def _run(nc, in_maps, out_key):
    res = run_bass_kernel_spmd(nc, in_maps, core_ids=list(range(NCORES)))
    return [np.asarray(res.results[i][out_key]) for i in range(NCORES)]


def run_l0aa(x_shards, Wq, Wk, Wv, Wo, stop_after=None):
    nc = build_l0aa(stop_after=stop_after)
    wq_p = rhs_pack(Wq)
    wk_p = rhs_pack(Wk)
    wv_p = lhsT_pack(Wv)
    wo_p = rhs_pack(Wo)
    in_maps = [dict(x=x_shards[i], wq_p=wq_p, wk_p=wk_p, wv_p=wv_p, wo_p=wo_p)
               for i in range(NCORES)]
    return _run(nc, in_maps, "xo")


def kernel(**inputs):
    f32 = lambda a: np.ascontiguousarray(np.asarray(a, np.float32))
    x = f32(inputs["L0_attn_a_x"])
    shards = _shard_x(x)
    prep = run_prep(inputs)
    shards = run_l0aa(shards,
                      f32(inputs["L0_attn_a_Wq"]), f32(inputs["L0_attn_a_Wk"]),
                      f32(inputs["L0_attn_a_Wv"]), f32(inputs["L0_attn_a_Wo"]))
    shards = run_l0ffn(shards, prep["wfcT"], prep["wprT"], prep["gam"])
    shards = run_l0ab(shards, prep["winT"], prep["wouT"],
                      f32(inputs["L0_attn_b_W_conv"]),
                      f32(inputs["L0_attn_b_b_conv"]),
                      f32(inputs["L0_attn_b_dt_bias"]),
                      f32(inputs["L0_attn_b_A_log"]),
                      f32(inputs["L0_attn_b_D"]))
    shards = run_l0ffn(shards, prep["wfcT"], prep["wprT"], prep["gam"])
    shards = run_l1aa(shards, f32(inputs["L1_attn_a_slope_rate"]),
                      f32(inputs["L1_attn_a_w_qkv"]),
                      f32(inputs["L1_attn_a_w_gate"]),
                      f32(inputs["L1_attn_a_w_out"]),
                      f32(inputs["L1_attn_a_rms_weight"]))
    shards = run_l1ffn(shards, prep["wgaT"], prep["wupT"], prep["wdoT"])
    shards = run_l1ab(shards,
                      f32(inputs["L1_attn_b_Wq"]), f32(inputs["L1_attn_b_Wk"]),
                      f32(inputs["L1_attn_b_Wv"]), f32(inputs["L1_attn_b_Wo"]))
    shards = run_l1ffn(shards, prep["wgaT"], prep["wupT"], prep["wdoT"])
    return _unshard_x(shards)


# version 04 (final)
# speedup vs baseline: 1.0376x; 1.0376x over previous
import math
from contextlib import ExitStack

import numpy as np
import ml_dtypes

import concourse.bass as bass
import concourse.mybir as mybir
from concourse.bass_utils import run_bass_kernel_spmd

F32 = mybir.dt.float32
BF16 = mybir.dt.bfloat16
AF = mybir.ActivationFunctionType
ALU = mybir.AluOpType
AX = mybir.AxisListType
BF = ml_dtypes.bfloat16

NCORES = 8
T = 1024
TC = T // 128
D = 2048
DC = D // 128

RG_PAIRS = [[0, 1], [2, 3], [4, 5], [6, 7]]
RG_ALL = [[0, 1, 2, 3, 4, 5, 6, 7]]



def rhs_pack(w):
    K, N = w.shape
    assert K % 128 == 0 and N % 512 == 0
    return np.ascontiguousarray(
        w.reshape(K // 128, 128, N // 512, 512).transpose(2, 1, 0, 3))


def lhsT_pack(w):
    K, M = w.shape
    assert K % 128 == 0 and M % 128 == 0
    return np.ascontiguousarray(
        w.reshape(K // 128, 128, M // 128, 128).transpose(2, 1, 0, 3))


def col_pack(v):
    return np.ascontiguousarray(np.asarray(v, np.float32).reshape(-1, 128).T)



class _Noop:
    def then_inc(self, *a, **k):
        return self


NOOP = _Noop()


class Sem:
    __slots__ = ("h", "v")

    def __init__(self, h):
        self.h = h
        self.v = 0


_DMA_METHODS = {"dma_start", "indirect_dma_start", "dma_start_transpose"}
_NOAUTO_METHODS = {"wait_ge", "nop", "collective_compute"}


class Eng:

    __slots__ = ("real", "chain", "auto")

    def __init__(self, real, chain=None, auto=False):
        self.real = real
        self.chain = chain
        self.auto = auto

    def __getattr__(self, name):
        real = self.real
        chain = self.chain
        if chain is None or name in _NOAUTO_METHODS or name in _DMA_METHODS:
            if real is None:
                return lambda *a, **k: NOOP
            return getattr(real, name)
        auto = self.auto

        def call(*a, **k):
            prev = chain.v
            chain.v += 1
            if real is None:
                return NOOP
            if auto and prev > 0:
                real.wait_ge(chain.h, prev)
            inst = getattr(real, name)(*a, **k)
            inst.then_inc(chain.h, 1)
            return inst
        return call


class Ctx:

    def __init__(self, nc, which, eng, chains):
        self.nc = nc
        self.which = which
        self.chains = chains
        self.t = Eng(eng if which == "t" else None)
        self.v = Eng(eng if which == "v" else None, chains["v"], True)
        self.s = Eng(eng if which == "s" else None, chains["s"], True)
        self.g = Eng(eng if which == "g" else None, chains["g"], True)
        self.y = Eng(eng if which == "y" else None)

    def inc(self, inst, sem, n=1):
        sem.v += n
        inst.then_inc(sem.h, n)
        return sem.v

    def inc_cc(self, inst, sem):
        sem.v += 1
        inst.then_inc(sem.h)
        return sem.v

    def wait(self, eng, sem, v=None):
        eng.wait_ge(sem.h, sem.v if v is None else v)

    def dma(self, eng, sem, **kw):
        inst = eng.dma_start(**kw)
        sem.v += 16
        inst.then_inc(sem.h, 16)
        return (sem, sem.v)

    def mark(self, engine_char):
        c = self.chains[engine_char]
        return (c, c.v)

    def waitm(self, eng, marker):
        if marker is None:
            return
        sem, v = marker
        if v > 0:
            eng.wait_ge(sem.h, v)


ENG_ORDER = [("y", "sync"), ("g", "gpsimd"), ("t", "tensor"), ("s", "scalar"),
             ("v", "vector")]


def run_walk(nc, block, sems, walk, resettables=()):
    for which, attr in ENG_ORDER:
        for s in sems.values():
            s.v = 0
        for r in resettables:
            r.reset()

        def body(eng, which=which):
            walk(Ctx(nc, which, eng, {k: sems["ch_" + k] for k in "vsg"}))

        getattr(block, attr)(body)


def make_sems(nc, st):
    sems = {}
    for k in "vsg":
        sems["ch_" + k] = Sem(st.enter_context(nc.semaphore("ch_" + k)))
    return sems



CONST_VALS = [0.0, 1.0, -1.0, 128.0, -128.0, 1e-5, 1e-6,
              256.0, 512.0, 768.0]


class Common:

    def __init__(self, nc, st):
        self.nc = nc
        vals = list(dict.fromkeys(CONST_VALS))
        self.cmap = {v: i for i, v in enumerate(vals)}
        carr = np.zeros((128, len(vals)), np.float32)
        for v, i in self.cmap.items():
            carr[:, i] = v
        self.c_dram = nc.inline_tensor(carr, "const_cols")
        self.ident_dram = nc.inline_tensor(np.eye(128, dtype=BF), "ident_bf")
        self.identf_dram = nc.inline_tensor(np.eye(128, dtype=np.float32),
                                            "ident_f32")
        self.ccol = st.enter_context(
            nc.sbuf_tensor("const_cols_sb", [128, len(vals)], F32))
        self.ident = st.enter_context(
            nc.sbuf_tensor("ident_sb", [128, 128], BF16))
        self.identf = st.enter_context(
            nc.sbuf_tensor("identf_sb", [128, 128], F32))
        self.ones_row_dram = nc.inline_tensor(np.ones((1, 128), np.float32),
                                              "ones_row")
        self.ones_row = st.enter_context(
            nc.sbuf_tensor("ones_row_sb", [1, 128], F32))

    def cc(self, val):
        i = self.cmap[float(val)]
        return self.ccol[:, i:i + 1]

    def load(self, p, sem):
        p.dma(p.y, sem, out=self.ccol[:, :], in_=self.c_dram[:, :])
        p.dma(p.y, sem, out=self.ident[:, :], in_=self.ident_dram[:, :])
        p.dma(p.y, sem, out=self.identf[:, :], in_=self.identf_dram[:, :])
        m = p.dma(p.y, sem, out=self.ones_row[:, :], in_=self.ones_row_dram[:, :])
        return m


class PsumPool:

    def __init__(self, nc, st, sems, name, nslots, shape=(128, 512), dtype=F32):
        self.nslots = nslots
        self.tiles = [st.enter_context(
            nc.psum_tensor(f"{name}{i}", list(shape), dtype))
            for i in range(nslots)]
        self.wrote = []
        for i in range(nslots):
            s = Sem(st.enter_context(nc.semaphore(f"{name}{i}w")))
            sems[f"{name}{i}w"] = s
            self.wrote.append(s)
        self.freed = [None] * nslots
        self.idx = 0

    def reset(self):
        self.idx = 0
        self.freed = [None] * self.nslots

    def next(self):
        i = self.idx
        self.idx = (self.idx + 1) % self.nslots
        return i


def mm_group(p, pool, mats, evac, evac_eng="v", ncols=None):
    i = pool.next()
    pt = pool.tiles[i]
    w = pool.wrote[i]
    p.waitm(p.t, pool.freed[i])
    n = len(mats)
    out_ap = pt[:, :] if ncols is None else pt[:, 0:ncols]
    for j, (lt, rh) in enumerate(mats):
        inst = p.t.matmul(out_ap, lt, rh, start=(j == 0), stop=(j == n - 1))
        if j == n - 1:
            p.inc(inst, w)
    eng = p.v if evac_eng == "v" else p.s
    p.wait(eng, w, w.v)
    evac(eng, pt)
    pool.freed[i] = p.mark(evac_eng)
    return (w, w.v)


def transpose_tile(p, com, tpool, src_ap, dst_ap, evac_eng="v", fp32=False):
    i = tpool.next()
    pt = tpool.tiles[i]
    w = tpool.wrote[i]
    p.waitm(p.t, tpool.freed[i])
    ident = com.identf if fp32 else com.ident
    p.inc(p.t.transpose(pt[:, 0:128], src_ap, ident[:, :]), w)
    eng = p.v if evac_eng == "v" else p.s
    p.wait(eng, w, w.v)
    if evac_eng == "v":
        eng.tensor_copy(dst_ap, pt[:, 0:128])
    else:
        eng.copy(dst_ap, pt[:, 0:128])
    tpool.freed[i] = p.mark(evac_eng)


class WStream:

    def __init__(self, nc, st, sems, name, piece_cols, dtype, nbuf=2, queue="y"):
        self.name = name
        self.piece_cols = piece_cols
        self.bufs = [st.enter_context(
            nc.sbuf_tensor(f"{name}_b{i}", [128, piece_cols], dtype))
            for i in range(nbuf)]
        self.nbuf = nbuf
        self.queue = queue
        self.lsem = []
        for i in range(nbuf):
            s = Sem(st.enter_context(nc.semaphore(f"{name}{i}l")))
            sems[f"{name}{i}l"] = s
            self.lsem.append(s)
        self.freed = [None] * nbuf

    def reset(self):
        self.freed = [None] * self.nbuf

    def qeng(self, p):
        return {"y": p.y, "g": p.g, "s": p.s}[self.queue]

    def load(self, p, slot, src_ap, cols=None):
        eng = self.qeng(p)
        p.waitm(eng, self.freed[slot])
        dst = self.bufs[slot][:, :] if cols is None else self.bufs[slot][:, 0:cols]
        p.dma(eng, self.lsem[slot], out=dst, in_=src_ap)

    def wait_loaded(self, p, eng, slot):
        p.wait(eng, self.lsem[slot])

    def set_free(self, p, slot, marker):
        self.freed[slot] = marker



def emit_ln_chunk(p, com, xc, out_ap, scr, width=D, quant=False):
    inv_w = 1.0 / width
    eps = 1e-6 if quant else 1e-5
    m = scr["m"]; q = scr["q"]; r = scr["r"]; t1 = scr["t1"]
    sq = scr["sq"][:, 0:width]
    p.v.reduce_sum(m[:, :], xc, axis=AX.X)
    p.v.tensor_scalar_mul(m[:, :], m[:, :], inv_w)
    p.waitm(p.s, p.mark("v"))
    p.s.activation(out=sq, in_=xc, func=AF.Square)
    p.waitm(p.v, p.mark("s"))
    p.v.reduce_sum(q[:, :], sq, axis=AX.X)
    p.v.tensor_scalar_mul(q[:, :], q[:, :], inv_w)
    p.v.tensor_tensor(t1[:, :], m[:, :], m[:, :], ALU.mult)
    p.v.tensor_sub(q[:, :], q[:, :], t1[:, :])
    p.waitm(p.s, p.mark("v"))
    p.s.activation(out=t1[:, :], in_=q[:, :], func=AF.Sqrt, bias=com.cc(eps))
    p.waitm(p.v, p.mark("s"))
    p.v.reciprocal(r[:, :], t1[:, :])
    if not quant:
        p.v.tensor_scalar(out_ap, xc, m[:, :], r[:, :],
                          op0=ALU.subtract, op1=ALU.mult)
        return p.mark("v")
    p.v.tensor_scalar(sq, xc, m[:, :], r[:, :], op0=ALU.subtract, op1=ALU.mult)
    p.v.reduce_max(t1[:, :], sq, axis=AX.X, apply_absolute_value=True)
    p.v.reciprocal(m[:, :], t1[:, :])
    p.v.tensor_scalar_mul(m[:, :], m[:, :], 128.0)
    p.v.tensor_scalar_mul(t1[:, :], t1[:, :], 1.0 / 128.0)
    p.v.tensor_scalar(sq, sq, m[:, :], -128.0, op0=ALU.mult, op1=ALU.max)
    p.v.tensor_scalar(sq, sq, 128.0, t1[:, :], op0=ALU.min, op1=ALU.mult)
    p.v.tensor_copy(out_ap, sq)
    return p.mark("v")



def build_l0aa(detect_races=True, stop_after=None):
    nc = bass.Bass("TRN2", target_bir_lowering=False,
                   detect_race_conditions=detect_races)
    x_d = nc.dram_tensor("x", [T, D], F32, kind="ExternalInput").ap()
    wq_d = nc.dram_tensor("wq_p", [4, 128, 16, 512], F32, kind="ExternalInput").ap()
    wk_d = nc.dram_tensor("wk_p", [4, 128, 16, 512], F32, kind="ExternalInput").ap()
    wv_d = nc.dram_tensor("wv_p", [16, 128, 16, 128], F32, kind="ExternalInput").ap()
    wo_d = nc.dram_tensor("wo_p", [4, 128, 16, 512], F32, kind="ExternalInput").ap()
    xo_d = nc.dram_tensor("xo", [T, D], F32, kind="ExternalOutput").ap()

    s_in = nc.dram_tensor("s_in", [2048, 2048], F32, kind="Internal").ap()
    s_out = nc.dram_tensor("s_out", [2048, 2048], F32, kind="Internal").ap()

    with ExitStack() as st:
        sems = make_sems(nc, st)

        def sem(name):
            s = Sem(st.enter_context(nc.semaphore(name)))
            sems[name] = s
            return s

        com = Common(nc, st)
        sb = lambda n, sh, dt: st.enter_context(nc.sbuf_tensor(n, sh, dt))

        x_fm = sb("x_fm", [128, DC * T], BF16)
        qk_sb = sb("qk_sb", [128, TC * D * 2], BF16)
        v_fm = sb("v_fm", [128, DC * T], BF16)
        fA = sb("fA", [128, 2048], F32)
        fB = sb("fB", [128, 2048], F32)
        xn_scr = sb("xn_scr", [128, 2048], BF16)
        outst = [sb(f"outst{i}", [128, 512], F32) for i in range(2)]
        xres = [sb(f"xres{i}", [128, 512], F32) for i in range(2)]
        scr = dict(m=sb("scr_m", [128, 1], F32), q=sb("scr_q", [128, 1], F32),
                   r=sb("scr_r", [128, 1], F32), t1=sb("scr_t1", [128, 1], F32),
                   sq=fB)
        wt_sb = qk_sb
        o_fm = x_fm

        ccs = [sem(f"cc{j}") for j in range(4)]
        out_done = sem("out_done")
        ld = sem("ld"); xld = sem("xld"); sst = sem("sst"); sld = sem("sld")
        osf = [sem("osf0"), sem("osf1")]
        xrl = [sem("xrl0"), sem("xrl1")]

        ppool = PsumPool(nc, st, sems, "pp", 4)
        tpool = PsumPool(nc, st, sems, "tp", 2, shape=(128, 1024), dtype=BF16)

        wstr = WStream(nc, st, sems, "ws", 16 * 512, BF16, nbuf=2, queue="g")
        wstr_l = WStream(nc, st, sems, "wl", 16 * 128, BF16, nbuf=2, queue="g")

        block = st.enter_context(nc.Block())

        def walk(p):
            mld = com.load(p, ld)
            p.waitm(p.t, mld)
            p.waitm(p.v, mld)
            p.waitm(p.s, mld)

            for c in range(TC):
                if c > 0:
                    p.waitm(p.y, p.mark("v"))
                    p.waitm(p.y, p.mark("s"))
                mx = p.dma(p.y, xld, out=fA[:, :],
                           in_=x_d[c * 128:(c + 1) * 128, :])
                p.waitm(p.v, mx)
                p.waitm(p.s, mx)
                mk = emit_ln_chunk(p, com, fA[:, :], xn_scr[:, :], scr,
                                   width=D, quant=False)
                p.waitm(p.t, mk)
                for fc in range(DC):
                    transpose_tile(
                        p, com, tpool,
                        xn_scr[:, fc * 128:(fc + 1) * 128],
                        x_fm[:, fc * T + c * 128: fc * T + (c + 1) * 128])
            m_xfm = p.mark("v")
            if stop_after == 1:
                p.waitm(p.y, m_xfm)
                p.dma(p.y, osf[0], out=xo_d[0:128, :],
                      in_=fA[:, :])
                p.wait(p.y, osf[0])
                return

            p.waitm(p.t, m_xfm)
            for which, w_d in (("q", wq_d), ("k", wk_d)):
                off = 0 if which == "q" else TC * D
                for nb in range(4):
                    slot = nb % 2
                    wstr.load(p, slot, w_d[nb].rearrange("p k n -> p (k n)"))
                    wstr.wait_loaded(p, p.t, slot)
                    last = None
                    for c in range(TC):
                        mats = [(x_fm[:, k * T + c * 128: k * T + (c + 1) * 128],
                                 wstr.bufs[slot][:, k * 512:(k + 1) * 512])
                                for k in range(DC)]
                        dst = qk_sb[:, off + c * D + nb * 512:
                                    off + c * D + (nb + 1) * 512]
                        scale = 0.125 if which == "q" else None

                        def evac(eng, pt, dst=dst, scale=scale):
                            if scale is None:
                                eng.tensor_copy(dst, pt[:, :])
                            else:
                                eng.tensor_scalar_mul(dst, pt[:, :], scale)

                        last = mm_group(p, ppool, mats, evac)
                    wstr.set_free(p, slot, last)
            m_qk = p.mark("v")
            if stop_after == 2:
                p.waitm(p.y, m_qk)
                p.dma(p.y, osf[0], out=xo_d[0:128, :], in_=fA[:, :])
                p.wait(p.y, osf[0])
                return

            p.waitm(p.t, m_qk)
            stage = fA
            for et in range(16):
                for fb in range(4):
                    mats = [(qk_sb[:, c * D + et * 128: c * D + (et + 1) * 128],
                             qk_sb[:, TC * D + c * D + fb * 512:
                                   TC * D + c * D + (fb + 1) * 512])
                            for c in range(TC)]
                    dst = stage[:, fb * 512:(fb + 1) * 512]

                    def evac(eng, pt, dst=dst):
                        eng.tensor_copy(dst, pt[:, :])

                    mm_group(p, ppool, mats, evac)
                p.waitm(p.y, p.mark("v"))
                mst = p.dma(p.y, sst, out=s_in[et * 128:(et + 1) * 128, :],
                            in_=stage[:, :])
                if et < 15:
                    p.waitm(p.v, mst)

            if stop_after == 3:
                p.wait(p.y, sst)
                p.dma(p.y, osf[0], out=xo_d[0:128, :], in_=s_in[0:128, :])
                p.wait(p.y, osf[0])
                return

            p.wait(p.g, sst)
            for j in range(4):
                p.inc_cc(p.g.collective_compute(
                    "AllReduce", ALU.add, replica_groups=RG_PAIRS,
                    ins=[s_in[j * 512:(j + 1) * 512, :]],
                    outs=[s_out[j * 512:(j + 1) * 512, :]]), ccs[j])

            if stop_after == 4:
                for j in range(4):
                    p.wait(p.y, ccs[j], 1)
                p.dma(p.y, osf[0], out=xo_d[0:128, :], in_=s_out[0:128, :])
                p.wait(p.y, osf[0])
                return

            for mb in range(16):
                slot = mb % 2
                wstr_l.load(p, slot, wv_d[mb].rearrange("p k n -> p (k n)"))
                wstr_l.wait_loaded(p, p.t, slot)
                last = None
                for sbk in range(2):
                    mats = [(wstr_l.bufs[slot][:, k * 128:(k + 1) * 128],
                             x_fm[:, k * T + sbk * 512: k * T + sbk * 512 + 512])
                            for k in range(DC)]
                    dst = v_fm[:, mb * T + sbk * 512: mb * T + sbk * 512 + 512]

                    def evac(eng, pt, dst=dst):
                        eng.tensor_copy(dst, pt[:, :])

                    last = mm_group(p, ppool, mats, evac)
                wstr_l.set_free(p, slot, last)
            m_vfm = p.mark("v")
            if stop_after == 5:
                p.waitm(p.y, m_vfm)
                p.dma(p.y, osf[0], out=xo_d[0:128, :], in_=fA[:, :])
                p.wait(p.y, osf[0])
                return

            for et in range(16):
                j = et // 4
                p.wait(p.y, ccs[j], 1)
                p.waitm(p.y, p.mark("v"))
                msl = p.dma(p.y, sld, out=stage[:, :],
                            in_=s_out[et * 128:(et + 1) * 128, :])
                p.waitm(p.v, msl)
                p.v.reduce_max(scr["m"][:, :], stage[:, :], axis=AX.X,
                               negate=True)
                p.waitm(p.s, p.mark("v"))
                p.s.activation(out=stage[:, :], in_=stage[:, :], func=AF.Exp,
                               bias=scr["m"][:, :])
                p.waitm(p.v, p.mark("s"))
                p.v.reduce_sum(scr["q"][:, :], stage[:, :], axis=AX.X)
                p.v.reciprocal(scr["r"][:, :], scr["q"][:, :])
                wrow = xn_scr[:, :]
                p.v.tensor_scalar_mul(wrow, stage[:, :], scr["r"][:, :])
                p.waitm(p.t, p.mark("v"))
                for fc in range(16):
                    transpose_tile(
                        p, com, tpool,
                        wrow[:, fc * 128:(fc + 1) * 128],
                        wt_sb[:, fc * D + et * 128: fc * D + (et + 1) * 128],
                        evac_eng="s")

            if stop_after == 6:
                p.waitm(p.y, p.mark("s"))
                p.dma(p.y, osf[0], out=xo_d[0:128, :], in_=fA[:, :])
                p.wait(p.y, osf[0])
                return

            p.waitm(p.t, p.mark("s"))
            p.waitm(p.t, m_vfm)
            for et in range(16):
                for sbk in range(2):
                    mats = [(wt_sb[:, fc * D + et * 128: fc * D + (et + 1) * 128],
                             v_fm[:, fc * T + sbk * 512: fc * T + sbk * 512 + 512])
                            for fc in range(16)]
                    dst = o_fm[:, et * T + sbk * 512: et * T + sbk * 512 + 512]

                    def evac(eng, pt, dst=dst):
                        eng.tensor_copy(dst, pt[:, :])

                    mm_group(p, ppool, mats, evac)
            m_ofm = p.mark("v")
            if stop_after == 7:
                p.waitm(p.y, m_ofm)
                p.dma(p.y, osf[0], out=xo_d[0:128, :], in_=fA[:, :])
                p.wait(p.y, osf[0])
                return

            p.waitm(p.t, m_ofm)
            free_markers = [None, None]
            xres_free = [None, None]
            for nb in range(4):
                slot = 0
                wstr.load(p, slot, wo_d[nb].rearrange("p k n -> p (k n)"))
                wstr.wait_loaded(p, p.t, slot)
                last = None
                for c in range(TC):
                    mats = [(o_fm[:, k * T + c * 128: k * T + (c + 1) * 128],
                             wstr.bufs[slot][:, k * 512:(k + 1) * 512])
                            for k in range(DC)]
                    ob = (nb * TC + c) % 2
                    p.waitm(p.y, xres_free[ob])
                    mxr = p.dma(
                        p.y, xrl[ob], out=xres[ob][:, :],
                        in_=x_d[c * 128:(c + 1) * 128, nb * 512:(nb + 1) * 512])

                    def evac(eng, pt, ob=ob, mxr=mxr):
                        p.waitm(eng, free_markers[ob])
                        p.waitm(eng, mxr)
                        eng.tensor_add(outst[ob][:, :], xres[ob][:, :], pt[:, :])

                    last = mm_group(p, ppool, mats, evac)
                    xres_free[ob] = p.mark("v")
                    p.waitm(p.y, xres_free[ob])
                    free_markers[ob] = p.dma(
                        p.y, osf[ob],
                        out=xo_d[c * 128:(c + 1) * 128, nb * 512:(nb + 1) * 512],
                        in_=outst[ob][:, :])
                wstr.set_free(p, slot, last)
            p.wait(p.y, osf[0])
            p.wait(p.y, osf[1])

        run_walk(nc, block, sems, walk, resettables=[ppool, tpool, wstr, wstr_l])
    return nc





PREP_JOBS = [
    ("wfc", 8, 2048, "tern"),
    ("wpr", 2, 8192, "tern"),
    ("win", 9, 2048, "cast"),
    ("wou", 2, 4096, "castscale"),
    ("wga", 8, 2048, "cast"),
    ("wup", 8, 2048, "cast"),
    ("wdo", 2, 8192, "cast"),
]


def build_prep(detect_races=True):
    nc = bass.Bass("TRN2", target_bir_lowering=False,
                   detect_race_conditions=detect_races)
    ins = {}
    outs = {}
    for name, nt, K, kind in PREP_JOBS:
        ins[name] = nc.dram_tensor(name, [nt * 128, K], F32,
                                   kind="ExternalInput").ap()
        outs[name] = nc.dram_tensor(name + "T", [K // 128, 128, nt * 128],
                                    BF16, kind="ExternalOutput").ap()
    normw_d = nc.dram_tensor("normw_p", [128, 32], F32,
                             kind="ExternalInput").ap()
    gam_d = nc.dram_tensor("gam", [1, 2], F32, kind="ExternalOutput").ap()
    cc_in = nc.dram_tensor("cc_in", [1, 512], F32, kind="Internal").ap()
    cc_out = nc.dram_tensor("cc_out", [1, 512], F32, kind="Internal").ap()

    NFC = 8192.0 * 2048.0
    NPR = 2048.0 * 8192.0

    with ExitStack() as st:
        sems = make_sems(nc, st)

        def sem(name):
            s = Sem(st.enter_context(nc.semaphore(name)))
            sems[name] = s
            return s

        com = Common(nc, st)
        sb = lambda n, sh, dt: st.enter_context(nc.sbuf_tensor(n, sh, dt))

        sliceA = sb("sliceA", [128, 9 * 2048], F32)
        sliceB = sb("sliceB", [128, 2 * 8192], F32)
        tbf = sb("tbf", [128, 9 * 2048], BF16)
        sgn = sb("sgn", [128, 4096], F32)
        outrow = sb("outrow", [128, 9 * 128], BF16)
        normw = sb("normw", [128, 32], F32)
        acc = sb("acc", [128, 1], F32)
        tmp1 = sb("tmp1", [128, 1], F32)
        gcols = sb("gcols", [128, 2], F32)
        grow = sb("grow", [1, 512], F32)

        ld = sem("ld"); wld = sem("wld"); od = sem("od")
        ccm = sem("ccm"); gld = sem("gld"); cc = sem("cc_g")
        nwl = sem("nwl")
        ppool = PsumPool(nc, st, sems, "pp", 2, shape=(128, 512))
        tpool = PsumPool(nc, st, sems, "tp", 2, shape=(128, 1024), dtype=BF16)

        block = st.enter_context(nc.Block())

        def abs_sum(p, slice_sb, nt, K, col_out):
            for t_ in range(nt):
                tile = slice_sb[:, t_ * K:(t_ + 1) * K]
                p.v.reduce_sum(tmp1[:, :], tile, axis=AX.X,
                               apply_absolute_value=True)
                if t_ == 0:
                    p.v.tensor_copy(acc[:, :], tmp1[:, :])
                else:
                    p.v.tensor_add(acc[:, :], acc[:, :], tmp1[:, :])
            p.waitm(p.t, p.mark("v"))
            i = ppool.next()
            pt = ppool.tiles[i]
            p.waitm(p.t, ppool.freed[i])
            p.inc(p.t.matmul(pt[0:1, 0:1], acc[:, :], com.cc(1.0),
                             start=True, stop=True), ppool.wrote[i])
            p.wait(p.v, ppool.wrote[i], ppool.wrote[i].v)
            p.v.tensor_copy(col_out, pt[0:1, 0:1])
            ppool.freed[i] = p.mark("v")

        def transpose_out(p, src_bf, nt, K, out_d, scale_col_of=None):
            for kc in range(K // 128):
                for rt in range(nt):
                    src = src_bf[:, rt * K + kc * 128: rt * K + kc * 128 + 128]
                    dst = outrow[:, rt * 128:(rt + 1) * 128]
                    i = tpool.next()
                    pt = tpool.tiles[i]
                    w = tpool.wrote[i]
                    p.waitm(p.t, tpool.freed[i])
                    p.inc(p.t.transpose(pt[:, 0:128], src, com.ident[:, :]), w)
                    p.wait(p.v, w, w.v)
                    if scale_col_of is None:
                        p.v.tensor_copy(dst, pt[:, 0:128])
                    else:
                        p.v.tensor_scalar_mul(dst, pt[:, 0:128],
                                              scale_col_of(kc))
                    tpool.freed[i] = p.mark("v")
                p.waitm(p.y, p.mark("v"))
                mo = p.dma(p.y, od, out=out_d[kc, :, 0:nt * 128],
                           in_=outrow[:, 0:nt * 128])
                p.waitm(p.v, mo)

        def load_slice(p, dst_sb, name, nt, K):
            m = p.dma(p.y, wld,
                      out=dst_sb[:, 0:nt * K].rearrange("p (t k) -> p t k", t=nt),
                      in_=ins[name].rearrange("(t p) k -> p t k", p=128))
            return m

        def ternarize(p, slice_sb, nt, K, gi):
            W = min(K, 4096)
            for t_ in range(nt):
                for j in range(K // W):
                    tile = slice_sb[:, t_ * K + j * W: t_ * K + (j + 1) * W]
                    sg = sgn[:, 0:W]
                    p.waitm(p.s, p.mark("v"))
                    p.s.activation(out=sg, in_=tile, func=AF.Sign)
                    p.s.activation(out=tile, in_=tile, func=AF.Abs)
                    p.waitm(p.v, p.mark("s"))
                    p.v.tensor_scalar(tile, tile, gcols[:, gi:gi + 1], 0.5,
                                      op0=ALU.mult, op1=ALU.is_gt)
                    p.v.tensor_tensor(
                        tbf[:, t_ * K + j * W: t_ * K + (j + 1) * W],
                        sg, tile, ALU.mult)

        def walk(p):
            mld = com.load(p, ld)
            p.waitm(p.v, mld)
            p.waitm(p.s, mld)
            p.waitm(p.t, mld)
            mnw = p.dma(p.y, nwl, out=normw[:, :], in_=normw_d)

            m1 = load_slice(p, sliceA, "wfc", 8, 2048)
            m2 = load_slice(p, sliceB, "wpr", 2, 8192)
            p.waitm(p.v, m2)
            abs_sum(p, sliceA, 8, 2048, gcols[0:1, 0:1])
            abs_sum(p, sliceB, 2, 8192, gcols[0:1, 1:2])
            p.v.memset(grow[:, :], 0.0)
            p.v.tensor_copy(grow[0:1, 0:2], gcols[0:1, 0:2])
            p.waitm(p.y, p.mark("v"))
            p.dma(p.y, ccm, out=cc_in, in_=grow[:, :])
            p.wait(p.g, ccm)
            p.inc_cc(p.g.collective_compute(
                "AllReduce", ALU.add, replica_groups=RG_ALL,
                ins=[cc_in], outs=[cc_out]), cc)
            p.wait(p.y, cc, 1)
            mg = p.dma(p.y, gld, out=grow[:, :], in_=cc_out)
            p.waitm(p.v, mg)
            p.v.tensor_scalar_mul(grow[0:1, 0:1], grow[0:1, 0:1], 1.0 / NFC)
            p.v.tensor_scalar_mul(grow[0:1, 1:2], grow[0:1, 1:2], 1.0 / NPR)
            p.waitm(p.y, p.mark("v"))
            mgam = p.dma(p.y, od, out=gam_d, in_=grow[0:1, 0:2])
            p.waitm(p.v, mgam)
            p.v.tensor_scalar_add(grow[0:1, 0:2], grow[0:1, 0:2], 1e-6)
            p.v.reciprocal(grow[0:1, 0:2], grow[0:1, 0:2])
            p.waitm(p.t, p.mark("v"))
            i = ppool.next()
            pt = ppool.tiles[i]
            p.waitm(p.t, ppool.freed[i])
            p.inc(p.t.matmul(pt[:, 0:2], com.ones_row[:, :], grow[0:1, 0:2],
                             start=True, stop=True), ppool.wrote[i])
            p.wait(p.v, ppool.wrote[i], ppool.wrote[i].v)
            p.v.tensor_copy(gcols[:, :], pt[:, 0:2])
            ppool.freed[i] = p.mark("v")

            ternarize(p, sliceA, 8, 2048, 0)
            p.waitm(p.t, p.mark("v"))
            transpose_out(p, tbf, 8, 2048, outs["wfc"])
            ternarize(p, sliceB, 2, 8192, 1)
            p.waitm(p.t, p.mark("v"))
            transpose_out(p, tbf, 2, 8192, outs["wpr"])

            def cast_job(name, nt, K, scale_col_of=None):
                dst_sb = sliceA if nt * K > 16384 else sliceB
                m = load_slice(p, dst_sb, name, nt, K)
                p.waitm(p.v, m)
                for t_ in range(nt):
                    p.v.tensor_copy(tbf[:, t_ * K:(t_ + 1) * K],
                                    dst_sb[:, t_ * K:(t_ + 1) * K])
                p.waitm(p.t, p.mark("v"))
                transpose_out(p, tbf, nt, K, outs[name], scale_col_of)

            cast_job("win", 9, 2048)
            p.waitm(p.v, mnw)
            cast_job("wou", 2, 4096,
                     scale_col_of=lambda kc: normw[:, kc:kc + 1])
            cast_job("wga", 8, 2048)
            cast_job("wup", 8, 2048)
            cast_job("wdo", 2, 8192)
            p.wait(p.y, od)

        run_walk(nc, block, sems, walk, resettables=[ppool, tpool])
    return nc


def run_prep(inputs):
    nc = build_prep()
    w_fc = np.ascontiguousarray(np.asarray(inputs["L0_ffn_w_fc"], np.float32))
    w_pr = np.ascontiguousarray(np.asarray(inputs["L0_ffn_w_proj"], np.float32))
    w_in = np.asarray(inputs["L0_attn_b_W_in"], np.float32)
    w_in_pad = np.zeros((9216, 2048), np.float32)
    w_in_pad[:8352] = w_in
    w_ou = np.ascontiguousarray(np.asarray(inputs["L0_attn_b_W_out"], np.float32))
    w_ga = np.ascontiguousarray(np.asarray(inputs["L1_ffn_w_gate"], np.float32))
    w_up = np.ascontiguousarray(np.asarray(inputs["L1_ffn_w_up"], np.float32))
    w_do = np.ascontiguousarray(np.asarray(inputs["L1_ffn_w_down"], np.float32))
    full = dict(wfc=w_fc, wpr=w_pr, win=w_in_pad, wou=w_ou, wga=w_ga,
                wup=w_up, wdo=w_do)
    normw_p = col_pack(inputs["L0_attn_b_norm_w"])
    in_maps = []
    for c in range(NCORES):
        m = {"normw_p": normw_p}
        for name, nt, K, kind in PREP_JOBS:
            m[name] = np.ascontiguousarray(
                full[name][c * nt * 128:(c + 1) * nt * 128, :])
        in_maps.append(m)
    res = run_bass_kernel_spmd(nc, in_maps, core_ids=list(range(NCORES)))
    out = {}
    for name, nt, K, kind in PREP_JOBS:
        parts = [np.asarray(res.results[c][name + "T"]).reshape(K, nt * 128)
                 for c in range(NCORES)]
        out[name + "T"] = np.concatenate(parts, axis=1)
    out["winT"] = out["winT"][:, :8352]
    gam = np.asarray(res.results[0]["gam"])
    out["g_fc"] = float(gam[0, 0])
    out["g_pr"] = float(gam[0, 1])
    out["gam"] = gam
    return out



def tproj_pack(tT):
    return np.ascontiguousarray(
        tT.reshape(4, 16, 128, 4, 512).transpose(3, 0, 2, 1, 4))


def build_l0ffn(detect_races=True):
    nc = bass.Bass("TRN2", target_bir_lowering=False,
                   detect_race_conditions=detect_races)
    x_d = nc.dram_tensor("x", [T, D], F32, kind="ExternalInput").ap()
    tfc_d = nc.dram_tensor("tfc_p", [16, 128, 16, 512], BF16,
                           kind="ExternalInput").ap()
    tpr_d = nc.dram_tensor("tpr_p", [4, 4, 128, 16, 512], BF16,
                           kind="ExternalInput").ap()
    gam_d = nc.dram_tensor("gam", [1, 2], F32, kind="ExternalInput").ap()
    xo_d = nc.dram_tensor("xo", [T, D], F32, kind="ExternalOutput").ap()
    hg_d = nc.dram_tensor("hg", [T, 8192], BF16, kind="Internal").ap()

    with ExitStack() as st:
        sems = make_sems(nc, st)

        def sem(name):
            s = Sem(st.enter_context(nc.semaphore(name)))
            sems[name] = s
            return s

        com = Common(nc, st)
        sb = lambda n, sh, dt: st.enter_context(nc.sbuf_tensor(n, sh, dt))

        yacc = sb("yacc", [128, TC * D], F32)
        xh_fm = sb("xh_fm", [128, DC * T], BF16)
        fA = sb("fA", [128, 2048], F32)
        fB = sb("fB", [128, 8192], F32)
        xn_scr = sb("xn_scr", [128, 2048], BF16)
        hgrow = sb("hgrow", [128, 8192], BF16)
        hst = [sb(f"hst{i}", [128, 512], BF16) for i in range(2)]
        scr = dict(m=sb("scr_m", [128, 1], F32), q=sb("scr_q", [128, 1], F32),
                   r=sb("scr_r", [128, 1], F32), t1=sb("scr_t1", [128, 1], F32),
                   sq=fB)
        sumac = sb("sumac", [128, TC], F32)
        sqac = sb("sqac", [128, TC], F32)
        m2c = sb("m2c", [128, TC], F32)
        c1c = sb("c1c", [128, TC], F32)
        c2c = sb("c2c", [128, TC], F32)
        gcol = sb("gcol", [128, 2], F32)
        grow = sb("grow", [1, 2], F32)
        xh2_fm = xh_fm

        ld = sem("ld"); xld = sem("xld"); gld = sem("gld")
        hst_d = [sem("hst_d0"), sem("hst_d1")]
        hld = sem("hld"); osf = sem("osf")

        ppool = PsumPool(nc, st, sems, "pp", 4)
        tpool = PsumPool(nc, st, sems, "tp", 2, shape=(128, 1024), dtype=BF16)
        wstr = WStream(nc, st, sems, "ws", 16 * 512, BF16, nbuf=2, queue="g")

        block = st.enter_context(nc.Block())

        def walk(p):
            mld = com.load(p, ld)
            p.waitm(p.v, mld)
            p.waitm(p.s, mld)
            p.waitm(p.t, mld)
            mgam = p.dma(p.y, gld, out=grow[:, :], in_=gam_d)
            p.waitm(p.t, mgam)
            i = ppool.next()
            pt = ppool.tiles[i]
            p.inc(p.t.matmul(pt[:, 0:2], com.ones_row[:, :], grow[:, :],
                             start=True, stop=True), ppool.wrote[i])
            p.wait(p.v, ppool.wrote[i], ppool.wrote[i].v)
            p.v.tensor_copy(gcol[:, :], pt[:, 0:2])
            ppool.freed[i] = p.mark("v")

            for c in range(TC):
                if c > 0:
                    p.waitm(p.y, p.mark("v"))
                    p.waitm(p.y, p.mark("s"))
                mx = p.dma(p.y, xld, out=fA[:, :],
                           in_=x_d[c * 128:(c + 1) * 128, :])
                p.waitm(p.v, mx)
                p.waitm(p.s, mx)
                p.v.tensor_copy(yacc[:, c * D:(c + 1) * D], fA[:, :])
                mk = emit_ln_chunk(p, com, fA[:, :], xn_scr[:, :], scr,
                                   width=D, quant=True)
                p.waitm(p.t, mk)
                for fc in range(DC):
                    transpose_tile(
                        p, com, tpool,
                        xn_scr[:, fc * 128:(fc + 1) * 128],
                        xh_fm[:, fc * T + c * 128: fc * T + (c + 1) * 128])
            m_xh = p.mark("v")

            p.waitm(p.t, m_xh)
            p.waitm(p.v, p.mark("s"))
            p.v.memset(sumac[:, :], 0.0)
            p.v.memset(sqac[:, :], 0.0)
            hfree = [None, None]
            for ob in range(16):
                slot = ob % 2
                wstr.load(p, slot, tfc_d[ob].rearrange("p k n -> p (k n)"))
                wstr.wait_loaded(p, p.t, slot)
                last = None
                for c in range(TC):
                    mats = [(xh_fm[:, k * T + c * 128: k * T + (c + 1) * 128],
                             wstr.bufs[slot][:, k * 512:(k + 1) * 512])
                            for k in range(DC)]
                    hb = (ob * TC + c) % 2

                    def evac(eng, pt, hb=hb, c=c):
                        p.waitm(eng, hfree[hb])
                        eng.activation(out=hst[hb][:, :], in_=pt[:, :],
                                       func=AF.Gelu, scale=gcol[:, 0:1])

                    last = mm_group(p, ppool, mats, evac, evac_eng="s")
                    mh = p.mark("s")
                    p.waitm(p.v, mh)
                    p.v.reduce_sum(scr["m"][:, :], hst[hb][:, :], axis=AX.X)
                    p.v.tensor_add(sumac[:, c:c + 1], sumac[:, c:c + 1],
                                   scr["m"][:, :])
                    p.waitm(p.s, p.mark("v"))
                    p.s.activation(out=fA[:, 0:512], in_=hst[hb][:, :],
                                   func=AF.Square)
                    p.waitm(p.v, p.mark("s"))
                    p.v.reduce_sum(scr["q"][:, :], fA[:, 0:512], axis=AX.X)
                    p.v.tensor_add(sqac[:, c:c + 1], sqac[:, c:c + 1],
                                   scr["q"][:, :])
                    p.waitm(p.y, mh)
                    hfree[hb] = p.dma(
                        p.y, hst_d[hb],
                        out=hg_d[c * 128:(c + 1) * 128, ob * 512:(ob + 1) * 512],
                        in_=hst[hb][:, :])
                wstr.set_free(p, slot, last)

            p.waitm(p.v, p.mark("s"))
            p.v.tensor_scalar_mul(m2c[:, :], sumac[:, :], 1.0 / 8192.0)
            p.v.tensor_scalar_mul(sqac[:, :], sqac[:, :], 1.0 / 8192.0)
            p.v.tensor_tensor(c1c[:, :], m2c[:, :], m2c[:, :], ALU.mult)
            p.v.tensor_sub(sqac[:, :], sqac[:, :], c1c[:, :])
            p.waitm(p.s, p.mark("v"))
            p.s.activation(out=sqac[:, :], in_=sqac[:, :], func=AF.Sqrt,
                           bias=com.cc(1e-6))
            p.waitm(p.v, p.mark("s"))
            p.v.reciprocal(sqac[:, :], sqac[:, :])
            p.wait(p.y, hst_d[0])
            p.wait(p.y, hst_d[1])
            for c in range(TC):
                if c > 0:
                    p.waitm(p.y, p.mark("v"))
                mh = p.dma(p.y, hld, out=hgrow[:, :],
                           in_=hg_d[c * 128:(c + 1) * 128, :])
                p.waitm(p.v, mh)
                p.v.tensor_scalar(fB[:, :], hgrow[:, :], m2c[:, c:c + 1], None,
                                  op0=ALU.subtract)
                p.v.reduce_max(scr["m"][:, :], fB[:, :], axis=AX.X,
                               apply_absolute_value=True)
                p.v.reciprocal(scr["q"][:, :], scr["m"][:, :])
                p.v.tensor_scalar_mul(c1c[:, c:c + 1], scr["q"][:, :], 128.0)
                p.v.tensor_tensor(scr["t1"][:, :], scr["m"][:, :],
                                  sqac[:, c:c + 1], ALU.mult)
                p.v.tensor_scalar_mul(c2c[:, c:c + 1], scr["t1"][:, :],
                                      1.0 / 128.0)

            for os_ in range(4):
                for c in range(TC):
                    p.waitm(p.y, p.mark("v"))
                    mh = p.dma(p.y, hld, out=hgrow[:, 0:2048],
                               in_=hg_d[c * 128:(c + 1) * 128,
                                        os_ * 2048:(os_ + 1) * 2048])
                    p.waitm(p.v, mh)
                    p.v.tensor_scalar(fB[:, 0:2048], hgrow[:, 0:2048],
                                      m2c[:, c:c + 1], c1c[:, c:c + 1],
                                      op0=ALU.subtract, op1=ALU.mult)
                    p.v.tensor_scalar(fB[:, 0:2048], fB[:, 0:2048],
                                      -128.0, 128.0, op0=ALU.max, op1=ALU.min)
                    p.v.tensor_scalar_mul(xn_scr[:, :], fB[:, 0:2048],
                                          c2c[:, c:c + 1])
                    p.waitm(p.t, p.mark("v"))
                    for oc in range(16):
                        transpose_tile(
                            p, com, tpool,
                            xn_scr[:, oc * 128:(oc + 1) * 128],
                            xh2_fm[:, oc * T + c * 128: oc * T + (c + 1) * 128])
                m_x2 = p.mark("v")
                p.waitm(p.t, m_x2)
                for db in range(4):
                    slot = db % 2
                    wstr.load(p, slot, tpr_d[db, os_].rearrange("p k n -> p (k n)"))
                    wstr.wait_loaded(p, p.t, slot)
                    last = None
                    for c in range(TC):
                        mats = [(xh2_fm[:, oc * T + c * 128: oc * T + (c + 1) * 128],
                                 wstr.bufs[slot][:, oc * 512:(oc + 1) * 512])
                                for oc in range(16)]
                        dst = yacc[:, c * D + db * 512: c * D + (db + 1) * 512]

                        def evac(eng, pt, dst=dst):
                            eng.scalar_tensor_tensor(
                                dst, pt[:, :], gcol[:, 1:2], dst,
                                op0=ALU.mult, op1=ALU.add)

                        last = mm_group(p, ppool, mats, evac)
                    wstr.set_free(p, slot, last)

            p.waitm(p.y, p.mark("v"))
            xo_r = xo_d.rearrange("(c p) d -> p c d", p=128)
            p.dma(p.y, osf, out=xo_r,
                  in_=yacc[:, :].rearrange("p (c d) -> p c d", c=TC))
            p.wait(p.y, osf)

        run_walk(nc, block, sems, walk, resettables=[ppool, tpool, wstr])
    return nc


def run_l0ffn(x_shards, tfcT, tprT, gam):
    nc = build_l0ffn()
    tfc_p = rhs_pack(tfcT.astype(BF))
    tpr_p = tproj_pack(tprT.astype(BF))
    in_maps = [dict(x=x_shards[i], tfc_p=tfc_p, tpr_p=tpr_p, gam=gam)
               for i in range(NCORES)]
    return _run(nc, in_maps, "xo")





def build_l1ffn(detect_races=True):
    nc = bass.Bass("TRN2", target_bir_lowering=False,
                   detect_race_conditions=detect_races)
    x_d = nc.dram_tensor("x", [T, D], F32, kind="ExternalInput").ap()
    wga_d = nc.dram_tensor("wga_p", [16, 128, 16, 512], BF16,
                           kind="ExternalInput").ap()
    wup_d = nc.dram_tensor("wup_p", [16, 128, 16, 512], BF16,
                           kind="ExternalInput").ap()
    wdo_d = nc.dram_tensor("wdo_p", [4, 4, 128, 16, 512], BF16,
                           kind="ExternalInput").ap()
    xo_d = nc.dram_tensor("xo", [T, D], F32, kind="ExternalOutput").ap()

    with ExitStack() as st:
        sems = make_sems(nc, st)

        def sem(name):
            s = Sem(st.enter_context(nc.semaphore(name)))
            sems[name] = s
            return s

        com = Common(nc, st)
        sb = lambda n, sh, dt: st.enter_context(nc.sbuf_tensor(n, sh, dt))

        yacc = sb("yacc", [128, TC * D], F32)
        xn_fm = sb("xn_fm", [128, DC * T], BF16)
        gatedT = sb("gatedT", [128, 16 * T], BF16)
        fA = sb("fA", [128, 2048], F32)
        fB = sb("fB", [128, 2048], F32)
        xn_scr = sb("xn_scr", [128, 2048], BF16)
        sil = sb("sil", [128, 512], BF16)
        gated_st = sb("gated_st", [128, 512], BF16)
        scr = dict(m=sb("scr_m", [128, 1], F32), q=sb("scr_q", [128, 1], F32),
                   r=sb("scr_r", [128, 1], F32), t1=sb("scr_t1", [128, 1], F32),
                   sq=fB)

        ld = sem("ld"); xld = sem("xld"); osf = sem("osf")

        ppool = PsumPool(nc, st, sems, "pp", 4)
        tpool = PsumPool(nc, st, sems, "tp", 2, shape=(128, 1024), dtype=BF16)
        wsg = WStream(nc, st, sems, "wsg", 16 * 512, BF16, nbuf=1, queue="g")
        wsu = WStream(nc, st, sems, "wsu", 16 * 512, BF16, nbuf=1, queue="g")
        wsd = WStream(nc, st, sems, "wsd", 16 * 512, BF16, nbuf=1, queue="y")

        block = st.enter_context(nc.Block())

        def walk(p):
            mld = com.load(p, ld)
            p.waitm(p.v, mld)
            p.waitm(p.s, mld)
            p.waitm(p.t, mld)

            for c in range(TC):
                if c > 0:
                    p.waitm(p.y, p.mark("v"))
                    p.waitm(p.y, p.mark("s"))
                mx = p.dma(p.y, xld, out=fA[:, :],
                           in_=x_d[c * 128:(c + 1) * 128, :])
                p.waitm(p.v, mx)
                p.waitm(p.s, mx)
                p.v.tensor_copy(yacc[:, c * D:(c + 1) * D], fA[:, :])
                mk = emit_ln_chunk(p, com, fA[:, :], xn_scr[:, :], scr,
                                   width=D, quant=False)
                p.waitm(p.t, mk)
                for fc in range(DC):
                    transpose_tile(
                        p, com, tpool,
                        xn_scr[:, fc * 128:(fc + 1) * 128],
                        xn_fm[:, fc * T + c * 128: fc * T + (c + 1) * 128])
            p.waitm(p.t, p.mark("v"))

            for hs in range(4):
                for hb in range(4):
                    ob = hs * 4 + hb
                    wsg.load(p, 0, wga_d[ob].rearrange("p k n -> p (k n)"))
                    wsu.load(p, 0, wup_d[ob].rearrange("p k n -> p (k n)"))
                    wsg.wait_loaded(p, p.t, 0)
                    wsu.wait_loaded(p, p.t, 0)
                    lg = lu = None
                    for c in range(TC):
                        mats_g = [(xn_fm[:, k * T + c * 128: k * T + (c + 1) * 128],
                                   wsg.bufs[0][:, k * 512:(k + 1) * 512])
                                  for k in range(DC)]
                        mats_u = [(xn_fm[:, k * T + c * 128: k * T + (c + 1) * 128],
                                   wsu.bufs[0][:, k * 512:(k + 1) * 512])
                                  for k in range(DC)]

                        def evac_g(eng, pt):
                            p.waitm(eng, p.mark("v"))
                            eng.activation(out=sil[:, :], in_=pt[:, :],
                                           func=AF.Silu)

                        lg = mm_group(p, ppool, mats_g, evac_g, evac_eng="s")

                        def evac_u(eng, pt):
                            p.waitm(eng, p.mark("s"))
                            eng.tensor_tensor(gated_st[:, :], sil[:, :],
                                              pt[:, :], ALU.mult)

                        lu = mm_group(p, ppool, mats_u, evac_u)
                        p.waitm(p.t, p.mark("v"))
                        for j in range(4):
                            hc = hb * 4 + j
                            transpose_tile(
                                p, com, tpool,
                                gated_st[:, j * 128:(j + 1) * 128],
                                gatedT[:, hc * T + c * 128: hc * T + (c + 1) * 128])
                    wsg.set_free(p, 0, lg)
                    wsu.set_free(p, 0, lu)
                p.waitm(p.t, p.mark("v"))
                for db in range(4):
                    wsd.load(p, 0, wdo_d[db, hs].rearrange("p k n -> p (k n)"))
                    wsd.wait_loaded(p, p.t, 0)
                    last = None
                    for c in range(TC):
                        mats = [(gatedT[:, oc * T + c * 128: oc * T + (c + 1) * 128],
                                 wsd.bufs[0][:, oc * 512:(oc + 1) * 512])
                                for oc in range(16)]
                        dst = yacc[:, c * D + db * 512: c * D + (db + 1) * 512]

                        def evac(eng, pt, dst=dst):
                            eng.tensor_add(dst, dst, pt[:, :])

                        last = mm_group(p, ppool, mats, evac)
                    wsd.set_free(p, 0, last)

            p.waitm(p.y, p.mark("v"))
            xo_r = xo_d.rearrange("(c p) d -> p c d", p=128)
            p.dma(p.y, osf, out=xo_r,
                  in_=yacc[:, :].rearrange("p (c d) -> p c d", c=TC))
            p.wait(p.y, osf)

        run_walk(nc, block, sems, walk, resettables=[ppool, tpool, wsg, wsu, wsd])
    return nc


def run_l1ffn(x_shards, wgaT, wupT, wdoT):
    nc = build_l1ffn()
    wga_p = rhs_pack(wgaT.astype(BF))
    wup_p = rhs_pack(wupT.astype(BF))
    wdo_p = tproj_pack(wdoT.astype(BF))
    in_maps = [dict(x=x_shards[i], wga_p=wga_p, wup_p=wup_p, wdo_p=wdo_p)
               for i in range(NCORES)]
    return _run(nc, in_maps, "xo")





def build_l1ab(detect_races=True):
    nc = bass.Bass("TRN2", target_bir_lowering=False,
                   detect_race_conditions=detect_races)
    x_d = nc.dram_tensor("x", [T, D], F32, kind="ExternalInput").ap()
    wq_d = nc.dram_tensor("wq_l", [16, 128, 16, 128], F32, kind="ExternalInput").ap()
    wk_d = nc.dram_tensor("wk_l", [16, 128, 16, 128], F32, kind="ExternalInput").ap()
    wv_d = nc.dram_tensor("wv_p", [4, 128, 16, 512], F32, kind="ExternalInput").ap()
    wo_d = nc.dram_tensor("wo_p", [4, 128, 16, 512], F32, kind="ExternalInput").ap()
    mask_d = nc.dram_tensor("mask", [T, 2048], BF16, kind="ExternalInput").ap()
    flags_d = nc.dram_tensor("flags", [128, 2], F32, kind="ExternalInput").ap()
    xo_d = nc.dram_tensor("xo", [T, D], F32, kind="ExternalOutput").ap()

    ck_in = nc.dram_tensor("ck_in", [16, 128, 2048], BF16, kind="Internal").ap()
    ck_out = nc.dram_tensor("ck_out", [16, 128, 2048], BF16, kind="Internal").ap()
    cv_in = nc.dram_tensor("cv_in", [2, 2048, 1024], BF16, kind="Internal").ap()
    cv_out = nc.dram_tensor("cv_out", [2, 2048, 1024], BF16, kind="Internal").ap()

    SCALE = 1.0 / math.sqrt(128.0)

    with ExitStack() as st:
        sems = make_sems(nc, st)

        def sem(name):
            s = Sem(st.enter_context(nc.semaphore(name)))
            sems[name] = s
            return s

        com = Common(nc, st)
        sb = lambda n, sh, dt: st.enter_context(nc.sbuf_tensor(n, sh, dt))

        xn_fm = sb("xn_fm", [128, DC * T], BF16)
        q_fm = sb("q_fm", [128, 16 * T], BF16)
        kg = sb("kg", [128, 4 * 2048], BF16)
        vg = sb("vg", [128, 16 * 512], BF16)
        o_tok = sb("o_tok", [128, TC * D], BF16)
        P_sb = sb("P_sb", [128, 2048], BF16)
        PT_sb = sb("PT_sb", [128, 16 * 128], BF16)
        fA = sb("fA", [128, 2048], F32)
        fB = sb("fB", [128, 2048], F32)
        xn_scr = sb("xn_scr", [128, 2048], BF16)
        S_sb = fA
        mask_sb = xn_scr
        st2 = [sb(f"st2{i}", [128, 512], BF16) for i in range(2)]
        outst = [sb(f"outst{i}", [128, 512], F32) for i in range(2)]
        xres = [sb(f"xres{i}", [128, 512], F32) for i in range(2)]
        flags = sb("flags_sb", [128, 2], F32)
        scr = dict(m=sb("scr_m", [128, 1], F32), q=sb("scr_q", [128, 1], F32),
                   r=sb("scr_r", [128, 1], F32), t1=sb("scr_t1", [128, 1], F32),
                   sq=fB)
        o_fm = xn_fm

        ld = sem("ld"); xld = sem("xld"); mld_s = sem("mld"); fld = sem("fld")
        kst = [sem("kst0"), sem("kst1")]
        kld = sem("kld"); vld = sem("vld")
        osf = [sem("osf0"), sem("osf1")]
        xrl = [sem("xrl0"), sem("xrl1")]
        ccs = [sem(f"ccab{j}") for j in range(6)]

        ppool = PsumPool(nc, st, sems, "pp", 2)
        tpool = PsumPool(nc, st, sems, "tp", 2, shape=(128, 1024), dtype=BF16)
        spool = PsumPool(nc, st, sems, "sp", 1, shape=(128, 2048))
        wstr = WStream(nc, st, sems, "ws", 16 * 512, BF16, nbuf=1, queue="g")
        wstr_l = WStream(nc, st, sems, "wl", 16 * 128, BF16, nbuf=2, queue="g")

        block = st.enter_context(nc.Block())

        def walk(p):
            mld = com.load(p, ld)
            p.waitm(p.v, mld)
            p.waitm(p.s, mld)
            p.waitm(p.t, mld)
            mf = p.dma(p.y, fld, out=flags[:, :], in_=flags_d)
            p.waitm(p.v, mf)

            for c in range(TC):
                if c > 0:
                    p.waitm(p.y, p.mark("v"))
                    p.waitm(p.y, p.mark("s"))
                mx = p.dma(p.y, xld, out=fA[:, :],
                           in_=x_d[c * 128:(c + 1) * 128, :])
                p.waitm(p.v, mx)
                p.waitm(p.s, mx)
                mk = emit_ln_chunk(p, com, fA[:, :], xn_scr[:, :], scr,
                                   width=D, quant=False)
                p.waitm(p.t, mk)
                for fc in range(DC):
                    transpose_tile(
                        p, com, tpool,
                        xn_scr[:, fc * 128:(fc + 1) * 128],
                        xn_fm[:, fc * T + c * 128: fc * T + (c + 1) * 128])
            p.waitm(p.t, p.mark("v"))

            for which, w_d in (("q", wq_d), ("k", wk_d)):
                for h in range(16):
                    slot = h % 2
                    wstr_l.load(p, slot, w_d[h].rearrange("p k n -> p (k n)"))
                    wstr_l.wait_loaded(p, p.t, slot)
                    last = None
                    for sbk in range(2):
                        mats = [(wstr_l.bufs[slot][:, k * 128:(k + 1) * 128],
                                 xn_fm[:, k * T + sbk * 512: k * T + sbk * 512 + 512])
                                for k in range(DC)]
                        if which == "q":
                            dst = q_fm[:, h * T + sbk * 512: h * T + sbk * 512 + 512]

                            def evac(eng, pt, dst=dst):
                                eng.tensor_scalar_mul(dst, pt[:, :], SCALE)

                            last = mm_group(p, ppool, mats, evac)
                        else:
                            def evac(eng, pt, sbk=sbk):
                                p.waitm(eng, (kst[0], kst[0].v))
                                p.waitm(eng, (kst[1], kst[1].v))
                                eng.tensor_scalar_mul(st2[0][:, :], pt[:, :],
                                                      flags[:, 0:1])
                                eng.tensor_scalar_mul(st2[1][:, :], pt[:, :],
                                                      flags[:, 1:2])

                            last = mm_group(p, ppool, mats, evac)
                            p.waitm(p.y, p.mark("v"))
                            p.dma(p.y, kst[0],
                                  out=ck_in[h, :, sbk * 512: sbk * 512 + 512],
                                  in_=st2[0][:, :])
                            p.dma(p.y, kst[1],
                                  out=ck_in[h, :, 1024 + sbk * 512: 1024 + sbk * 512 + 512],
                                  in_=st2[1][:, :])
                    wstr_l.set_free(p, slot, last)

            for nb in range(4):
                slot = 0
                wstr.load(p, slot, wv_d[nb].rearrange("p k n -> p (k n)"))
                wstr.wait_loaded(p, p.t, slot)
                last = None
                for c in range(TC):
                    mats = [(xn_fm[:, k * T + c * 128: k * T + (c + 1) * 128],
                             wstr.bufs[slot][:, k * 512:(k + 1) * 512])
                            for k in range(DC)]

                    def evac(eng, pt):
                        p.waitm(eng, (kst[0], kst[0].v))
                        p.waitm(eng, (kst[1], kst[1].v))
                        eng.tensor_scalar_mul(st2[0][:, :], pt[:, :],
                                              flags[:, 0:1])
                        eng.tensor_scalar_mul(st2[1][:, :], pt[:, :],
                                              flags[:, 1:2])

                    last = mm_group(p, ppool, mats, evac)
                    p.waitm(p.y, p.mark("v"))
                    eh = nb // 2
                    ec = (nb % 2) * 512
                    p.dma(p.y, kst[0],
                          out=cv_in[eh, c * 128:(c + 1) * 128, ec:ec + 512],
                          in_=st2[0][:, :])
                    p.dma(p.y, kst[1],
                          out=cv_in[eh, 1024 + c * 128: 1024 + (c + 1) * 128,
                                    ec:ec + 512],
                          in_=st2[1][:, :])
                wstr.set_free(p, slot, last)

            p.wait(p.g, kst[0])
            p.wait(p.g, kst[1])
            for j in range(4):
                p.inc_cc(p.g.collective_compute(
                    "AllReduce", ALU.add, replica_groups=RG_PAIRS,
                    ins=[ck_in[j * 4:(j + 1) * 4].rearrange("h p s -> (h p) s")],
                    outs=[ck_out[j * 4:(j + 1) * 4].rearrange("h p s -> (h p) s")]),
                    ccs[j])
            for j in range(2):
                p.inc_cc(p.g.collective_compute(
                    "AllReduce", ALU.add, replica_groups=RG_PAIRS,
                    ins=[cv_in[j]], outs=[cv_out[j]]), ccs[4 + j])

            for grp in range(4):
                p.wait(p.y, ccs[grp], 1)
                p.wait(p.y, ccs[4 + grp // 2], 1)
                if grp > 0:
                    for w_ in spool.wrote + ppool.wrote:
                        if w_.v > 0:
                            p.wait(p.y, w_, w_.v)
                mkld = p.dma(p.y, kld,
                    out=kg[:, :].rearrange("p (h s) -> p h s", h=4),
                    in_=ck_out[grp * 4:(grp + 1) * 4].rearrange(
                        "h p s -> p h s"))
                mvld = p.dma(p.y, vld,
                    out=vg[:, :].rearrange("p (kc e) -> p kc e", kc=16),
                    in_=cv_out[grp // 2][:, (grp % 2) * 512:
                                         (grp % 2) * 512 + 512].rearrange(
                        "(kc p) e -> p kc e", p=128))
                p.waitm(p.t, mkld)
                p.waitm(p.t, mvld)
                for c in range(TC):
                    p.waitm(p.y, p.mark("v"))
                    mm = p.dma(p.y, mld_s, out=mask_sb[:, :],
                               in_=mask_d[c * 128:(c + 1) * 128, :])
                    p.waitm(p.v, mm)
                    for h8 in range(4):
                        h = grp * 4 + h8
                        i = spool.next()
                        pt = spool.tiles[i]
                        w = spool.wrote[i]
                        p.waitm(p.t, spool.freed[i])
                        for kb in range(4):
                            p.inc(p.t.matmul(
                                pt[:, kb * 512:(kb + 1) * 512],
                                q_fm[:, h * T + c * 128: h * T + (c + 1) * 128],
                                kg[:, h8 * 2048 + kb * 512: h8 * 2048 + (kb + 1) * 512],
                                start=True, stop=True),
                                w) if kb == 3 else p.t.matmul(
                                pt[:, kb * 512:(kb + 1) * 512],
                                q_fm[:, h * T + c * 128: h * T + (c + 1) * 128],
                                kg[:, h8 * 2048 + kb * 512: h8 * 2048 + (kb + 1) * 512],
                                start=True, stop=True)
                        p.wait(p.v, w, w.v)
                        p.v.tensor_tensor(S_sb[:, :], pt[:, :], mask_sb[:, :],
                                          ALU.add)
                        spool.freed[i] = p.mark("v")
                        p.v.reduce_max(scr["m"][:, :], S_sb[:, :], axis=AX.X,
                                       negate=True)
                        p.waitm(p.s, p.mark("v"))
                        p.s.activation(out=P_sb[:, :], in_=S_sb[:, :],
                                       func=AF.Exp, bias=scr["m"][:, :])
                        p.waitm(p.v, p.mark("s"))
                        p.v.reduce_sum(scr["q"][:, :], P_sb[:, :], axis=AX.X)
                        p.v.reciprocal(scr["r"][:, :], scr["q"][:, :])
                        mkp = p.mark("v")
                        p.waitm(p.t, mkp)
                        for kc in range(16):
                            transpose_tile(
                                p, com, tpool,
                                P_sb[:, kc * 128:(kc + 1) * 128],
                                PT_sb[:, kc * 128:(kc + 1) * 128])
                        p.waitm(p.t, p.mark("v"))
                        mats = [(PT_sb[:, kc * 128:(kc + 1) * 128],
                                 vg[:, kc * 512 + h8 * 128: kc * 512 + (h8 + 1) * 128])
                                for kc in range(16)]
                        dst = o_tok[:, c * D + h * 128: c * D + (h + 1) * 128]

                        def evac(eng, pt2, dst=dst):
                            eng.tensor_scalar_mul(dst, pt2[:, 0:128],
                                                  scr["r"][:, :])

                        mm_group(p, ppool, mats, evac, ncols=128)

            m_o = p.mark("v")
            p.waitm(p.t, m_o)
            for c in range(TC):
                for fc in range(DC):
                    transpose_tile(
                        p, com, tpool,
                        o_tok[:, c * D + fc * 128: c * D + (fc + 1) * 128],
                        o_fm[:, fc * T + c * 128: fc * T + (c + 1) * 128])
            p.waitm(p.t, p.mark("v"))
            free_markers = [None, None]
            xres_free = [None, None]
            for nb in range(4):
                slot = 0
                wstr.load(p, slot, wo_d[nb].rearrange("p k n -> p (k n)"))
                wstr.wait_loaded(p, p.t, slot)
                last = None
                for c in range(TC):
                    mats = [(o_fm[:, k * T + c * 128: k * T + (c + 1) * 128],
                             wstr.bufs[slot][:, k * 512:(k + 1) * 512])
                            for k in range(DC)]
                    ob = (nb * TC + c) % 2
                    p.waitm(p.y, xres_free[ob])
                    mxr = p.dma(
                        p.y, xrl[ob], out=xres[ob][:, :],
                        in_=x_d[c * 128:(c + 1) * 128, nb * 512:(nb + 1) * 512])

                    def evac(eng, pt, ob=ob, mxr=mxr):
                        p.waitm(eng, free_markers[ob])
                        p.waitm(eng, mxr)
                        eng.tensor_add(outst[ob][:, :], xres[ob][:, :], pt[:, :])

                    last = mm_group(p, ppool, mats, evac)
                    xres_free[ob] = p.mark("v")
                    p.waitm(p.y, xres_free[ob])
                    free_markers[ob] = p.dma(
                        p.y, osf[ob],
                        out=xo_d[c * 128:(c + 1) * 128, nb * 512:(nb + 1) * 512],
                        in_=outst[ob][:, :])
                wstr.set_free(p, slot, last)
            p.wait(p.y, osf[0])
            p.wait(p.y, osf[1])

        run_walk(nc, block, sems, walk,
                 resettables=[ppool, tpool, spool, wstr, wstr_l])
    return nc


def make_l1ab_mask(parity):
    q = np.arange(T)[:, None] + parity * T
    k = np.arange(2048)[None, :]
    return np.where(k <= q, np.float32(0.0), np.float32(-1e9)).astype(BF)


def make_flags(parity):
    f = np.zeros((128, 2), np.float32)
    f[:, 0] = 1.0 if parity == 0 else 0.0
    f[:, 1] = 0.0 if parity == 0 else 1.0
    return f


def run_l1ab(x_shards, Wq, Wk, Wv, Wo):
    nc = build_l1ab()
    wq_l = lhsT_pack(Wq)
    wk_l = lhsT_pack(Wk)
    wv_p = rhs_pack(Wv)
    wo_p = rhs_pack(Wo)
    masks = [make_l1ab_mask(i % 2) for i in range(2)]
    flags = [make_flags(i % 2) for i in range(2)]
    in_maps = [dict(x=x_shards[i], wq_l=wq_l, wk_l=wk_l, wv_p=wv_p, wo_p=wo_p,
                    mask=masks[i % 2], flags=flags[i % 2])
               for i in range(NCORES)]
    return _run(nc, in_maps, "xo")





def build_l1aa(detect_races=True):
    nc = bass.Bass("TRN2", target_bir_lowering=False,
                   detect_race_conditions=detect_races)
    x_d = nc.dram_tensor("x", [T, D], F32, kind="ExternalInput").ap()
    wqkv_d = nc.dram_tensor("wqkv_p", [12, 128, 16, 512], F32,
                            kind="ExternalInput").ap()
    wg_d = nc.dram_tensor("wg_p", [4, 128, 16, 512], F32,
                          kind="ExternalInput").ap()
    wo_d = nc.dram_tensor("wo_p", [4, 128, 16, 512], F32,
                          kind="ExternalInput").ap()
    slope_d = nc.dram_tensor("slope_r", [1, 16], F32, kind="ExternalInput").ap()
    rms_d = nc.dram_tensor("rms_p", [128, 16], F32, kind="ExternalInput").ap()
    flags_d = nc.dram_tensor("flags", [128, 2], F32, kind="ExternalInput").ap()
    iotaq_d = nc.dram_tensor("iotaq", [128, 2], F32, kind="ExternalInput").ap()
    iotak_d = nc.dram_tensor("iotak", [128, 2], F32, kind="ExternalInput").ap()
    lmm_d = nc.dram_tensor("lmm", [128, 512], BF16, kind="ExternalInput").ap()
    trim_d = nc.dram_tensor("trim", [128, 512], BF16, kind="ExternalInput").ap()
    xo_d = nc.dram_tensor("xo", [T, D], F32, kind="ExternalOutput").ap()

    kv_in = nc.dram_tensor("kv_in", [128, 2048], F32, kind="Internal").ap()
    kv_out = nc.dram_tensor("kv_out", [128, 2048], F32, kind="Internal").ap()

    with ExitStack() as st:
        sems = make_sems(nc, st)

        def sem(name):
            s = Sem(st.enter_context(nc.semaphore(name)))
            sems[name] = s
            return s

        com = Common(nc, st)
        sb = lambda n, sh, dt: st.enter_context(nc.sbuf_tensor(n, sh, dt))

        xn_fm = sb("xn_fm", [128, DC * T], BF16)
        qkv_h = sb("qkv_h", [128, 3 * 8 * 1024], BF16)
        o_tok = sb("o_tok", [128, TC * D], BF16)
        qs_fm = sb("qs_fm", [128, 16 * T], BF16)
        kv_run = sb("kv_run", [128, 2048], F32)
        fA = sb("fA", [128, 2048], F32)
        fB = sb("fB", [128, 2048], F32)
        xn_scr = sb("xn_scr", [128, 2048], BF16)
        kvs_bf = xn_scr
        qf = sb("qf", [128, 256], BF16)
        kf = sb("kf", [128, 256], BF16)
        qs_st = sb("qs_st", [128, 256], BF16)
        ks_st = sb("ks_st", [128, 256], BF16)
        kvb = sb("kvb", [128, 128], BF16)
        M_sb = sb("M_sb", [128, 512], BF16)
        ddTh = sb("ddTh", [128, 512], BF16)
        sil = sb("sil", [128, 512], BF16)
        outst = [sb(f"outst{i}", [128, 512], F32) for i in range(2)]
        xres = [sb(f"xres{i}", [128, 512], F32) for i in range(2)]
        flags = sb("flags_sb", [128, 2], F32)
        srow = sb("srow", [1, 16], F32)
        negS = sb("negS", [128, 16], F32)
        qd_all = sb("qd_all", [128, 32], F32)
        kd_all = sb("kd_all", [128, 32], F32)
        bd_all = sb("bd_all", [128, 16], F32)
        bdjs = sb("bdjs", [128, 64], F32)
        rstdc = sb("rstdc", [128, TC], F32)
        iotaq = sb("iotaq_sb", [128, 2], F32)
        iotak = sb("iotak_sb", [128, 2], F32)
        lmm = sb("lmm_sb", [128, 512], BF16)
        trim = sb("trim_sb", [128, 512], BF16)
        rmsw = sb("rmsw_sb", [128, 16], F32)
        scr = dict(m=sb("scr_m", [128, 1], F32), q=sb("scr_q", [128, 1], F32),
                   r=sb("scr_r", [128, 1], F32), t1=sb("scr_t1", [128, 1], F32),
                   sq=fB)
        og_tok = qs_fm
        og_fm = qkv_h

        ld = sem("ld"); xld = sem("xld"); cld = sem("cld")
        kvst = sem("kvst"); kvld = sem("kvld")
        osf = [sem("osf0"), sem("osf1")]
        xrl = [sem("xrl0"), sem("xrl1")]
        cckv = sem("cckv")

        ppool = PsumPool(nc, st, sems, "pp", 4)
        tpool = PsumPool(nc, st, sems, "tp", 2, shape=(128, 1024), dtype=BF16)
        wstr = WStream(nc, st, sems, "ws", 16 * 512, BF16, nbuf=1, queue="g")

        block = st.enter_context(nc.Block())

        def walk(p):
            mld = com.load(p, ld)
            p.waitm(p.v, mld)
            p.waitm(p.s, mld)
            p.waitm(p.t, mld)
            p.dma(p.y, cld, out=flags[:, :], in_=flags_d)
            p.dma(p.y, cld, out=srow[:, :], in_=slope_d)
            p.dma(p.y, cld, out=iotaq[:, :], in_=iotaq_d)
            p.dma(p.y, cld, out=iotak[:, :], in_=iotak_d)
            p.dma(p.y, cld, out=lmm[:, :], in_=lmm_d)
            p.dma(p.y, cld, out=trim[:, :], in_=trim_d)
            mc = p.dma(p.y, cld, out=rmsw[:, :], in_=rms_d)
            p.waitm(p.v, mc)
            p.waitm(p.s, mc)
            p.waitm(p.t, mc)

            i = ppool.next()
            pt = ppool.tiles[i]
            p.inc(p.t.matmul(pt[:, 0:16], com.ones_row[:, :], srow[:, :],
                             start=True, stop=True), ppool.wrote[i])
            p.wait(p.v, ppool.wrote[i], ppool.wrote[i].v)
            p.v.tensor_scalar_mul(negS[:, :], pt[:, 0:16], -1.0)
            ppool.freed[i] = p.mark("v")
            p.waitm(p.s, p.mark("v"))
            for h in range(16):
                nsc = negS[:, h:h + 1]
                for j in range(2):
                    p.s.activation(out=qd_all[:, h * 2 + j: h * 2 + j + 1],
                                   in_=iotaq[:, j:j + 1], func=AF.Exp, scale=nsc)
                    p.s.activation(out=kd_all[:, h * 2 + j: h * 2 + j + 1],
                                   in_=iotak[:, j:j + 1], func=AF.Exp, scale=nsc)
                p.s.activation(out=bd_all[:, h:h + 1], in_=com.cc(256.0),
                               func=AF.Exp, scale=nsc)
            for j in range(4):
                p.s.activation(out=bdjs[:, j * 16:(j + 1) * 16], in_=negS[:, :],
                               func=AF.Exp, scale=com.cc(256.0 * j))
            p.waitm(p.v, p.mark("s"))
            for j in range(4):
                p.v.tensor_scalar_mul(bdjs[:, j * 16:(j + 1) * 16],
                                      bdjs[:, j * 16:(j + 1) * 16],
                                      flags[:, 1:2])
            p.v.memset(kv_run[:, :], 0.0)

            for c in range(TC):
                if c > 0:
                    p.waitm(p.y, p.mark("v"))
                    p.waitm(p.y, p.mark("s"))
                mx = p.dma(p.y, xld, out=fA[:, :],
                           in_=x_d[c * 128:(c + 1) * 128, :])
                p.waitm(p.v, mx)
                p.waitm(p.s, mx)
                mk = emit_ln_chunk(p, com, fA[:, :], xn_scr[:, :], scr,
                                   width=D, quant=False)
                p.waitm(p.t, mk)
                for fc in range(DC):
                    transpose_tile(
                        p, com, tpool,
                        xn_scr[:, fc * 128:(fc + 1) * 128],
                        xn_fm[:, fc * T + c * 128: fc * T + (c + 1) * 128])
            p.waitm(p.t, p.mark("v"))

            for half in range(2):
                for nb in range(12):
                    wstr.load(p, 0, wqkv_d[nb].rearrange("p k n -> p (k n)"))
                    wstr.wait_loaded(p, p.t, 0)
                    last = None
                    which = nb // 4
                    col = (nb % 4) * 512
                    for cl in range(4):
                        c = half * 4 + cl
                        mats = [(xn_fm[:, k * T + c * 128: k * T + (c + 1) * 128],
                                 wstr.bufs[0][:, k * 512:(k + 1) * 512])
                                for k in range(DC)]
                        dst = qkv_h[:, which * 8192 + cl * D + col:
                                    which * 8192 + cl * D + col + 512]

                        def evac(eng, pt, dst=dst):
                            eng.activation(out=dst, in_=pt[:, :], func=AF.Silu)

                        last = mm_group(p, ppool, mats, evac, evac_eng="s")
                    wstr.set_free(p, 0, last)
                m_qkv = p.mark("s")
                p.waitm(p.v, m_qkv)
                p.waitm(p.t, m_qkv)

                for h in range(16):
                    hcol = h * 128
                    nsc = negS[:, h:h + 1]
                    for mcx in range(2):
                        base = lmm[:, mcx * 256:(mcx + 1) * 256]
                        p.waitm(p.v, p.mark("s"))
                        p.v.tensor_scalar(fA[:, 0:256], base, nsc, 0.0,
                                          op0=ALU.mult, op1=ALU.min)
                        p.waitm(p.s, p.mark("v"))
                        p.s.activation(out=fA[:, 256:512], in_=fA[:, 0:256],
                                       func=AF.Exp)
                        p.waitm(p.v, p.mark("s"))
                        p.v.tensor_tensor(
                            ddTh[:, mcx * 256:(mcx + 1) * 256],
                            fA[:, 256:512], trim[:, mcx * 256:(mcx + 1) * 256],
                            ALU.mult)
                    for blh in range(2):
                        bl = half * 2 + blh
                        for j in range(2):
                            cl = blh * 2 + j
                            c = half * 4 + cl
                            src_q = qkv_h[:, cl * D + hcol: cl * D + hcol + 128]
                            src_k = qkv_h[:, 8192 + cl * D + hcol:
                                          8192 + cl * D + hcol + 128]
                            p.v.tensor_scalar_mul(
                                qs_st[:, j * 128:(j + 1) * 128], src_q,
                                qd_all[:, h * 2 + j: h * 2 + j + 1])
                            p.v.tensor_scalar_mul(
                                ks_st[:, j * 128:(j + 1) * 128], src_k,
                                kd_all[:, h * 2 + j: h * 2 + j + 1])
                            mkv = p.mark("v")
                            p.waitm(p.t, mkv)
                            transpose_tile(p, com, tpool, src_q,
                                           qf[:, j * 128:(j + 1) * 128])
                            transpose_tile(p, com, tpool, src_k,
                                           kf[:, j * 128:(j + 1) * 128])
                            transpose_tile(
                                p, com, tpool,
                                qs_st[:, j * 128:(j + 1) * 128],
                                qs_fm[:, h * T + c * 128: c * 128 + h * T + 128])
                        p.v.tensor_copy(kvb[:, :], kv_run[:, hcol:hcol + 128])
                        m_pre = p.mark("v")
                        p.waitm(p.t, m_pre)
                        for mcx in range(2):
                            def evacm(eng, pt, mcx=mcx):
                                eng.tensor_tensor(
                                    M_sb[:, mcx * 256:(mcx + 1) * 256],
                                    pt[:, 0:256],
                                    ddTh[:, mcx * 256:(mcx + 1) * 256],
                                    ALU.mult)

                            mm_group(p, ppool,
                                     [(kf[:, mcx * 128:(mcx + 1) * 128],
                                       qf[:, 0:256])],
                                     evacm, ncols=256)
                        m_M = p.mark("v")
                        p.waitm(p.t, m_M)
                        for lc in range(2):
                            c = bl * 2 + lc
                            cl = blh * 2 + lc
                            mats = [(M_sb[:, mcx * 256 + lc * 128:
                                          mcx * 256 + lc * 128 + 128],
                                     qkv_h[:, 16384 + (blh * 2 + mcx) * D + hcol:
                                           16384 + (blh * 2 + mcx) * D + hcol + 128])
                                    for mcx in range(2)]
                            mats.append((qs_fm[:, h * T + c * 128:
                                               h * T + c * 128 + 128],
                                         kvb[:, :]))
                            dst = o_tok[:, c * D + hcol: c * D + hcol + 128]

                            def evaco(eng, pt, dst=dst):
                                eng.tensor_copy(dst, pt[:, 0:128])

                            mm_group(p, ppool, mats, evaco, ncols=128)
                        mats = [(ks_st[:, mcx * 128:(mcx + 1) * 128],
                                 qkv_h[:, 16384 + (blh * 2 + mcx) * D + hcol:
                                       16384 + (blh * 2 + mcx) * D + hcol + 128])
                                for mcx in range(2)]

                        def evacu(eng, pt, hcol=hcol, h=h):
                            eng.scalar_tensor_tensor(
                                kv_run[:, hcol:hcol + 128],
                                kv_run[:, hcol:hcol + 128],
                                bd_all[:, h:h + 1], pt[:, 0:128],
                                op0=ALU.mult, op1=ALU.add)

                        mm_group(p, ppool, mats, evacu, ncols=128)

            p.v.tensor_scalar_mul(fB[:, :], kv_run[:, :], flags[:, 0:1])
            p.waitm(p.y, p.mark("v"))
            p.dma(p.y, kvst, out=kv_in, in_=fB[:, :])
            p.wait(p.g, kvst)
            p.inc_cc(p.g.collective_compute(
                "AllReduce", ALU.add, replica_groups=RG_PAIRS,
                ins=[kv_in], outs=[kv_out]), cckv)
            p.wait(p.y, cckv, 1)
            mkv2 = p.dma(p.y, kvld, out=fB[:, :], in_=kv_out)
            p.waitm(p.v, mkv2)
            p.v.tensor_copy(kvs_bf[:, :], fB[:, :])
            m_seed = p.mark("v")
            p.waitm(p.t, m_seed)

            for bl in range(4):
                for h in range(16):
                    hcol = h * 128
                    for lc in range(2):
                        c = bl * 2 + lc
                        dst = o_tok[:, c * D + hcol: c * D + hcol + 128]

                        def evacc(eng, pt, dst=dst, bl=bl, h=h):
                            eng.scalar_tensor_tensor(
                                dst, pt[:, 0:128],
                                bdjs[:, bl * 16 + h: bl * 16 + h + 1], dst,
                                op0=ALU.mult, op1=ALU.add)

                        mm_group(p, ppool,
                                 [(qs_fm[:, h * T + c * 128: h * T + c * 128 + 128],
                                   kvs_bf[:, hcol:hcol + 128])],
                                 evacc, ncols=128)

            for c in range(TC):
                p.waitm(p.s, p.mark("v"))
                p.s.activation(out=fB[:, :], in_=o_tok[:, c * D:(c + 1) * D],
                               func=AF.Square)
                p.waitm(p.v, p.mark("s"))
                p.v.reduce_sum(scr["m"][:, :], fB[:, :], axis=AX.X)
                p.v.tensor_scalar_mul(scr["m"][:, :], scr["m"][:, :],
                                      1.0 / 2048.0)
                p.waitm(p.s, p.mark("v"))
                p.s.activation(out=scr["t1"][:, :], in_=scr["m"][:, :],
                               func=AF.Sqrt, bias=com.cc(1e-6))
                p.waitm(p.v, p.mark("s"))
                p.v.reciprocal(rstdc[:, c:c + 1], scr["t1"][:, :])

            for nb in range(4):
                wstr.load(p, 0, wg_d[nb].rearrange("p k n -> p (k n)"))
                wstr.wait_loaded(p, p.t, 0)
                last = None
                for c in range(TC):
                    mats = [(xn_fm[:, k * T + c * 128: k * T + (c + 1) * 128],
                             wstr.bufs[0][:, k * 512:(k + 1) * 512])
                            for k in range(DC)]

                    def evacg(eng, pt):
                        p.waitm(eng, p.mark("v"))
                        eng.activation(out=sil[:, :], in_=pt[:, :],
                                       func=AF.Sigmoid)

                    last = mm_group(p, ppool, mats, evacg, evac_eng="s")
                    p.waitm(p.v, p.mark("s"))
                    dst = og_tok[:, c * D + nb * 512: c * D + (nb + 1) * 512]
                    p.v.scalar_tensor_tensor(
                        dst, o_tok[:, c * D + nb * 512: c * D + (nb + 1) * 512],
                        rstdc[:, c:c + 1], sil[:, :],
                        op0=ALU.mult, op1=ALU.mult)
                wstr.set_free(p, 0, last)

            p.waitm(p.t, p.mark("v"))
            for c in range(TC):
                for fc in range(DC):
                    i2 = tpool.next()
                    pt2 = tpool.tiles[i2]
                    w2 = tpool.wrote[i2]
                    p.waitm(p.t, tpool.freed[i2])
                    p.inc(p.t.transpose(
                        pt2[:, 0:128],
                        og_tok[:, c * D + fc * 128: c * D + (fc + 1) * 128],
                        com.ident[:, :]), w2)
                    p.wait(p.v, w2, w2.v)
                    p.v.tensor_scalar_mul(
                        og_fm[:, fc * T + c * 128: fc * T + (c + 1) * 128],
                        pt2[:, 0:128], rmsw[:, fc:fc + 1])
                    tpool.freed[i2] = p.mark("v")
            p.waitm(p.t, p.mark("v"))

            free_markers = [None, None]
            xres_free = [None, None]
            for nb in range(4):
                wstr.load(p, 0, wo_d[nb].rearrange("p k n -> p (k n)"))
                wstr.wait_loaded(p, p.t, 0)
                last = None
                for c in range(TC):
                    mats = [(og_fm[:, k * T + c * 128: k * T + (c + 1) * 128],
                             wstr.bufs[0][:, k * 512:(k + 1) * 512])
                            for k in range(DC)]
                    ob = (nb * TC + c) % 2
                    p.waitm(p.y, xres_free[ob])
                    mxr = p.dma(
                        p.y, xrl[ob], out=xres[ob][:, :],
                        in_=x_d[c * 128:(c + 1) * 128, nb * 512:(nb + 1) * 512])

                    def evacf(eng, pt, ob=ob, mxr=mxr):
                        p.waitm(eng, free_markers[ob])
                        p.waitm(eng, mxr)
                        eng.tensor_add(outst[ob][:, :], xres[ob][:, :], pt[:, :])

                    last = mm_group(p, ppool, mats, evacf)
                    xres_free[ob] = p.mark("v")
                    p.waitm(p.y, xres_free[ob])
                    free_markers[ob] = p.dma(
                        p.y, osf[ob],
                        out=xo_d[c * 128:(c + 1) * 128, nb * 512:(nb + 1) * 512],
                        in_=outst[ob][:, :])
                wstr.set_free(p, 0, last)
            p.wait(p.y, osf[0])
            p.wait(p.y, osf[1])

        run_walk(nc, block, sems, walk, resettables=[ppool, tpool, wstr])
    return nc


def l1aa_consts():
    i = np.arange(128, dtype=np.float32)
    iotaq = np.stack([i + 1, 128 + i + 1], axis=1)
    iotak = np.stack([255 - i, 255 - 128 - i], axis=1)
    l = np.arange(256, dtype=np.float32)[None, :]
    lmm = np.concatenate([l - i[:, None], l - (128 + i[:, None])], axis=1)
    trim = (lmm >= 0).astype(BF)
    return (np.ascontiguousarray(iotaq), np.ascontiguousarray(iotak),
            np.ascontiguousarray(lmm.astype(BF)), np.ascontiguousarray(trim))


def run_l1aa(x_shards, slope_rate, w_qkv, w_gate, w_out, rms_weight):
    nc = build_l1aa()
    wqkv_p = rhs_pack(w_qkv)
    wg_p = rhs_pack(w_gate)
    wo_p = rhs_pack(w_out)
    slope_r = np.ascontiguousarray(
        np.asarray(slope_rate, np.float32).reshape(1, 16))
    rms_p = col_pack(rms_weight)
    iotaq, iotak, lmm, trim = l1aa_consts()
    in_maps = [dict(x=x_shards[i], wqkv_p=wqkv_p, wg_p=wg_p, wo_p=wo_p,
                    slope_r=slope_r, rms_p=rms_p, flags=make_flags(i % 2),
                    iotaq=iotaq, iotak=iotak, lmm=lmm, trim=trim)
               for i in range(NCORES)]
    return _run(nc, in_maps, "xo")





def build_l0ab(detect_races=True):
    nc = bass.Bass("TRN2", target_bir_lowering=False,
                   detect_race_conditions=detect_races)
    x_d = nc.dram_tensor("x", [T, D], F32, kind="ExternalInput").ap()
    winz_d = nc.dram_tensor("winz_p", [8, 128, 16, 512], BF16,
                            kind="ExternalInput").ap()
    winx_d = nc.dram_tensor("winx_l", [33, 128, 16, 128], BF16,
                            kind="ExternalInput").ap()
    windt_d = nc.dram_tensor("windt_p", [16, 128, 32], BF16,
                             kind="ExternalInput").ap()
    wout_d = nc.dram_tensor("wout_p", [4, 4, 128, 8, 512], BF16,
                            kind="ExternalInput").ap()
    wconv_d = nc.dram_tensor("wconv_p", [128, 132], F32, kind="ExternalInput").ap()
    bconv_d = nc.dram_tensor("bconv_p", [128, 33], F32, kind="ExternalInput").ap()
    dtb_d = nc.dram_tensor("dtb_r", [1, 32], F32, kind="ExternalInput").ap()
    alog_d = nc.dram_tensor("alog_r", [1, 32], F32, kind="ExternalInput").ap()
    dco_d = nc.dram_tensor("dco_r", [1, 32], F32, kind="ExternalInput").ap()
    ucum_d = nc.dram_tensor("ucum", [128, 512], F32, kind="ExternalInput").ap()
    trim_d = nc.dram_tensor("trim", [128, 512], BF16, kind="ExternalInput").ap()
    flags_d = nc.dram_tensor("flags", [128, 2], F32, kind="ExternalInput").ap()
    xo_d = nc.dram_tensor("xo", [T, D], F32, kind="ExternalOutput").ap()

    z_dram = nc.dram_tensor("z_sc", [T, 4096], F32, kind="Internal").ap()
    xbc_dram = nc.dram_tensor("xbc_sc", [33, 128, T], F32, kind="Internal").ap()
    xpost_dram = nc.dram_tensor("xpost_sc", [32, 128, T], F32, kind="Internal").ap()
    y_dram = nc.dram_tensor("y_sc", [T, 4096], F32, kind="Internal").ap()
    acum_dram = nc.dram_tensor("acum_sc", [4, 32, 256], F32, kind="Internal").ap()
    bnd_in = nc.dram_tensor("bnd_in", [128, 99], F32, kind="Internal").ap()
    bnd_out = nc.dram_tensor("bnd_out", [128, 99], F32, kind="Internal").ap()
    st_in = nc.dram_tensor("st_in", [64, 4096], F32, kind="Internal").ap()
    st_out = nc.dram_tensor("st_out", [64, 4096], F32, kind="Internal").ap()
    cbc_dram = nc.dram_tensor("cbc_sc", [128, 1024], F32, kind="Internal").ap()

    with ExitStack() as st:
        sems = make_sems(nc, st)

        def sem(name):
            s = Sem(st.enter_context(nc.semaphore(name)))
            sems[name] = s
            return s

        com = Common(nc, st)
        sb = lambda n, sh, dt: st.enter_context(nc.sbuf_tensor(n, sh, dt))

        u_fm = sb("u_fm", [128, DC * T], BF16)
        fA = sb("fA", [128, 2048], F32)
        fB = sb("fB", [128, 2048], F32)
        xn_scr = sb("xn_scr", [128, 2048], BF16)
        zst = [sb(f"zst{i}", [128, 512], F32) for i in range(2)]
        dt_raw = sb("dt_raw", [128, 256], F32)
        adt_tok = dt_raw
        dt_tok = sb("dt_tok", [128, 256], F32)
        ws_cv = sb("ws_cv", [128, 1027], F32)
        convf = sb("convf", [128, 1024], F32)
        bnd_my = sb("bnd_my", [128, 99], F32)
        bnd_eff = sb("bnd_eff", [128, 99], F32)
        Bc_fm = sb("Bc_fm", [128, 1024], BF16)
        C_fm = sb("C_fm", [64, 1024], BF16)
        acum_sc = sb("acum_sc_sb", [32, 256], F32)
        sd_sc = sb("sd_sc", [32, 256], F32)
        nacum_tok = sb("nacum_tok", [128, 256], F32)
        sdo_tok = sb("sdo_tok", [128, 256], F32)
        decay_tok = sb("decay_tok", [128, 256], F32)
        pisdo_tok = sb("pisdo_tok", [128, 256], F32)
        GT_sb = sb("GT_sb", [128, 512], F32)
        M_bf = sb("M_bf", [128, 512], BF16)
        rb_sb = sb("rb_sb", [128, 256], F32)
        xload = [sb(f"xload{i}", [128, 256], F32) for i in range(2)]
        X_bf = sb("X_bf", [128, 256], BF16)
        xs_st = sb("xs_st", [128, 256], BF16)
        xd_st = sb("xd_st", [128, 256], F32)
        bd_st = sb("bd_st", [128, 128], BF16)
        yst = [sb(f"yst{i}", [128, 128], F32) for i in range(2)]
        arow = [sb(f"arow{i}", [1, 256], F32) for i in range(2)]
        S_run = sb("S_run", [64, 4096], F32)
        S_bf = sb("S_bfh", [64, 128], BF16)
        seed_bf = sb("seed_bf", [64, 4096], BF16)
        cd64 = sb("cd64", [64, 32], F32)
        crow = sb("crow", [1, 32], F32)
        cd_row = sb("cd_row", [1, 32], F32)
        pi_row = sb("pi_row", [1, 32], F32)
        pif_row = sb("pif_row", [1, 32], F32)
        dtb_bc = sb("dtb_bc", [128, 32], F32)
        A_bc = sb("A_bc", [128, 32], F32)
        D_bc = sb("D_bc", [128, 32], F32)
        pibc = sb("pibc", [128, 32], F32)
        rowbuf = sb("rowbuf", [1, 32], F32)
        wdt_sb = sb("wdt_sb", [128, 512], BF16)
        B_tok = wdt_sb
        ucum = sb("ucum_sb", [128, 512], F32)
        trim = sb("trim_sb", [128, 512], BF16)
        flags = sb("flags_sb", [128, 2], F32)
        wconv = sb("wconv_sb", [128, 132], F32)
        bconv = sb("bconv_sb", [128, 33], F32)
        yrow = sb("yrow", [128, 4096], F32)
        zrow = sb("zrow", [128, 4096], F32)
        bfx = sb("bfx", [128, 4096], BF16)
        gnT_c = sb("gnT_c", [128, 4096], BF16)
        yacc_c = sb("yacc_c", [128, 2048], F32)
        scr = dict(m=sb("scr_m", [128, 1], F32), q=sb("scr_q", [128, 1], F32),
                   r=sb("scr_r", [128, 1], F32), t1=sb("scr_t1", [128, 1], F32),
                   sq=fB)

        ld = sem("ld"); xld = sem("xld"); cld = sem("cld")
        zsf = [sem("zsf0"), sem("zsf1")]
        xbs = sem("xbs"); bndm = sem("bndm"); bndl = sem("bndl")
        cvl = sem("cvl"); xps = sem("xps")
        acs = sem("acs"); arl = [sem("arl0"), sem("arl1")]
        xll = [sem("xll0"), sem("xll1")]
        ysf = [sem("ysf0"), sem("ysf1")]
        stx = sem("stx"); stl = sem("stl")
        yrl = sem("yrl"); zrl = sem("zrl"); gns = sem("gns"); gnl = sem("gnl")
        osf = sem("osf"); xrl2 = sem("xrl2")
        ccb = sem("ccb"); ccst = sem("ccst"); wdt = sem("wdt")
        cfl = sem("cfl"); wofr = sem("wofr")

        ppool = PsumPool(nc, st, sems, "pp", 4)
        tpool = PsumPool(nc, st, sems, "tp", 2, shape=(128, 1024), dtype=BF16)
        tpoolf = PsumPool(nc, st, sems, "tf", 2, shape=(128, 512), dtype=F32)
        wstr = WStream(nc, st, sems, "ws", 16 * 512, BF16, nbuf=1, queue="g")
        wstx = WStream(nc, st, sems, "wx", 16 * 128, BF16, nbuf=2, queue="g")
        wsto = WStream(nc, st, sems, "wo", 8 * 512, BF16, nbuf=1, queue="g")

        block = st.enter_context(nc.Block())

        def bcast_row(p, src_row, ncol, dst, nrow=128):
            i = ppool.next()
            pt = ppool.tiles[i]
            p.waitm(p.t, ppool.freed[i])
            p.inc(p.t.matmul(pt[0:nrow, 0:ncol], com.ones_row[:, 0:nrow],
                             src_row, start=True, stop=True), ppool.wrote[i])
            p.wait(p.v, ppool.wrote[i], ppool.wrote[i].v)
            p.v.tensor_copy(dst, pt[0:nrow, 0:ncol])
            ppool.freed[i] = p.mark("v")

        def walk(p):
            mld = com.load(p, ld)
            p.waitm(p.v, mld)
            p.waitm(p.s, mld)
            p.waitm(p.t, mld)
            p.dma(p.y, cld, out=flags[:, :], in_=flags_d)
            p.dma(p.y, cld, out=ucum[:, :], in_=ucum_d)
            p.dma(p.y, cld, out=trim[:, :], in_=trim_d)
            p.dma(p.y, cld, out=wconv[:, :], in_=wconv_d)
            p.dma(p.y, cld, out=bconv[:, :], in_=bconv_d)
            p.dma(p.y, cld, out=rowbuf[:, :], in_=dtb_d)
            mc = p.dma(p.y, cld, out=crow[:, :], in_=alog_d)
            p.waitm(p.v, mc)
            p.waitm(p.s, mc)
            p.waitm(p.t, mc)
            bcast_row(p, rowbuf[:, :], 32, dtb_bc[:, :])
            bcast_row(p, crow[:, :], 32, A_bc[:, :])
            p.waitm(p.s, p.mark("v"))
            p.s.activation(out=A_bc[:, :], in_=A_bc[:, :], func=AF.Exp)
            p.waitm(p.v, p.mark("s"))
            p.v.tensor_scalar_mul(A_bc[:, :], A_bc[:, :], -1.0)
            for w_ in ppool.wrote:
                if w_.v > 0:
                    p.wait(p.y, w_, w_.v)
            mcd = p.dma(p.y, cld, out=rowbuf[:, :], in_=dco_d)
            p.waitm(p.v, mcd)
            p.waitm(p.t, mcd)
            bcast_row(p, rowbuf[:, :], 32, D_bc[:, :])
            p.v.memset(pi_row[:, :], 1.0)
            p.v.memset(S_run[:, :], 0.0)

            for c in range(TC):
                if c > 0:
                    p.waitm(p.y, p.mark("v"))
                    p.waitm(p.y, p.mark("s"))
                mx = p.dma(p.y, xld, out=fA[:, :],
                           in_=x_d[c * 128:(c + 1) * 128, :])
                p.waitm(p.v, mx)
                p.waitm(p.s, mx)
                mk = emit_ln_chunk(p, com, fA[:, :], xn_scr[:, :], scr,
                                   width=D, quant=False)
                p.waitm(p.t, mk)
                for fc in range(DC):
                    transpose_tile(
                        p, com, tpool,
                        xn_scr[:, fc * 128:(fc + 1) * 128],
                        u_fm[:, fc * T + c * 128: fc * T + (c + 1) * 128])
            p.waitm(p.t, p.mark("v"))

            zfree = [None, None]
            for nb in range(8):
                wstr.load(p, 0, winz_d[nb].rearrange("p k n -> p (k n)"))
                wstr.wait_loaded(p, p.t, 0)
                last = None
                for c in range(TC):
                    mats = [(u_fm[:, k * T + c * 128: k * T + (c + 1) * 128],
                             wstr.bufs[0][:, k * 512:(k + 1) * 512])
                            for k in range(DC)]
                    zb = (nb * TC + c) % 2

                    def evac(eng, pt, zb=zb):
                        p.waitm(eng, zfree[zb])
                        eng.tensor_copy(zst[zb][:, :], pt[:, :])

                    last = mm_group(p, ppool, mats, evac)
                    p.waitm(p.y, p.mark("v"))
                    zfree[zb] = p.dma(
                        p.y, zsf[zb],
                        out=z_dram[c * 128:(c + 1) * 128, nb * 512:(nb + 1) * 512],
                        in_=zst[zb][:, :])
                wstr.set_free(p, 0, last)

            mwd = p.dma(p.y, wdt,
                        out=wdt_sb[:, :].rearrange("p (k n) -> p k n", k=16),
                        in_=windt_d.rearrange("k p n -> p k n"))
            p.waitm(p.t, mwd)
            for c in range(TC):
                mats = [(u_fm[:, k * T + c * 128: k * T + (c + 1) * 128],
                         wdt_sb[:, k * 32:(k + 1) * 32])
                        for k in range(DC)]
                dst = dt_raw[:, c * 32:(c + 1) * 32]

                def evac(eng, pt, dst=dst):
                    eng.tensor_copy(dst, pt[:, 0:32])

                mm_group(p, ppool, mats, evac, ncols=32)

            for mb in range(33):
                slot = mb % 2
                wstx.load(p, slot, winx_d[mb].rearrange("p k n -> p (k n)"))
                wstx.wait_loaded(p, p.t, slot)
                last = None
                for sbk in range(2):
                    mats = [(wstx.bufs[slot][:, k * 128:(k + 1) * 128],
                             u_fm[:, k * T + sbk * 512: k * T + sbk * 512 + 512])
                            for k in range(DC)]
                    zb = sbk % 2

                    def evac(eng, pt, zb=zb, sbk=sbk, mb=mb):
                        p.waitm(eng, zfree[zb])
                        eng.tensor_copy(zst[zb][:, :], pt[:, :])
                        if sbk == 1:
                            eng.tensor_copy(bnd_my[:, mb * 3:(mb + 1) * 3],
                                            zst[zb][:, 509:512])

                    last = mm_group(p, ppool, mats, evac)
                    p.waitm(p.y, p.mark("v"))
                    zfree[zb] = p.dma(
                        p.y, zsf[zb],
                        out=xbc_dram[mb, :, sbk * 512: sbk * 512 + 512],
                        in_=zst[zb][:, :])
                wstx.set_free(p, slot, last)

            p.v.tensor_scalar_mul(bnd_my[:, :], bnd_my[:, :], flags[:, 0:1])
            p.waitm(p.y, p.mark("v"))
            p.dma(p.y, bndm, out=bnd_in, in_=bnd_my[:, :])
            p.wait(p.g, bndm)
            p.inc_cc(p.g.collective_compute(
                "AllReduce", ALU.add, replica_groups=RG_PAIRS,
                ins=[bnd_in], outs=[bnd_out]), ccb)
            p.wait(p.y, ccb, 1)
            mb_ = p.dma(p.y, bndl, out=bnd_eff[:, :], in_=bnd_out)
            p.waitm(p.v, mb_)
            p.v.tensor_scalar_mul(bnd_eff[:, :], bnd_eff[:, :], flags[:, 1:2])

            for c in range(TC):
                sl = slice(c * 32, (c + 1) * 32)
                p.v.tensor_tensor(dt_raw[:, sl], dt_raw[:, sl], dtb_bc[:, :],
                                  ALU.add)
            p.waitm(p.s, p.mark("v"))
            p.s.activation(out=fA[:, 0:256], in_=dt_raw[:, :], func=AF.Abs)
            p.waitm(p.v, p.mark("s"))
            p.v.tensor_scalar_mul(fA[:, 256:512], fA[:, 0:256], -1.0)
            p.waitm(p.s, p.mark("v"))
            p.s.activation(out=fA[:, 0:256], in_=fA[:, 256:512], func=AF.Exp)
            p.s.activation(out=fA[:, 256:512], in_=fA[:, 0:256], func=AF.Ln,
                           bias=1.0)
            p.waitm(p.v, p.mark("s"))
            p.v.tensor_scalar_max(dt_tok[:, :], dt_raw[:, :], 0.0)
            p.v.tensor_tensor(dt_tok[:, :], dt_tok[:, :], fA[:, 256:512],
                              ALU.add)
            for c in range(TC):
                sl = slice(c * 32, (c + 1) * 32)
                p.v.tensor_tensor(adt_tok[:, sl], dt_tok[:, sl], A_bc[:, :],
                                  ALU.mult)

            p.waitm(p.v, p.mark("s"))
            p.wait(p.y, zsf[0])
            p.wait(p.y, zsf[1])
            cfree = None
            for mb in range(33):
                p.waitm(p.y, p.mark("v"))
                mw = p.dma(p.y, cvl, out=ws_cv[:, 3:1027],
                           in_=xbc_dram[mb, :, :])
                p.waitm(p.v, mw)
                p.v.tensor_copy(ws_cv[:, 0:3], bnd_eff[:, mb * 3:(mb + 1) * 3])
                p.waitm(p.v, cfree)
                p.v.tensor_scalar_mul(convf[:, :], ws_cv[:, 0:1024],
                                      wconv[:, mb * 4: mb * 4 + 1])
                for j in range(1, 4):
                    p.v.scalar_tensor_tensor(
                        convf[:, :], ws_cv[:, j:j + 1024],
                        wconv[:, mb * 4 + j: mb * 4 + j + 1], convf[:, :],
                        op0=ALU.mult, op1=ALU.add)
                p.waitm(p.s, p.mark("v"))
                p.s.activation(out=convf[:, :], in_=convf[:, :], func=AF.Silu,
                               bias=bconv[:, mb:mb + 1])
                msl = p.mark("s")
                if mb < 32:
                    p.waitm(p.y, msl)
                    cfree = p.dma(p.y, xps, out=xpost_dram[mb, :, :],
                                  in_=convf[:, :])
                else:
                    p.waitm(p.v, msl)
                    p.v.tensor_copy(Bc_fm[:, :], convf[:, :])
                    p.waitm(p.y, msl)
                    mcb = p.dma(p.y, xps, out=cbc_dram, in_=convf[:, :])
                    p.wait(p.g, xps)
                    mcf = p.dma(p.g, cfl, out=C_fm[:, :],
                                in_=cbc_dram[64:128, :])
                    p.waitm(p.t, mcf)
                    p.waitm(p.v, mcf)
                    cfree = None
            p.waitm(p.t, p.mark("v"))
            for c in range(TC):
                i2 = tpool.next()
                pt2 = tpool.tiles[i2]
                w2 = tpool.wrote[i2]
                p.waitm(p.t, tpool.freed[i2])
                p.inc(p.t.transpose(pt2[:, 0:64],
                                    Bc_fm[0:64, c * 128:(c + 1) * 128],
                                    com.ident[0:64, 0:64]), w2)
                p.wait(p.v, w2, w2.v)
                p.v.tensor_copy(B_tok[:, c * 64:(c + 1) * 64], pt2[:, 0:64])
                tpool.freed[i2] = p.mark("v")

            ylast = [None, None]
            for sc in range(4):
                p.waitm(p.t, p.mark("v"))
                i2 = tpoolf.next()
                pt2 = tpoolf.tiles[i2]
                w2 = tpoolf.wrote[i2]
                p.waitm(p.t, tpoolf.freed[i2])
                for j in range(2):
                    c = sc * 2 + j
                    p.inc(p.t.matmul(pt2[0:32, 0:256],
                                     adt_tok[:, c * 32:(c + 1) * 32],
                                     ucum[:, j * 256:(j + 1) * 256],
                                     start=(j == 0), stop=(j == 1)),
                          w2) if j == 1 else p.t.matmul(
                        pt2[0:32, 0:256], adt_tok[:, c * 32:(c + 1) * 32],
                        ucum[:, j * 256:(j + 1) * 256],
                        start=(j == 0), stop=(j == 1))
                p.wait(p.v, w2, w2.v)
                p.v.tensor_copy(acum_sc[:, :], pt2[0:32, 0:256])
                tpoolf.freed[i2] = p.mark("v")
                p.waitm(p.y, p.mark("v"))
                p.dma(p.y, acs, out=acum_dram[sc], in_=acum_sc[:, :])
                p.waitm(p.s, p.mark("v"))
                p.s.activation(out=sd_sc[:, :], in_=acum_sc[:, :], func=AF.Exp)
                msd = p.mark("s")
                for j in range(2):
                    i2 = tpoolf.next()
                    pt2 = tpoolf.tiles[i2]
                    w2 = tpoolf.wrote[i2]
                    p.waitm(p.t, tpoolf.freed[i2])
                    p.waitm(p.t, msd)
                    p.inc(p.t.transpose(pt2[:, 0:32],
                                        sd_sc[0:32, j * 128:(j + 1) * 128],
                                        com.identf[0:32, 0:32]), w2)
                    p.wait(p.v, w2, w2.v)
                    p.v.tensor_copy(sdo_tok[:, j * 32:(j + 1) * 32],
                                    pt2[:, 0:32])
                    tpoolf.freed[i2] = p.mark("v")
                p.waitm(p.s, p.mark("v"))
                p.s.activation(out=sd_sc[:, :], in_=acum_sc[:, :], func=AF.Exp,
                               bias=acum_sc[:, 255:256],
                               scale=com.cc(-1.0)[0:32, :])
                msd = p.mark("s")
                for j in range(2):
                    i2 = tpoolf.next()
                    pt2 = tpoolf.tiles[i2]
                    w2 = tpoolf.wrote[i2]
                    p.waitm(p.t, tpoolf.freed[i2])
                    p.waitm(p.t, msd)
                    p.inc(p.t.transpose(pt2[:, 0:32],
                                        sd_sc[0:32, j * 128:(j + 1) * 128],
                                        com.identf[0:32, 0:32]), w2)
                    p.wait(p.v, w2, w2.v)
                    p.v.tensor_copy(decay_tok[:, j * 32:(j + 1) * 32],
                                    pt2[:, 0:32])
                    tpoolf.freed[i2] = p.mark("v")
                for j in range(2):
                    i2 = tpoolf.next()
                    pt2 = tpoolf.tiles[i2]
                    w2 = tpoolf.wrote[i2]
                    p.waitm(p.t, tpoolf.freed[i2])
                    p.inc(p.t.transpose(pt2[:, 0:32],
                                        acum_sc[0:32, j * 128:(j + 1) * 128],
                                        com.identf[0:32, 0:32]), w2)
                    p.wait(p.v, w2, w2.v)
                    p.v.tensor_scalar_mul(nacum_tok[:, j * 32:(j + 1) * 32],
                                          pt2[:, 0:32], -1.0)
                    tpoolf.freed[i2] = p.mark("v")
                i2 = tpoolf.next()
                pt2 = tpoolf.tiles[i2]
                w2 = tpoolf.wrote[i2]
                p.waitm(p.t, tpoolf.freed[i2])
                p.inc(p.t.transpose(pt2[0:1, 0:32], acum_sc[0:32, 255:256],
                                    com.identf[0:32, 0:32]), w2)
                p.wait(p.s, w2, w2.v)
                p.s.activation(out=cd_row[:, :], in_=pt2[0:1, 0:32], func=AF.Exp)
                tpoolf.freed[i2] = p.mark("s")
                p.waitm(p.v, p.mark("s"))
                p.v.tensor_scalar_mul(pif_row[:, :], pi_row[:, :],
                                      flags[0:1, 1:2])
                p.waitm(p.t, p.mark("v"))
                bcast_row(p, pif_row[:, :], 32, pibc[:, :])
                for j in range(2):
                    p.v.tensor_tensor(pisdo_tok[:, j * 32:(j + 1) * 32],
                                      sdo_tok[:, j * 32:(j + 1) * 32],
                                      pibc[:, :], ALU.mult)
                p.v.tensor_tensor(pi_row[:, :], pi_row[:, :], cd_row[:, :],
                                  ALU.mult)
                bcast_row(p, cd_row[:, :], 32, cd64[:, :], nrow=64)
                for j in range(2):
                    c = sc * 2 + j

                    def evg(eng, pt3, j=j):
                        eng.tensor_tensor(GT_sb[:, j * 256:(j + 1) * 256],
                                          pt3[:, 0:256],
                                          trim[:, j * 256:(j + 1) * 256],
                                          ALU.mult)

                    mm_group(p, ppool,
                             [(Bc_fm[0:64, c * 128:(c + 1) * 128],
                               C_fm[:, sc * 256:(sc + 1) * 256])],
                             evg, ncols=256)

                for h in range(32):
                    hcol = h * 128
                    ab = h % 2
                    p.wait(p.y, acs)
                    if h < 2:
                        p.wait(p.y, xps)
                    mar = p.dma(p.y, arl[ab], out=arow[ab][:, :],
                                in_=acum_dram[sc, h:h + 1, :])
                    mxl = p.dma(p.y, xll[ab], out=xload[ab][:, :],
                                in_=xpost_dram[h, :, sc * 256:(sc + 1) * 256])
                    p.waitm(p.t, mar)
                    i3 = ppool.next()
                    pt3 = ppool.tiles[i3]
                    p.waitm(p.t, ppool.freed[i3])
                    p.inc(p.t.matmul(pt3[:, 0:256], com.ones_row[:, :],
                                     arow[ab][:, :], start=True, stop=True),
                          ppool.wrote[i3])
                    p.wait(p.v, ppool.wrote[i3], ppool.wrote[i3].v)
                    p.v.tensor_copy(rb_sb[:, :], pt3[:, 0:256])
                    ppool.freed[i3] = p.mark("v")
                    p.waitm(p.t, mxl)
                    for j in range(2):
                        i2 = tpoolf.next()
                        pt2 = tpoolf.tiles[i2]
                        w2 = tpoolf.wrote[i2]
                        p.waitm(p.t, tpoolf.freed[i2])
                        p.inc(p.t.transpose(pt2[:, 0:128],
                                            xload[ab][:, j * 128:(j + 1) * 128],
                                            com.identf[:, :]), w2)
                        p.wait(p.v, w2, w2.v)
                        p.v.tensor_copy(X_bf[:, j * 128:(j + 1) * 128],
                                        pt2[:, 0:128])
                        tpoolf.freed[i2] = p.mark("v")
                    for j in range(2):
                        c = sc * 2 + j
                        p.v.tensor_scalar_mul(xs_st[:, j * 128:(j + 1) * 128],
                                              X_bf[:, j * 128:(j + 1) * 128],
                                              dt_tok[:, c * 32 + h: c * 32 + h + 1])
                        p.v.tensor_scalar_mul(xd_st[:, j * 128:(j + 1) * 128],
                                              X_bf[:, j * 128:(j + 1) * 128],
                                              D_bc[:, h:h + 1])
                    for j in range(2):
                        p.v.tensor_scalar(fA[:, 0:256], rb_sb[:, :],
                                          nacum_tok[:, j * 32 + h: j * 32 + h + 1],
                                          0.0, op0=ALU.add, op1=ALU.min)
                        p.waitm(p.s, p.mark("v"))
                        p.s.activation(out=fA[:, 256:512], in_=fA[:, 0:256],
                                       func=AF.Exp)
                        p.waitm(p.v, p.mark("s"))
                        p.v.tensor_tensor(M_bf[:, j * 256:(j + 1) * 256],
                                          fA[:, 256:512],
                                          GT_sb[:, j * 256:(j + 1) * 256],
                                          ALU.mult)
                    p.v.tensor_copy(S_bf[:, :], S_run[:, hcol:hcol + 128])
                    m_pre = p.mark("v")
                    p.waitm(p.t, m_pre)
                    for lc in range(2):
                        c = sc * 2 + lc
                        id_ = ppool.next()
                        ptd = ppool.tiles[id_]
                        wd_ = ppool.wrote[id_]
                        p.waitm(p.t, ppool.freed[id_])
                        for j in range(2):
                            p.inc(p.t.matmul(
                                ptd[:, 0:128],
                                M_bf[:, j * 256 + lc * 128: j * 256 + lc * 128 + 128],
                                xs_st[:, j * 128:(j + 1) * 128],
                                start=(j == 0), stop=(j == 1)), wd_) \
                                if j == 1 else p.t.matmul(
                                ptd[:, 0:128],
                                M_bf[:, j * 256 + lc * 128: j * 256 + lc * 128 + 128],
                                xs_st[:, j * 128:(j + 1) * 128],
                                start=(j == 0), stop=(j == 1))
                        io_ = ppool.next()
                        pto = ppool.tiles[io_]
                        wo_ = ppool.wrote[io_]
                        p.waitm(p.t, ppool.freed[io_])
                        p.inc(p.t.matmul(
                            pto[:, 0:128],
                            C_fm[:, c * 128:(c + 1) * 128],
                            S_bf[:, :], start=True, stop=True), wo_)
                        yb = lc % 2
                        p.wait(p.v, wo_, wo_.v)
                        p.wait(p.v, wd_, wd_.v)
                        p.waitm(p.v, ylast[yb])
                        p.v.scalar_tensor_tensor(
                            yst[yb][:, :], pto[:, 0:128],
                            sdo_tok[:, lc * 32 + h: lc * 32 + h + 1],
                            xd_st[:, lc * 128:(lc + 1) * 128],
                            op0=ALU.mult, op1=ALU.add)
                        p.v.tensor_tensor(yst[yb][:, :], yst[yb][:, :],
                                          ptd[:, 0:128], ALU.add)
                        ppool.freed[id_] = p.mark("v")
                        ppool.freed[io_] = ppool.freed[id_]
                        p.waitm(p.y, ppool.freed[id_])
                        ylast[yb] = p.dma(
                            p.y, ysf[yb],
                            out=y_dram[c * 128:(c + 1) * 128, hcol:hcol + 128],
                            in_=yst[yb][:, :])
                    for j in range(2):
                        c = sc * 2 + j
                        p.v.tensor_scalar_mul(
                            bd_st[:, j * 64:(j + 1) * 64],
                            B_tok[:, c * 64:(c + 1) * 64],
                            decay_tok[:, j * 32 + h: j * 32 + h + 1])
                    m_bd = p.mark("v")
                    p.waitm(p.t, m_bd)
                    iu = ppool.next()
                    ptu = ppool.tiles[iu]
                    wu_ = ppool.wrote[iu]
                    p.waitm(p.t, ppool.freed[iu])
                    for j in range(2):
                        p.inc(p.t.matmul(
                            ptu[0:64, 0:128], bd_st[:, j * 64:(j + 1) * 64],
                            xs_st[:, j * 128:(j + 1) * 128],
                            start=(j == 0), stop=(j == 1)), wu_) \
                            if j == 1 else p.t.matmul(
                            ptu[0:64, 0:128], bd_st[:, j * 64:(j + 1) * 64],
                            xs_st[:, j * 128:(j + 1) * 128],
                            start=(j == 0), stop=(j == 1))
                    p.wait(p.v, wu_, wu_.v)
                    p.v.scalar_tensor_tensor(
                        S_run[:, hcol:hcol + 128], S_run[:, hcol:hcol + 128],
                        cd64[:, h:h + 1], ptu[0:64, 0:128],
                        op0=ALU.mult, op1=ALU.add)
                    ppool.freed[iu] = p.mark("v")
                for j in range(2):
                    c = sc * 2 + j
                    p.v.tensor_copy(fB[:, c * 32:(c + 1) * 32],
                                    pisdo_tok[:, j * 32:(j + 1) * 32])

            p.v.tensor_scalar_mul(S_run[:, :], S_run[:, :], flags[0:64, 0:1])
            p.waitm(p.y, p.mark("v"))
            p.dma(p.y, stx, out=st_in, in_=S_run[:, :])
            p.wait(p.g, stx)
            p.inc_cc(p.g.collective_compute(
                "AllReduce", ALU.add, replica_groups=RG_PAIRS,
                ins=[st_in], outs=[st_out]), ccst)
            p.wait(p.y, ccst, 1)
            msl2 = p.dma(p.y, stl, out=S_run[:, :], in_=st_out)
            p.waitm(p.v, msl2)
            p.v.tensor_copy(seed_bf[:, :], S_run[:, :])
            m_seed = p.mark("v")
            p.waitm(p.t, m_seed)

            p.wait(p.y, ysf[0])
            p.wait(p.y, ysf[1])
            for c in range(TC):
                sc = c // 2
                p.waitm(p.y, p.mark("v"))
                myr = p.dma(p.y, yrl, out=yrow[:, :],
                            in_=y_dram[c * 128:(c + 1) * 128, :])
                mzr = p.dma(p.y, zrl, out=zrow[:, :],
                            in_=z_dram[c * 128:(c + 1) * 128, :])
                p.waitm(p.v, myr)
                for h in range(32):
                    ic = ppool.next()
                    ptc = ppool.tiles[ic]
                    wc_ = ppool.wrote[ic]
                    p.waitm(p.t, ppool.freed[ic])
                    p.inc(p.t.matmul(ptc[:, 0:128],
                                     C_fm[:, c * 128:(c + 1) * 128],
                                     seed_bf[:, h * 128:(h + 1) * 128],
                                     start=True, stop=True), wc_)
                    p.wait(p.v, wc_, wc_.v)
                    p.v.scalar_tensor_tensor(
                        yrow[:, h * 128:(h + 1) * 128], ptc[:, 0:128],
                        fB[:, c * 32 + h: c * 32 + h + 1],
                        yrow[:, h * 128:(h + 1) * 128],
                        op0=ALU.mult, op1=ALU.add)
                    ppool.freed[ic] = p.mark("v")
                p.waitm(p.s, mzr)
                p.waitm(p.s, p.mark("v"))
                p.s.activation(out=bfx[:, :], in_=zrow[:, :], func=AF.Silu)
                p.waitm(p.v, p.mark("s"))
                p.v.tensor_tensor(yrow[:, :], yrow[:, :], bfx[:, :], ALU.mult)
                p.waitm(p.s, p.mark("v"))
                p.s.activation(out=zrow[:, :], in_=yrow[:, :], func=AF.Square)
                p.waitm(p.v, p.mark("s"))
                p.v.reduce_sum(scr["m"][:, :], zrow[:, :], axis=AX.X)
                p.v.tensor_scalar_mul(scr["m"][:, :], scr["m"][:, :],
                                      1.0 / 4096.0)
                p.waitm(p.s, p.mark("v"))
                p.s.activation(out=scr["t1"][:, :], in_=scr["m"][:, :],
                               func=AF.Sqrt, bias=com.cc(1e-5))
                p.waitm(p.v, p.mark("s"))
                p.v.reciprocal(scr["r"][:, :], scr["t1"][:, :])
                p.v.tensor_scalar_mul(bfx[:, :], yrow[:, :], scr["r"][:, :])
                p.waitm(p.t, p.mark("v"))
                for fc in range(32):
                    transpose_tile(p, com, tpool,
                                   bfx[:, fc * 128:(fc + 1) * 128],
                                   gnT_c[:, fc * 128:(fc + 1) * 128])
                if c > 0:
                    p.wait(p.y, osf)
                mxr2 = p.dma(p.y, xrl2, out=yacc_c[:, :],
                             in_=x_d[c * 128:(c + 1) * 128, :])
                p.waitm(p.v, mxr2)
                p.waitm(p.t, p.mark("v"))
                for db in range(4):
                    ib = ppool.next()
                    ptb = ppool.tiles[ib]
                    wb_ = ppool.wrote[ib]
                    p.waitm(p.t, ppool.freed[ib])
                    for os_ in range(4):
                        slot = 0
                        wsto.load(p, slot,
                                  wout_d[db, os_].rearrange("p k n -> p (k n)"))
                        wsto.wait_loaded(p, p.t, slot)
                        for k8 in range(8):
                            fc = os_ * 8 + k8
                            lastmm = (os_ == 3 and k8 == 7)
                            inst = p.t.matmul(
                                ptb[:, :],
                                gnT_c[:, fc * 128:(fc + 1) * 128],
                                wsto.bufs[slot][:, k8 * 512:(k8 + 1) * 512],
                                start=(os_ == 0 and k8 == 0), stop=lastmm)
                            if lastmm:
                                p.inc(inst, wb_)
                            elif k8 == 7:
                                p.inc(inst, wofr)
                        if os_ < 3:
                            wsto.set_free(p, 0, (wofr, wofr.v))
                    wsto.set_free(p, 0, (wb_, wb_.v))
                    p.wait(p.v, wb_, wb_.v)
                    p.v.tensor_add(yacc_c[:, db * 512:(db + 1) * 512],
                                   yacc_c[:, db * 512:(db + 1) * 512],
                                   ptb[:, :])
                    ppool.freed[ib] = p.mark("v")
                p.waitm(p.y, p.mark("v"))
                p.dma(p.y, osf, out=xo_d[c * 128:(c + 1) * 128, :],
                      in_=yacc_c[:, :])
            p.wait(p.y, osf)

        run_walk(nc, block, sems, walk,
                 resettables=[ppool, tpool, tpoolf, wstr, wstx, wsto])
    return nc


def l0ab_consts():
    i = np.arange(128, dtype=np.float32)
    n = np.arange(256, dtype=np.float32)[None, :]
    u0 = (i[:, None] <= n).astype(np.float32)
    u1 = ((128 + i[:, None]) <= n).astype(np.float32)
    return np.ascontiguousarray(np.concatenate([u0, u1], axis=1))


def run_l0ab(x_shards, winT, woutT, W_conv, b_conv, dt_bias, A_log, Dv):
    nc = build_l0ab()
    winz_p = rhs_pack(winT[:, 0:4096].astype(BF))
    winx_l = lhsT_pack(winT[:, 4096:8320].astype(BF))
    windt_p = np.ascontiguousarray(
        winT[:, 8320:8352].astype(BF).reshape(16, 128, 32))
    wout_p = np.ascontiguousarray(
        woutT.astype(BF).reshape(4, 8, 128, 4, 512).transpose(3, 0, 2, 1, 4))
    wconv_p = np.ascontiguousarray(
        np.asarray(W_conv, np.float32)[:, 0, :].reshape(33, 128, 4)
        .transpose(1, 0, 2).reshape(128, 132))
    bconv_p = np.ascontiguousarray(
        np.asarray(b_conv, np.float32).reshape(33, 128).T)
    dtb_r = np.ascontiguousarray(np.asarray(dt_bias, np.float32).reshape(1, 32))
    alog_r = np.ascontiguousarray(np.asarray(A_log, np.float32).reshape(1, 32))
    dco_r = np.ascontiguousarray(np.asarray(Dv, np.float32).reshape(1, 32))
    ucum = l0ab_consts()
    _, _, _, trim = l1aa_consts()
    in_maps = [dict(x=x_shards[i], winz_p=winz_p, winx_l=winx_l,
                    windt_p=windt_p, wout_p=wout_p, wconv_p=wconv_p,
                    bconv_p=bconv_p, dtb_r=dtb_r, alog_r=alog_r, dco_r=dco_r,
                    ucum=ucum, trim=trim, flags=make_flags(i % 2))
               for i in range(NCORES)]
    return _run(nc, in_maps, "xo")





OVERLAY_ALIGN = 32


class SubAlloc:

    def __init__(self, nc, base, top, tag):
        self.nc = nc
        self.off = base
        self.top = top
        self.tag = tag
        self.n = 0

    def sb(self, name, shape, dtype):
        nbytes = shape[1] * {F32: 4, BF16: 2}[dtype]
        off = (self.off + OVERLAY_ALIGN - 1) // OVERLAY_ALIGN * OVERLAY_ALIGN
        assert off + nbytes <= self.top, \
            f"overlay overflow {self.tag}/{name}: {off}+{nbytes} > {self.top}"
        h = self.nc.alloc_sbuf_tensor_at(f"{self.tag}_{name}", list(shape),
                                         dtype, offset=off)
        self.off = off + nbytes
        self.n += 1
        return h

    def at(self, offset, tag):
        a = SubAlloc(self.nc, offset, self.top, tag)
        return a


def barrier(p, sems, skip=()):
    for eng in (p.t, p.v, p.s, p.g, p.y):
        for name, s in sems.items():
            if name in skip:
                continue
            if s.v > 0:
                eng.wait_ge(s.h, s.v)


def build_fused(inputs_meta=None, detect_races=True):
    nc = bass.Bass("TRN2", target_bir_lowering=False,
                   detect_race_conditions=detect_races)
    x_d = nc.dram_tensor("x", [T, D], F32, kind="ExternalInput").ap()
    xo_d = nc.dram_tensor("xo", [T, D], F32, kind="ExternalOutput").ap()
    aa_wq = nc.dram_tensor("aa_wq", [4, 128, 16, 512], F32, kind="ExternalInput").ap()
    aa_wk = nc.dram_tensor("aa_wk", [4, 128, 16, 512], F32, kind="ExternalInput").ap()
    aa_wv = nc.dram_tensor("aa_wv", [16, 128, 16, 128], F32, kind="ExternalInput").ap()
    aa_wo = nc.dram_tensor("aa_wo", [4, 128, 16, 512], F32, kind="ExternalInput").ap()
    wfcT_f = nc.dram_tensor("wfcT_f", [2048, 8192], F32, kind="ExternalInput").ap()
    wprT_f = nc.dram_tensor("wprT_f", [8192, 2048], F32, kind="ExternalInput").ap()
    wfc_sl = nc.dram_tensor("wfc_sl", [1024, 2048], F32, kind="ExternalInput").ap()
    wpr_sl = nc.dram_tensor("wpr_sl", [256, 8192], F32, kind="ExternalInput").ap()
    winT_f = nc.dram_tensor("winT_f", [2048, 8352], F32, kind="ExternalInput").ap()
    wouT_f = nc.dram_tensor("wouT_f", [4096, 2048], F32, kind="ExternalInput").ap()
    wgaT_f = nc.dram_tensor("wgaT_f", [2048, 8192], F32, kind="ExternalInput").ap()
    wupT_f = nc.dram_tensor("wupT_f", [2048, 8192], F32, kind="ExternalInput").ap()
    wdoT_f = nc.dram_tensor("wdoT_f", [8192, 2048], F32, kind="ExternalInput").ap()
    wconv_d = nc.dram_tensor("wconv_p", [128, 132], F32, kind="ExternalInput").ap()
    bconv_d = nc.dram_tensor("bconv_p", [128, 33], F32, kind="ExternalInput").ap()
    dtb_d = nc.dram_tensor("dtb_r", [1, 32], F32, kind="ExternalInput").ap()
    alog_d = nc.dram_tensor("alog_r", [1, 32], F32, kind="ExternalInput").ap()
    dco_d = nc.dram_tensor("dco_r", [1, 32], F32, kind="ExternalInput").ap()
    normw_d = nc.dram_tensor("normw_p", [128, 32], F32, kind="ExternalInput").ap()
    ucum_d = nc.dram_tensor("ucum", [128, 512], F32, kind="ExternalInput").ap()
    la_wqkv = nc.dram_tensor("la_wqkv", [12, 128, 16, 512], F32, kind="ExternalInput").ap()
    la_wg = nc.dram_tensor("la_wg", [4, 128, 16, 512], F32, kind="ExternalInput").ap()
    la_wo = nc.dram_tensor("la_wo", [4, 128, 16, 512], F32, kind="ExternalInput").ap()
    slope_d = nc.dram_tensor("slope_r", [1, 16], F32, kind="ExternalInput").ap()
    rms_d = nc.dram_tensor("rms_p", [128, 16], F32, kind="ExternalInput").ap()
    iotaq_d = nc.dram_tensor("iotaq", [128, 2], F32, kind="ExternalInput").ap()
    iotak_d = nc.dram_tensor("iotak", [128, 2], F32, kind="ExternalInput").ap()
    lmm_d = nc.dram_tensor("lmm", [128, 512], BF16, kind="ExternalInput").ap()
    trim_d = nc.dram_tensor("trim", [128, 512], BF16, kind="ExternalInput").ap()
    ab_wq = nc.dram_tensor("ab_wq", [16, 128, 16, 128], F32, kind="ExternalInput").ap()
    ab_wk = nc.dram_tensor("ab_wk", [16, 128, 16, 128], F32, kind="ExternalInput").ap()
    ab_wv = nc.dram_tensor("ab_wv", [4, 128, 16, 512], F32, kind="ExternalInput").ap()
    ab_wo = nc.dram_tensor("ab_wo", [4, 128, 16, 512], F32, kind="ExternalInput").ap()
    mask_d = nc.dram_tensor("mask", [T, 2048], BF16, kind="ExternalInput").ap()
    flags_d = nc.dram_tensor("flags", [128, 2], F32, kind="ExternalInput").ap()

    dram = lambda n, sh, dt: nc.dram_tensor(n, sh, dt, kind="Internal").ap()
    x_a = dram("x_a", [T, D], F32)
    x_b = dram("x_b", [T, D], F32)
    tfc_bf = dram("tfc_bf", [2048, 8192], BF16)
    tpr_bf = dram("tpr_bf", [8192, 2048], BF16)
    wou_bf = dram("wou_bf", [4096, 2048], BF16)
    wga_bf = dram("wga_bf", [2048, 8192], BF16)
    wup_bf = dram("wup_bf", [2048, 8192], BF16)
    wdo_bf = dram("wdo_bf", [8192, 2048], BF16)
    s_in = dram("s_in", [2048, 2048], F32)
    s_out = dram("s_out", [2048, 2048], F32)
    gam_in = dram("gam_in", [1, 512], F32)
    gam_out = dram("gam_out", [1, 512], F32)
    z_dram = dram("z_sc", [T, 4096], F32)
    xbc_dram = dram("xbc_sc", [33, 128, T], F32)
    xpost_dram = dram("xpost_sc", [32, 128, T], F32)
    y_dram = dram("y_sc", [T, 4096], F32)
    acum_dram = dram("acum_sc", [4, 32, 256], F32)
    bnd_in = dram("bnd_in", [128, 99], F32)
    bnd_out = dram("bnd_out", [128, 99], F32)
    st_in = dram("st_in", [64, 4096], F32)
    st_out = dram("st_out", [64, 4096], F32)
    cbc_dram = dram("cbc_sc", [128, 1024], F32)
    hg_d = dram("hg", [T, 8192], BF16)
    kv_in = dram("kv_in", [128, 2048], F32)
    kv_out = dram("kv_out", [128, 2048], F32)
    ck_in = dram("ck_in", [16, 128, 2048], BF16)
    ck_out = dram("ck_out", [16, 128, 2048], BF16)
    cv_in = dram("cv_in", [2, 2048, 1024], BF16)
    cv_out = dram("cv_out", [2, 2048, 1024], BF16)

    with ExitStack() as st:
        sems = make_sems(nc, st)
        _cc_n = [0]

        def sem(name):
            if name in sems:
                return sems[name]
            s = Sem(st.enter_context(nc.semaphore(name)))
            sems[name] = s
            return s

        def ccsem():
            _cc_n[0] += 1
            return sem(f"cc{_cc_n[0]}")

        com = Common(nc, st)
        sba = lambda n, sh, dt: st.enter_context(nc.sbuf_tensor(n, sh, dt))

        fA = sba("fA", [128, 2048], F32)
        fB = sba("fB", [128, 2048], F32)
        xn_scr = sba("xn_scr", [128, 2048], BF16)
        scr = dict(m=sba("scr_m", [128, 1], F32), q=sba("scr_q", [128, 1], F32),
                   r=sba("scr_r", [128, 1], F32), t1=sba("scr_t1", [128, 1], F32),
                   sq=fB)
        flags = sba("flags_sb", [128, 2], F32)
        ucum = sba("ucum_sb", [128, 512], F32)
        trim = sba("trim_sb", [128, 512], BF16)
        lmm = sba("lmm_sb", [128, 512], BF16)
        iotaq = sba("iotaq_sb", [128, 2], F32)
        iotak = sba("iotak_sb", [128, 2], F32)
        wconv = sba("wconv_sb", [128, 132], F32)
        bconv = sba("bconv_sb", [128, 33], F32)
        rmsw = sba("rmsw_sb", [128, 16], F32)
        normw = sba("normw_sb", [128, 32], F32)
        srow = sba("srow", [1, 16], F32)
        rowbuf = sba("rowbuf", [1, 32], F32)
        gcols = sba("gcols", [128, 2], F32)
        gamcol = sba("gamcol", [128, 2], F32)
        grow = sba("grow", [1, 512], F32)

        ppool = PsumPool(nc, st, sems, "pp", 4)
        tpool = PsumPool(nc, st, sems, "tp", 2, shape=(128, 1024), dtype=BF16)
        tpoolf = PsumPool(nc, st, sems, "tf", 2, shape=(128, 512), dtype=F32)

        base0 = nc.sbuf_base
        TOP = 229344 - 64
        mk_alloc = lambda tag: SubAlloc(nc, base0, TOP, tag)

        A = {}

        def sect(tag):
            a = mk_alloc(tag)
            A[tag] = a
            return a

        a = sect("pr")
        pr = dict(
            sliceA=a.sb("sliceA", [128, 8 * 2048], F32),
            tA=a.sb("tA", [128, 2048], F32),
            tS=a.sb("tS", [128, 2048], F32),
            tB=a.sb("tB", [128, 2048], BF16),
        )
        a = sect("s0")
        aa = dict(
            x_fm=a.sb("x_fm", [128, DC * T], BF16),
            qk_sb=a.sb("qk_sb", [128, TC * D * 2], BF16),
            v_fm=a.sb("v_fm", [128, DC * T], BF16),
            outst=[a.sb(f"outst{i}", [128, 512], F32) for i in range(2)],
            xres=[a.sb(f"xres{i}", [128, 512], F32) for i in range(2)],
            ws=[a.sb(f"ws{i}", [128, 16 * 512], BF16) for i in range(2)],
            wl=[a.sb(f"wl{i}", [128, 16 * 128], BF16) for i in range(2)],
        )
        a = sect("s1")
        ff = dict(
            yacc=a.sb("yacc", [128, TC * D], F32),
            xh_fm=a.sb("xh_fm", [128, DC * T], BF16),
            fBig=a.sb("fBig", [128, 8192], F32),
            hgrow=a.sb("hgrow", [128, 8192], BF16),
            hst=[a.sb(f"hst{i}", [128, 512], BF16) for i in range(2)],
            sumac=a.sb("sumac", [128, TC], F32),
            sqac=a.sb("sqac", [128, TC], F32),
            m2c=a.sb("m2c", [128, TC], F32),
            c1c=a.sb("c1c", [128, TC], F32),
            c2c=a.sb("c2c", [128, TC], F32),
            ws=[a.sb(f"ws{i}", [128, 16 * 512], BF16) for i in range(2)],
        )
        a = sect("s2")
        _u_off = a.off
        _ufm = a.sb("u_fm", [128, DC * T], BF16)
        _ws_off = a.off
        _ws = [a.sb(f"ws{i}", [128, 16 * 512], BF16) for i in range(1)]
        _wx = [a.sb(f"wx{i}", [128, 16 * 128], BF16) for i in range(2)]
        _ep = a.at(_u_off, "s2e")
        _ep2 = a.at(_ws_off, "s2f")
        ab = dict(
            u_fm=_ufm,
            ws=_ws,
            wx=_wx,
            yrow=_ep.sb("yrow", [128, 4096], F32),
            zrow=_ep.sb("zrow", [128, 4096], F32),
            bfx=_ep2.sb("bfx", [128, 4096], BF16),
            gnT_c=_ep2.sb("gnT_c", [128, 4096], BF16),
            zst=[a.sb(f"zst{i}", [128, 512], F32) for i in range(2)],
            dt_raw=a.sb("dt_raw", [128, 256], F32),
            dt_tok=a.sb("dt_tok", [128, 256], F32),
            ws_cv=a.sb("ws_cv", [128, 1027], F32),
            convf=a.sb("convf", [128, 1024], F32),
            bnd_my=a.sb("bnd_my", [128, 99], F32),
            bnd_eff=a.sb("bnd_eff", [128, 99], F32),
            Bc_fm=a.sb("Bc_fm", [128, 1024], BF16),
            C_fm=a.sb("C_fm", [64, 1024], BF16),
            B_tok=a.sb("B_tok", [128, 512], BF16),
            acum_sc=a.sb("acum_sc", [32, 256], F32),
            sd_sc=a.sb("sd_sc", [32, 256], F32),
            nacum_tok=a.sb("nacum_tok", [128, 256], F32),
            sdo_tok=a.sb("sdo_tok", [128, 256], F32),
            decay_tok=a.sb("decay_tok", [128, 256], F32),
            pisdo_tok=a.sb("pisdo_tok", [128, 256], F32),
            pisdo_all=a.sb("pisdo_all", [128, 256], F32),
            GT_sb=a.sb("GT_sb", [128, 512], F32),
            M_bf=a.sb("M_bf", [128, 512], BF16),
            rb_sb=a.sb("rb_sb", [128, 256], F32),
            xload=[a.sb(f"xload{i}", [128, 256], F32) for i in range(2)],
            X_bf=a.sb("X_bf", [128, 256], BF16),
            xs_st=a.sb("xs_st", [128, 256], BF16),
            xd_st=a.sb("xd_st", [128, 256], F32),
            bd_st=a.sb("bd_st", [128, 128], BF16),
            yst=[a.sb(f"yst{i}", [128, 128], F32) for i in range(2)],
            arow=[a.sb(f"arow{i}", [1, 256], F32) for i in range(2)],
            S_run=a.sb("S_run", [64, 4096], F32),
            S_bf=a.sb("S_bfh", [64, 128], BF16),
            seed_bf=a.sb("seed_bf", [64, 4096], BF16),
            cd64=a.sb("cd64", [64, 32], F32),
            crow=a.sb("crow", [1, 32], F32),
            cd_row=a.sb("cd_row", [1, 32], F32),
            pi_row=a.sb("pi_row", [1, 32], F32),
            pif_row=a.sb("pif_row", [1, 32], F32),
            dtb_bc=a.sb("dtb_bc", [128, 32], F32),
            A_bc=a.sb("A_bc", [128, 32], F32),
            D_bc=a.sb("D_bc", [128, 32], F32),
            pibc=a.sb("pibc", [128, 32], F32),
            wdt_sb=a.sb("wdt_sb", [128, 512], BF16),
            yacc_c=a.sb("yacc_c", [128, 2048], F32),
            wo=[a.sb(f"wo{i}", [128, 8 * 512], BF16) for i in range(1)],
        )
        a = sect("s3")
        _xn3 = a.sb("xn_fm", [128, DC * T], BF16)
        _qkv_off = a.off
        _qkv = a.sb("qkv_h", [128, 3 * 8 * 1024], BF16)
        _fin = a.at(_qkv_off + 32768, "s3f")
        la = dict(
            xn_fm=_xn3,
            qkv_h=_qkv,
            outst=[_fin.sb(f"outst{i}", [128, 512], F32) for i in range(2)],
            xres=[_fin.sb(f"xres{i}", [128, 512], F32) for i in range(2)],
            o_tok=a.sb("o_tok", [128, TC * D], BF16),
            qs_fm=a.sb("qs_fm", [128, 16 * T], BF16),
            kv_run=a.sb("kv_run", [128, 2048], F32),
            qf=a.sb("qf", [128, 256], BF16),
            kf=a.sb("kf", [128, 256], BF16),
            qs_st=a.sb("qs_st", [128, 256], BF16),
            ks_st=a.sb("ks_st", [128, 256], BF16),
            kvb=a.sb("kvb", [128, 128], BF16),
            M_sb=a.sb("M_sb", [128, 512], BF16),
            ddTh=a.sb("ddTh", [128, 512], BF16),
            sil=a.sb("sil", [128, 512], BF16),
            negS=a.sb("negS", [128, 16], F32),
            qd_all=a.sb("qd_all", [128, 32], F32),
            kd_all=a.sb("kd_all", [128, 32], F32),
            bd_all=a.sb("bd_all", [128, 16], F32),
            bdjs=a.sb("bdjs", [128, 64], F32),
            rstdc=a.sb("rstdc", [128, TC], F32),
            ws=[a.sb(f"ws{i}", [128, 16 * 512], BF16) for i in range(1)],
        )
        a = sect("s4")
        lf = dict(
            yacc=a.sb("yacc", [128, TC * D], F32),
            xn_fm=a.sb("xn_fm", [128, DC * T], BF16),
            gatedT=a.sb("gatedT", [128, 16 * T], BF16),
            sil=a.sb("sil", [128, 512], BF16),
            gated_st=a.sb("gated_st", [128, 512], BF16),
            wsg=[a.sb("wsg0", [128, 16 * 512], BF16)],
            wsu=[a.sb("wsu0", [128, 16 * 512], BF16)],
            wsd=[a.sb("wsd0", [128, 16 * 512], BF16)],
        )
        a = sect("s5")
        lb = dict(
            xn_fm=a.sb("xn_fm", [128, DC * T], BF16),
            q_fm=a.sb("q_fm", [128, 16 * T], BF16),
            kg=a.sb("kg", [128, 4 * 2048], BF16),
            vg=a.sb("vg", [128, 16 * 512], BF16),
            o_tok=a.sb("o_tok", [128, TC * D], BF16),
            P_sb=a.sb("P_sb", [128, 2048], BF16),
            PT_sb=a.sb("PT_sb", [128, 16 * 128], BF16),
            mask_sb=a.sb("mask_sb", [128, 2048], BF16),
            st2=[a.sb(f"st2{i}", [128, 512], BF16) for i in range(2)],
            outst=[a.sb(f"outst{i}", [128, 512], F32) for i in range(2)],
            xres=[a.sb(f"xres{i}", [128, 512], F32) for i in range(2)],
            ws=[a.sb(f"ws{i}", [128, 16 * 512], BF16) for i in range(1)],
            wl=[a.sb(f"wl{i}", [128, 16 * 128], BF16) for i in range(2)],
        )
        ovmax = max(x.off for x in A.values())
        assert ovmax <= TOP, f"overlay {ovmax} > {TOP}"

        block = st.enter_context(nc.Block())

        def ln_transpose_x(p, x_in, x_fm_dst, quant=False):
            xld = sem("xld")
            for c in range(TC):
                if c > 0:
                    p.waitm(p.y, p.mark("v"))
                    p.waitm(p.y, p.mark("s"))
                mx = p.dma(p.y, xld, out=fA[:, :],
                           in_=x_in[c * 128:(c + 1) * 128, :])
                p.waitm(p.v, mx)
                p.waitm(p.s, mx)
                mk = emit_ln_chunk(p, com, fA[:, :], xn_scr[:, :], scr,
                                   width=D, quant=quant)
                p.waitm(p.t, mk)
                for fc in range(DC):
                    transpose_tile(
                        p, com, tpool,
                        xn_scr[:, fc * 128:(fc + 1) * 128],
                        x_fm_dst[:, fc * T + c * 128: fc * T + (c + 1) * 128])
            p.waitm(p.t, p.mark("v"))

        class TStream:

            def __init__(self, bufs, name):
                self.bufs = bufs
                self.lsem = [sem(f"{name}{i}l") for i in range(len(bufs))]
                self.freed = [None] * len(bufs)

            def load_t(self, p, slot, subs):
                p.waitm(p.y, self.freed[slot])
                s_ = self.lsem[slot]
                for (c0, c1, src) in subs:
                    inst = p.y.dma_start(out=self.bufs[slot][:, c0:c1],
                                         in_=src, transpose=True)
                    s_.v += 16
                    inst.then_inc(s_.h, 16)

            def load_plain(self, p, slot, src, queue_eng, cols=None):
                p.waitm(queue_eng, self.freed[slot])
                dst = self.bufs[slot][:, :] if cols is None else \
                    self.bufs[slot][:, 0:cols]
                inst = queue_eng.dma_start(out=dst, in_=src)
                s_ = self.lsem[slot]
                s_.v += 16
                inst.then_inc(s_.h, 16)

            def wait_loaded(self, p, eng, slot):
                s_ = self.lsem[slot]
                if s_.v > 0:
                    eng.wait_ge(s_.h, s_.v)

            def set_free(self, p, slot, marker):
                self.freed[slot] = marker

        def rhsT_subs(w_bf, nb, nk, ncols):
            return [(k * ncols, (k + 1) * ncols,
                     w_bf[nb * ncols:(nb + 1) * ncols, k * 128:(k + 1) * 128])
                    for k in range(nk)]

        def walk(p):
            _cc_n[0] = 0
            ld = sem("ld")
            com.load(p, ld)
            p.dma(p.y, ld, out=flags[:, :], in_=flags_d)
            p.dma(p.y, ld, out=ucum[:, :], in_=ucum_d)
            p.dma(p.y, ld, out=trim[:, :], in_=trim_d)
            p.dma(p.y, ld, out=lmm[:, :], in_=lmm_d)
            p.dma(p.y, ld, out=iotaq[:, :], in_=iotaq_d)
            p.dma(p.y, ld, out=iotak[:, :], in_=iotak_d)
            p.dma(p.y, ld, out=wconv[:, :], in_=wconv_d)
            p.dma(p.y, ld, out=bconv[:, :], in_=bconv_d)
            p.dma(p.y, ld, out=rmsw[:, :], in_=rms_d)
            p.dma(p.y, ld, out=normw[:, :], in_=normw_d)
            mi = p.dma(p.y, ld, out=srow[:, :], in_=slope_d)
            p.waitm(p.v, mi)
            p.waitm(p.s, mi)
            p.waitm(p.t, mi)

            bg = ("pr_cst",)
            with nc.named_scope("prep"):
                emit_prep(p)
                barrier(p, sems, skip=bg)
            with nc.named_scope("l0aa"):
                emit_l0aa(p, x_d, x_a)
                barrier(p, sems, skip=bg)
            with nc.named_scope("l0ffn_a"):
                emit_l0ffn(p, x_a, x_b)
                barrier(p, sems, skip=bg)
            with nc.named_scope("l0ab"):
                emit_l0ab(p, x_b, x_a)
                barrier(p, sems)
            with nc.named_scope("l0ffn_b"):
                emit_l0ffn(p, x_a, x_b)
                barrier(p, sems)
            with nc.named_scope("l1aa"):
                emit_l1aa(p, x_b, x_a)
                barrier(p, sems)
            with nc.named_scope("l1ffn_a"):
                emit_l1ffn(p, x_a, x_b)
                barrier(p, sems)
            with nc.named_scope("l1ab"):
                emit_l1ab(p, x_b, x_a)
                barrier(p, sems)
            with nc.named_scope("l1ffn_b"):
                emit_l1ffn(p, x_a, xo_d)
            osf = sems["ffosf"]
            p.wait(p.y, osf)
            p.wait(p.g, osf)

        def emit_prep(p):
            wld = sem("pr_wld")
            csm = sem("pr_cst")
            tst = sem("pr_tst")
            NFC = 8192.0 * 2048.0
            NPR = 2048.0 * 8192.0

            for (src, dst, nr, nc_) in ((wouT_f, wou_bf, 4096, 2048),
                                        (wgaT_f, wga_bf, 2048, 8192),
                                        (wupT_f, wup_bf, 2048, 8192),
                                        (wdoT_f, wdo_bf, 8192, 2048)):
                step = max(128, (1 << 20) // nc_)
                for r0 in range(0, nr, step):
                    r1 = min(nr, r0 + step)
                    p.dma(p.g, csm, out=dst[r0:r1, :], in_=src[r0:r1, :])

            m1 = p.dma(p.y, wld, out=pr["sliceA"][:, :].rearrange(
                "p (t k) -> p t k", t=8), in_=wfc_sl.rearrange(
                "(t p) k -> p t k", p=128))
            p.waitm(p.v, m1)
            for t_ in range(8):
                tile = pr["sliceA"][:, t_ * 2048:(t_ + 1) * 2048]
                p.v.reduce_sum(scr["t1"][:, :], tile, axis=AX.X,
                               apply_absolute_value=True)
                if t_ == 0:
                    p.v.tensor_copy(scr["m"][:, :], scr["t1"][:, :])
                else:
                    p.v.tensor_add(scr["m"][:, :], scr["m"][:, :],
                                   scr["t1"][:, :])
            p.waitm(p.y, p.mark("v"))
            m2 = p.dma(p.y, wld, out=pr["sliceA"][:, :].rearrange(
                "p (t k) -> p t k", t=2), in_=wpr_sl.rearrange(
                "(t p) k -> p t k", p=128))
            p.waitm(p.v, m2)
            for t_ in range(2):
                tile = pr["sliceA"][:, t_ * 8192:(t_ + 1) * 8192]
                p.v.reduce_sum(scr["t1"][:, :], tile, axis=AX.X,
                               apply_absolute_value=True)
                if t_ == 0:
                    p.v.tensor_copy(scr["q"][:, :], scr["t1"][:, :])
                else:
                    p.v.tensor_add(scr["q"][:, :], scr["q"][:, :],
                                   scr["t1"][:, :])
            p.waitm(p.t, p.mark("v"))
            for (src_col, gi) in ((scr["m"], 0), (scr["q"], 1)):
                i = ppool.next()
                pt = ppool.tiles[i]
                p.waitm(p.t, ppool.freed[i])
                p.inc(p.t.matmul(pt[0:1, 0:1], src_col[:, :], com.cc(1.0),
                                 start=True, stop=True), ppool.wrote[i])
                p.wait(p.v, ppool.wrote[i], ppool.wrote[i].v)
                p.v.tensor_copy(gcols[0:1, gi:gi + 1], pt[0:1, 0:1])
                ppool.freed[i] = p.mark("v")
            p.v.memset(grow[:, :], 0.0)
            p.v.tensor_copy(grow[0:1, 0:2], gcols[0:1, 0:2])
            p.waitm(p.y, p.mark("v"))
            ccm = sem("pr_ccm")
            p.dma(p.y, ccm, out=gam_in, in_=grow[:, :])
            p.wait(p.g, ccm)
            cc = ccsem()
            p.inc_cc(p.g.collective_compute(
                "AllReduce", ALU.add, replica_groups=RG_ALL,
                ins=[gam_in], outs=[gam_out]), cc)
            p.wait(p.y, cc, 1)
            mg = p.dma(p.y, wld, out=grow[:, :], in_=gam_out)
            p.waitm(p.v, mg)
            p.v.tensor_scalar_mul(grow[0:1, 0:1], grow[0:1, 0:1], 1.0 / NFC)
            p.v.tensor_scalar_mul(grow[0:1, 1:2], grow[0:1, 1:2], 1.0 / NPR)
            p.waitm(p.t, p.mark("v"))
            i = ppool.next()
            pt = ppool.tiles[i]
            p.waitm(p.t, ppool.freed[i])
            p.inc(p.t.matmul(pt[:, 0:2], com.ones_row[:, :], grow[0:1, 0:2],
                             start=True, stop=True), ppool.wrote[i])
            p.wait(p.v, ppool.wrote[i], ppool.wrote[i].v)
            p.v.tensor_copy(gamcol[:, :], pt[:, 0:2])
            p.v.tensor_scalar_add(grow[0:1, 0:2], grow[0:1, 0:2], 1e-6)
            p.v.reciprocal(grow[0:1, 0:2], grow[0:1, 0:2])
            p.waitm(p.t, p.mark("v"))
            i = ppool.next()
            pt = ppool.tiles[i]
            p.waitm(p.t, ppool.freed[i])
            p.inc(p.t.matmul(pt[:, 0:2], com.ones_row[:, :], grow[0:1, 0:2],
                             start=True, stop=True), ppool.wrote[i])
            p.wait(p.v, ppool.wrote[i], ppool.wrote[i].v)
            p.v.tensor_copy(gcols[:, :], pt[:, 0:2])
            ppool.freed[i] = p.mark("v")

            mprev = None
            mtb = None
            for (w_full, t_bf, ntile, gi) in ((wfcT_f, tfc_bf, 16, 0),
                                              (wprT_f, tpr_bf, 64, 1)):
                K = w_full.shape[1]
                for t_ in range(ntile):
                    p.waitm(p.y, p.mark("v"))
                    p.waitm(p.y, p.mark("s"))
                    mw = p.dma(p.y, wld, out=pr["tA"][:, 0:K] if K == 2048
                               else pr["sliceA"][:, 0:K],
                               in_=w_full[t_ * 128:(t_ + 1) * 128, :])
                    src_t = pr["tA"][:, 0:K] if K == 2048 else \
                        pr["sliceA"][:, 0:K]
                    p.waitm(p.s, mw)
                    p.waitm(p.v, mw)
                    for j in range(K // 2048):
                        tile = src_t[:, j * 2048:(j + 1) * 2048]
                        p.s.activation(out=pr["tS"][:, :], in_=tile,
                                       func=AF.Sign)
                        p.s.activation(out=tile, in_=tile, func=AF.Abs)
                        p.waitm(p.v, p.mark("s"))
                        p.v.tensor_scalar(tile, tile, gcols[:, gi:gi + 1], 0.5,
                                          op0=ALU.mult, op1=ALU.is_gt)
                        p.waitm(p.v, mtb)
                        p.v.tensor_tensor(pr["tB"][:, :], pr["tS"][:, :],
                                          tile, ALU.mult)
                        p.waitm(p.y, p.mark("v"))
                        p.waitm(p.s, p.mark("v"))
                        mtb = p.dma(p.y, tst,
                                    out=t_bf[t_ * 128:(t_ + 1) * 128,
                                             j * 2048:(j + 1) * 2048]
                                    if K == 8192 else
                                    t_bf[t_ * 128:(t_ + 1) * 128, :],
                                    in_=pr["tB"][:, :])
            p.wait(p.y, tst)

        def emit_l0aa(p, x_in, x_out):
            x_fm = aa["x_fm"]; qk_sb = aa["qk_sb"]; v_fm = aa["v_fm"]
            outst = aa["outst"]; xres = aa["xres"]
            wt_sb = qk_sb
            o_fm = x_fm
            wstr = WStream2(aa["ws"], "aaws")
            wstr_l = WStream2(aa["wl"], "aawl")
            ccs = [ccsem() for _ in range(4)]
            stg = sem("aastg")
            sst = sem("aasst"); sld = sem("aasld")
            osf = [sem("aaosf0"), sem("aaosf1")]
            xrl = [sem("aaxrl0"), sem("aaxrl1")]

            ln_transpose_x(p, x_in, x_fm)

            for which, w_d in (("q", aa_wq), ("k", aa_wk)):
                off = 0 if which == "q" else TC * D
                for nb in range(4):
                    slot = nb % 2
                    wstr.load_plain(p, slot,
                                    w_d[nb].rearrange("p k n -> p (k n)"), p.g)
                    wstr.wait_loaded(p, p.t, slot)
                    last = None
                    for c in range(TC):
                        mats = [(x_fm[:, k * T + c * 128: k * T + (c + 1) * 128],
                                 wstr.bufs[slot][:, k * 512:(k + 1) * 512])
                                for k in range(DC)]
                        dst = qk_sb[:, off + c * D + nb * 512:
                                    off + c * D + (nb + 1) * 512]
                        scale = 0.125 if which == "q" else None

                        def evac(eng, pt, dst=dst, scale=scale):
                            if scale is None:
                                eng.tensor_copy(dst, pt[:, :])
                            else:
                                eng.tensor_scalar_mul(dst, pt[:, :], scale)

                        last = mm_group(p, ppool, mats, evac)
                    wstr.set_free(p, slot, last)
            m_qk = p.mark("v")

            p.waitm(p.t, m_qk)
            stage = fA
            for et in range(16):
                for fb in range(4):
                    mats = [(qk_sb[:, c * D + et * 128: c * D + (et + 1) * 128],
                             qk_sb[:, TC * D + c * D + fb * 512:
                                   TC * D + c * D + (fb + 1) * 512])
                            for c in range(TC)]
                    dst = stage[:, fb * 512:(fb + 1) * 512]

                    def evac(eng, pt, dst=dst):
                        eng.tensor_copy(dst, pt[:, :])

                    mm_group(p, ppool, mats, evac)
                p.waitm(p.y, p.mark("v"))
                mst = p.dma(p.y, sst, out=s_in[et * 128:(et + 1) * 128, :],
                            in_=stage[:, :])
                if et < 15:
                    p.waitm(p.v, mst)

            p.wait(p.g, sst)
            for j in range(4):
                p.inc_cc(p.g.collective_compute(
                    "AllReduce", ALU.add, replica_groups=RG_PAIRS,
                    ins=[s_in[j * 512:(j + 1) * 512, :]],
                    outs=[s_out[j * 512:(j + 1) * 512, :]]), ccs[j])

            for mb in range(16):
                slot = mb % 2
                wstr_l.load_plain(p, slot,
                                  aa_wv[mb].rearrange("p k n -> p (k n)"), p.g)
                wstr_l.wait_loaded(p, p.t, slot)
                last = None
                for sbk in range(2):
                    mats = [(wstr_l.bufs[slot][:, k * 128:(k + 1) * 128],
                             x_fm[:, k * T + sbk * 512: k * T + sbk * 512 + 512])
                            for k in range(DC)]
                    dst = v_fm[:, mb * T + sbk * 512: mb * T + sbk * 512 + 512]

                    def evac(eng, pt, dst=dst):
                        eng.tensor_copy(dst, pt[:, :])

                    last = mm_group(p, ppool, mats, evac)
                wstr_l.set_free(p, slot, last)
            m_vfm = p.mark("v")

            for et in range(16):
                j = et // 4
                p.wait(p.y, ccs[j], 1)
                p.waitm(p.y, p.mark("v"))
                msl = p.dma(p.y, sld, out=stage[:, :],
                            in_=s_out[et * 128:(et + 1) * 128, :])
                p.waitm(p.v, msl)
                p.v.reduce_max(scr["m"][:, :], stage[:, :], axis=AX.X,
                               negate=True)
                p.waitm(p.s, p.mark("v"))
                p.s.activation(out=stage[:, :], in_=stage[:, :], func=AF.Exp,
                               bias=scr["m"][:, :])
                p.waitm(p.v, p.mark("s"))
                p.v.reduce_sum(scr["q"][:, :], stage[:, :], axis=AX.X)
                p.v.reciprocal(scr["r"][:, :], scr["q"][:, :])
                wrow = xn_scr[:, :]
                p.waitm(p.v, (sems["tp0w"], sems["tp0w"].v))
                p.waitm(p.v, (sems["tp1w"], sems["tp1w"].v))
                p.v.tensor_scalar_mul(wrow, stage[:, :], scr["r"][:, :])
                p.waitm(p.t, p.mark("v"))
                for fc in range(16):
                    transpose_tile(
                        p, com, tpool,
                        wrow[:, fc * 128:(fc + 1) * 128],
                        wt_sb[:, fc * D + et * 128: fc * D + (et + 1) * 128],
                        evac_eng="s")

            p.waitm(p.t, p.mark("s"))
            p.waitm(p.t, m_vfm)
            for et in range(16):
                for sbk in range(2):
                    mats = [(wt_sb[:, fc * D + et * 128: fc * D + (et + 1) * 128],
                             v_fm[:, fc * T + sbk * 512: fc * T + sbk * 512 + 512])
                            for fc in range(16)]
                    dst = o_fm[:, et * T + sbk * 512: et * T + sbk * 512 + 512]

                    def evac(eng, pt, dst=dst):
                        eng.tensor_copy(dst, pt[:, :])

                    mm_group(p, ppool, mats, evac)
            m_ofm = p.mark("v")

            p.waitm(p.t, m_ofm)
            free_markers = [None, None]
            xres_free = [None, None]
            for nb in range(4):
                slot = nb % 2
                wstr.load_plain(p, slot,
                                aa_wo[nb].rearrange("p k n -> p (k n)"), p.g)
                wstr.wait_loaded(p, p.t, slot)
                last = None
                for c in range(TC):
                    mats = [(o_fm[:, k * T + c * 128: k * T + (c + 1) * 128],
                             wstr.bufs[slot][:, k * 512:(k + 1) * 512])
                            for k in range(DC)]
                    ob = (nb * TC + c) % 2
                    p.waitm(p.y, xres_free[ob])
                    mxr = p.dma(
                        p.y, xrl[ob], out=xres[ob][:, :],
                        in_=x_in[c * 128:(c + 1) * 128, nb * 512:(nb + 1) * 512])

                    def evac(eng, pt, ob=ob, mxr=mxr):
                        p.waitm(eng, free_markers[ob])
                        p.waitm(eng, mxr)
                        eng.tensor_add(outst[ob][:, :], xres[ob][:, :], pt[:, :])

                    last = mm_group(p, ppool, mats, evac)
                    xres_free[ob] = p.mark("v")
                    p.waitm(p.y, xres_free[ob])
                    free_markers[ob] = p.dma(
                        p.y, osf[ob],
                        out=x_out[c * 128:(c + 1) * 128, nb * 512:(nb + 1) * 512],
                        in_=outst[ob][:, :])
                wstr.set_free(p, slot, last)
            p.wait(p.y, osf[0])
            p.wait(p.y, osf[1])

        class WStream2:
            def __init__(self, bufs, name):
                self.bufs = bufs
                self.lsem = [sem(f"{name}{i}l") for i in range(len(bufs))]
                self.freed = [None] * len(bufs)

            def load_plain(self, p, slot, src, queue_eng, cols=None):
                p.waitm(queue_eng, self.freed[slot])
                dst = self.bufs[slot][:, :] if cols is None else \
                    self.bufs[slot][:, 0:cols]
                inst = queue_eng.dma_start(out=dst, in_=src)
                s_ = self.lsem[slot]
                s_.v += 16
                inst.then_inc(s_.h, 16)

            def load_t(self, p, slot, subs):
                p.waitm(p.y, self.freed[slot])
                s_ = self.lsem[slot]
                for (c0, c1, src) in subs:
                    inst = p.y.dma_start(out=self.bufs[slot][:, c0:c1],
                                         in_=src, transpose=True)
                    s_.v += 16
                    inst.then_inc(s_.h, 16)

            def load_re(self, p, slot, src, queue_eng, kt, ncols):
                p.waitm(queue_eng, self.freed[slot])
                dst = self.bufs[slot][:, 0:kt * ncols].rearrange(
                    "p (k n) -> p k n", k=kt)
                inst = queue_eng.dma_start(
                    out=dst, in_=src.rearrange("(k p) n -> p k n", p=128))
                s_ = self.lsem[slot]
                s_.v += 16
                inst.then_inc(s_.h, 16)

            def wait_loaded(self, p, eng, slot):
                s_ = self.lsem[slot]
                if s_.v > 0:
                    eng.wait_ge(s_.h, s_.v)

            def set_free(self, p, slot, marker):
                self.freed[slot] = marker

        def emit_l0ffn(p, x_in, x_out):
            yacc = ff["yacc"]; xh_fm = ff["xh_fm"]; fBig = ff["fBig"]
            hgrow = ff["hgrow"]; hst = ff["hst"]
            sumac = ff["sumac"]; sqac = ff["sqac"]; m2c = ff["m2c"]
            c1c = ff["c1c"]; c2c = ff["c2c"]
            xh2_fm = xh_fm
            wstr = WStream2(ff["ws"], "ffws")
            xld = sem("xld")
            hst_d = [sem("ffhst0"), sem("ffhst1")]
            hld = sem("ffhld"); osf = sem("ffosf")

            for c in range(TC):
                if c > 0:
                    p.waitm(p.y, p.mark("v"))
                    p.waitm(p.y, p.mark("s"))
                mx = p.dma(p.y, xld, out=fA[:, :],
                           in_=x_in[c * 128:(c + 1) * 128, :])
                p.waitm(p.v, mx)
                p.waitm(p.s, mx)
                p.v.tensor_copy(yacc[:, c * D:(c + 1) * D], fA[:, :])
                mk = emit_ln_chunk(p, com, fA[:, :], xn_scr[:, :], scr,
                                   width=D, quant=True)
                p.waitm(p.t, mk)
                for fc in range(DC):
                    transpose_tile(
                        p, com, tpool,
                        xn_scr[:, fc * 128:(fc + 1) * 128],
                        xh_fm[:, fc * T + c * 128: fc * T + (c + 1) * 128])
            m_xh = p.mark("v")

            p.waitm(p.t, m_xh)
            p.waitm(p.v, p.mark("s"))
            p.v.memset(sumac[:, :], 0.0)
            p.v.memset(sqac[:, :], 0.0)
            hfree = [None, None]
            for ob in range(16):
                slot = ob % 2
                wstr.load_re(p, slot, tfc_bf[:, ob * 512:(ob + 1) * 512],
                             p.y, 16, 512)
                wstr.wait_loaded(p, p.t, slot)
                last = None
                for c in range(TC):
                    mats = [(xh_fm[:, k * T + c * 128: k * T + (c + 1) * 128],
                             wstr.bufs[slot][:, k * 512:(k + 1) * 512])
                            for k in range(DC)]
                    hb = (ob * TC + c) % 2

                    def evac(eng, pt, hb=hb):
                        p.waitm(eng, hfree[hb])
                        eng.activation(out=hst[hb][:, :], in_=pt[:, :],
                                       func=AF.Gelu, scale=gamcol[:, 0:1])

                    last = mm_group(p, ppool, mats, evac, evac_eng="s")
                    mh = p.mark("s")
                    p.waitm(p.v, mh)
                    p.v.reduce_sum(scr["m"][:, :], hst[hb][:, :], axis=AX.X)
                    p.v.tensor_add(sumac[:, c:c + 1], sumac[:, c:c + 1],
                                   scr["m"][:, :])
                    p.waitm(p.s, p.mark("v"))
                    p.s.activation(out=fA[:, 0:512], in_=hst[hb][:, :],
                                   func=AF.Square)
                    p.waitm(p.v, p.mark("s"))
                    p.v.reduce_sum(scr["q"][:, :], fA[:, 0:512], axis=AX.X)
                    p.v.tensor_add(sqac[:, c:c + 1], sqac[:, c:c + 1],
                                   scr["q"][:, :])
                    p.waitm(p.y, mh)
                    hfree[hb] = p.dma(
                        p.y, hst_d[hb],
                        out=hg_d[c * 128:(c + 1) * 128,
                                 ob * 512:(ob + 1) * 512],
                        in_=hst[hb][:, :])
                wstr.set_free(p, slot, last)

            p.waitm(p.v, p.mark("s"))
            p.v.tensor_scalar_mul(m2c[:, :], sumac[:, :], 1.0 / 8192.0)
            p.v.tensor_scalar_mul(sqac[:, :], sqac[:, :], 1.0 / 8192.0)
            p.v.tensor_tensor(c1c[:, :], m2c[:, :], m2c[:, :], ALU.mult)
            p.v.tensor_sub(sqac[:, :], sqac[:, :], c1c[:, :])
            p.waitm(p.s, p.mark("v"))
            p.s.activation(out=sqac[:, :], in_=sqac[:, :], func=AF.Sqrt,
                           bias=com.cc(1e-6))
            p.waitm(p.v, p.mark("s"))
            p.v.reciprocal(sqac[:, :], sqac[:, :])
            p.wait(p.y, hst_d[0])
            p.wait(p.y, hst_d[1])
            for c in range(TC):
                if c > 0:
                    p.waitm(p.y, p.mark("v"))
                mh = p.dma(p.y, hld, out=hgrow[:, :],
                           in_=hg_d[c * 128:(c + 1) * 128, :])
                p.waitm(p.v, mh)
                p.v.tensor_scalar(fBig[:, :], hgrow[:, :], m2c[:, c:c + 1],
                                  None, op0=ALU.subtract)
                p.v.reduce_max(scr["m"][:, :], fBig[:, :], axis=AX.X,
                               apply_absolute_value=True)
                p.v.reciprocal(scr["q"][:, :], scr["m"][:, :])
                p.v.tensor_scalar_mul(c1c[:, c:c + 1], scr["q"][:, :], 128.0)
                p.v.tensor_tensor(scr["t1"][:, :], scr["m"][:, :],
                                  sqac[:, c:c + 1], ALU.mult)
                p.v.tensor_scalar_mul(c2c[:, c:c + 1], scr["t1"][:, :],
                                      1.0 / 128.0)

            for os_ in range(4):
                for c in range(TC):
                    p.waitm(p.y, p.mark("v"))
                    mh = p.dma(p.y, hld, out=hgrow[:, 0:2048],
                               in_=hg_d[c * 128:(c + 1) * 128,
                                        os_ * 2048:(os_ + 1) * 2048])
                    p.waitm(p.v, mh)
                    p.v.tensor_scalar(fBig[:, 0:2048], hgrow[:, 0:2048],
                                      m2c[:, c:c + 1], c1c[:, c:c + 1],
                                      op0=ALU.subtract, op1=ALU.mult)
                    p.v.tensor_scalar(fBig[:, 0:2048], fBig[:, 0:2048],
                                      -128.0, 128.0, op0=ALU.max, op1=ALU.min)
                    p.v.tensor_scalar_mul(xn_scr[:, :], fBig[:, 0:2048],
                                          c2c[:, c:c + 1])
                    p.waitm(p.t, p.mark("v"))
                    for oc in range(16):
                        transpose_tile(
                            p, com, tpool,
                            xn_scr[:, oc * 128:(oc + 1) * 128],
                            xh2_fm[:, oc * T + c * 128:
                                   oc * T + (c + 1) * 128])
                m_x2 = p.mark("v")
                p.waitm(p.t, m_x2)
                for db in range(4):
                    slot = db % 2
                    wstr.load_re(
                        p, slot,
                        tpr_bf[os_ * 2048:(os_ + 1) * 2048,
                               db * 512:(db + 1) * 512],
                        p.y, 16, 512)
                    wstr.wait_loaded(p, p.t, slot)
                    last = None
                    for c in range(TC):
                        mats = [(xh2_fm[:, oc * T + c * 128:
                                        oc * T + (c + 1) * 128],
                                 wstr.bufs[slot][:, oc * 512:(oc + 1) * 512])
                                for oc in range(16)]
                        dst = yacc[:, c * D + db * 512: c * D + (db + 1) * 512]

                        def evac(eng, pt, dst=dst):
                            eng.scalar_tensor_tensor(
                                dst, pt[:, :], gamcol[:, 1:2], dst,
                                op0=ALU.mult, op1=ALU.add)

                        last = mm_group(p, ppool, mats, evac)
                    wstr.set_free(p, slot, last)

            p.waitm(p.y, p.mark("v"))
            xo_r = x_out.rearrange("(c p) d -> p c d", p=128)
            p.dma(p.y, osf, out=xo_r,
                  in_=yacc[:, :].rearrange("p (c d) -> p c d", c=TC))
            p.wait(p.y, osf)

        def emit_l1ffn(p, x_in, x_out):
            yacc = lf["yacc"]; xn_fm = lf["xn_fm"]; gatedT = lf["gatedT"]
            sil = lf["sil"]; gated_st = lf["gated_st"]
            wsg = WStream2(lf["wsg"], "lfg")
            wsu = WStream2(lf["wsu"], "lfu")
            wsd = WStream2(lf["wsd"], "lfd")
            xld = sem("xld"); osf = sem("ffosf")
            csm = sem("pr_cst")
            if csm.v > 0:
                p.wait(p.y, csm)

            for c in range(TC):
                if c > 0:
                    p.waitm(p.y, p.mark("v"))
                    p.waitm(p.y, p.mark("s"))
                mx = p.dma(p.y, xld, out=fA[:, :],
                           in_=x_in[c * 128:(c + 1) * 128, :])
                p.waitm(p.v, mx)
                p.waitm(p.s, mx)
                p.v.tensor_copy(yacc[:, c * D:(c + 1) * D], fA[:, :])
                mk = emit_ln_chunk(p, com, fA[:, :], xn_scr[:, :], scr,
                                   width=D, quant=False)
                p.waitm(p.t, mk)
                for fc in range(DC):
                    transpose_tile(
                        p, com, tpool,
                        xn_scr[:, fc * 128:(fc + 1) * 128],
                        xn_fm[:, fc * T + c * 128: fc * T + (c + 1) * 128])
            p.waitm(p.t, p.mark("v"))

            for hs in range(4):
                for hb in range(4):
                    ob = hs * 4 + hb
                    wsg.load_re(p, 0, wga_bf[:, ob * 512:(ob + 1) * 512],
                                p.y, 16, 512)
                    wsu.load_re(p, 0, wup_bf[:, ob * 512:(ob + 1) * 512],
                                p.s, 16, 512)
                    wsg.wait_loaded(p, p.t, 0)
                    wsu.wait_loaded(p, p.t, 0)
                    lg = lu = None
                    for c in range(TC):
                        mats_g = [(xn_fm[:, k * T + c * 128:
                                         k * T + (c + 1) * 128],
                                   wsg.bufs[0][:, k * 512:(k + 1) * 512])
                                  for k in range(DC)]
                        mats_u = [(xn_fm[:, k * T + c * 128:
                                         k * T + (c + 1) * 128],
                                   wsu.bufs[0][:, k * 512:(k + 1) * 512])
                                  for k in range(DC)]

                        def evac_g(eng, pt):
                            p.waitm(eng, p.mark("v"))
                            eng.activation(out=sil[:, :], in_=pt[:, :],
                                           func=AF.Silu)

                        lg = mm_group(p, ppool, mats_g, evac_g, evac_eng="s")

                        def evac_u(eng, pt):
                            p.waitm(eng, p.mark("s"))
                            eng.tensor_tensor(gated_st[:, :], sil[:, :],
                                              pt[:, :], ALU.mult)

                        lu = mm_group(p, ppool, mats_u, evac_u)
                        p.waitm(p.t, p.mark("v"))
                        for j in range(4):
                            hc = hb * 4 + j
                            transpose_tile(
                                p, com, tpool,
                                gated_st[:, j * 128:(j + 1) * 128],
                                gatedT[:, hc * T + c * 128:
                                       hc * T + (c + 1) * 128])
                    wsg.set_free(p, 0, lg)
                    wsu.set_free(p, 0, lu)
                p.waitm(p.t, p.mark("v"))
                for db in range(4):
                    wsd.load_re(
                        p, 0,
                        wdo_bf[hs * 2048:(hs + 1) * 2048,
                               db * 512:(db + 1) * 512],
                        p.y, 16, 512)
                    wsd.wait_loaded(p, p.t, 0)
                    last = None
                    for c in range(TC):
                        mats = [(gatedT[:, oc * T + c * 128:
                                        oc * T + (c + 1) * 128],
                                 wsd.bufs[0][:, oc * 512:(oc + 1) * 512])
                                for oc in range(16)]
                        dst = yacc[:, c * D + db * 512: c * D + (db + 1) * 512]

                        def evac(eng, pt, dst=dst):
                            eng.tensor_add(dst, dst, pt[:, :])

                        last = mm_group(p, ppool, mats, evac)
                    wsd.set_free(p, 0, last)

            p.waitm(p.y, p.mark("v"))
            xo_r = x_out.rearrange("(c p) d -> p c d", p=128)
            p.dma(p.y, osf, out=xo_r,
                  in_=yacc[:, :].rearrange("p (c d) -> p c d", c=TC))
            p.wait(p.y, osf)

        def emit_l0ab(p, x_in, x_out):
            u_fm = ab["u_fm"]; zst = ab["zst"]
            dt_raw = ab["dt_raw"]; dt_tok = ab["dt_tok"]; adt_tok = ab["dt_raw"]
            ws_cv = ab["ws_cv"]; convf = ab["convf"]
            bnd_my = ab["bnd_my"]; bnd_eff = ab["bnd_eff"]
            Bc_fm = ab["Bc_fm"]; C_fm = ab["C_fm"]; B_tok = ab["B_tok"]
            acum_sc = ab["acum_sc"]; sd_sc = ab["sd_sc"]
            nacum_tok = ab["nacum_tok"]; sdo_tok = ab["sdo_tok"]
            decay_tok = ab["decay_tok"]; pisdo_tok = ab["pisdo_tok"]
            pisdo_all = ab["pisdo_all"]
            GT_sb = ab["GT_sb"]; M_bf = ab["M_bf"]; rb_sb = ab["rb_sb"]
            xload = ab["xload"]; X_bf = ab["X_bf"]; xs_st = ab["xs_st"]
            xd_st = ab["xd_st"]; bd_st = ab["bd_st"]; yst = ab["yst"]
            arow = ab["arow"]; S_run = ab["S_run"]; S_bf = ab["S_bf"]
            seed_bf = ab["seed_bf"]; cd64 = ab["cd64"]; crow = ab["crow"]
            cd_row = ab["cd_row"]; pi_row = ab["pi_row"]; pif_row = ab["pif_row"]
            dtb_bc = ab["dtb_bc"]; A_bc = ab["A_bc"]; D_bc = ab["D_bc"]
            pibc = ab["pibc"]; wdt_sb = ab["wdt_sb"]
            yrow = ab["yrow"]; zrow = ab["zrow"]; bfx = ab["bfx"]
            gnT_c = ab["gnT_c"]; yacc_c = ab["yacc_c"]
            wstr = WStream2(ab["ws"], "abws")
            wstx = WStream2(ab["wx"], "abwx")
            wsto = WStream2(ab["wo"], "abwo")
            xld = sem("xld")
            zsf = [sem("abzsf0"), sem("abzsf1")]
            bndm = sem("abbndm"); bndl = sem("abbndl")
            cvl = sem("abcvl"); xps = sem("abxps")
            acs = sem("abacs")
            arl = [sem("abarl0"), sem("abarl1")]
            xll = [sem("abxll0"), sem("abxll1")]
            ysf = [sem("abysf0"), sem("abysf1")]
            stx = sem("abstx"); stl = sem("abstl")
            yrl = sem("abyrl"); zrl = sem("abzrl")
            osf = sem("abosf"); xrl2 = sem("abxrl2")
            wdt = sem("abwdt"); cfl = sem("abcfl"); wofr = sem("abwofr")

            def bcast_row(src_row, ncol, dst, nrow=128):
                i = ppool.next()
                pt = ppool.tiles[i]
                p.waitm(p.t, ppool.freed[i])
                p.inc(p.t.matmul(pt[0:nrow, 0:ncol], com.ones_row[:, 0:nrow],
                                 src_row, start=True, stop=True),
                      ppool.wrote[i])
                p.wait(p.v, ppool.wrote[i], ppool.wrote[i].v)
                p.v.tensor_copy(dst, pt[0:nrow, 0:ncol])
                ppool.freed[i] = p.mark("v")

            cl2 = sem("abcl2")
            mr = p.dma(p.y, cl2, out=rowbuf[:, :], in_=dtb_d)
            mc = p.dma(p.y, cl2, out=crow[:, :], in_=alog_d)
            p.waitm(p.v, mc)
            p.waitm(p.s, mc)
            p.waitm(p.t, mc)
            bcast_row(rowbuf[:, :], 32, dtb_bc[:, :])
            bcast_row(crow[:, :], 32, A_bc[:, :])
            p.waitm(p.s, p.mark("v"))
            p.s.activation(out=A_bc[:, :], in_=A_bc[:, :], func=AF.Exp)
            p.waitm(p.v, p.mark("s"))
            p.v.tensor_scalar_mul(A_bc[:, :], A_bc[:, :], -1.0)
            for w_ in ppool.wrote:
                if w_.v > 0:
                    p.wait(p.y, w_, w_.v)
            mcd = p.dma(p.y, cl2, out=rowbuf[:, :], in_=dco_d)
            p.waitm(p.v, mcd)
            p.waitm(p.t, mcd)
            bcast_row(rowbuf[:, :], 32, D_bc[:, :])
            p.v.memset(pi_row[:, :], 1.0)
            p.v.memset(S_run[:, :], 0.0)

            ln_transpose_x(p, x_in, u_fm)

            zfree = [None, None]
            for nb in range(8):
                wstr.load_re(p, 0, winT_f[:, nb * 512:(nb + 1) * 512],
                             p.g, 16, 512)
                wstr.wait_loaded(p, p.t, 0)
                last = None
                for c in range(TC):
                    mats = [(u_fm[:, k * T + c * 128: k * T + (c + 1) * 128],
                             wstr.bufs[0][:, k * 512:(k + 1) * 512])
                            for k in range(DC)]
                    zb = (nb * TC + c) % 2

                    def evac(eng, pt, zb=zb):
                        p.waitm(eng, zfree[zb])
                        eng.tensor_copy(zst[zb][:, :], pt[:, :])

                    last = mm_group(p, ppool, mats, evac)
                    p.waitm(p.y, p.mark("v"))
                    zfree[zb] = p.dma(
                        p.y, zsf[zb],
                        out=z_dram[c * 128:(c + 1) * 128,
                                   nb * 512:(nb + 1) * 512],
                        in_=zst[zb][:, :])
                wstr.set_free(p, 0, last)

            inst = p.g.dma_start(
                out=wdt_sb[:, :].rearrange("p (k n) -> p k n", k=16),
                in_=winT_f[:, 8320:8352].rearrange("(k p) n -> p k n", p=128))
            wdt.v += 16
            inst.then_inc(wdt.h, 16)
            p.wait(p.t, wdt)
            for c in range(TC):
                mats = [(u_fm[:, k * T + c * 128: k * T + (c + 1) * 128],
                         wdt_sb[:, k * 32:(k + 1) * 32])
                        for k in range(DC)]
                dst = dt_raw[:, c * 32:(c + 1) * 32]

                def evac(eng, pt, dst=dst):
                    eng.tensor_copy(dst, pt[:, 0:32])

                mm_group(p, ppool, mats, evac, ncols=32)

            for mb in range(33):
                slot = mb % 2
                wstx.load_re(
                    p, slot,
                    winT_f[:, 4096 + mb * 128:4096 + (mb + 1) * 128],
                    p.g, 16, 128)
                wstx.wait_loaded(p, p.t, slot)
                last = None
                for sbk in range(2):
                    mats = [(wstx.bufs[slot][:, k * 128:(k + 1) * 128],
                             u_fm[:, k * T + sbk * 512: k * T + sbk * 512 + 512])
                            for k in range(DC)]
                    zb = sbk % 2

                    def evac(eng, pt, zb=zb, sbk=sbk, mb=mb):
                        p.waitm(eng, zfree[zb])
                        eng.tensor_copy(zst[zb][:, :], pt[:, :])
                        if sbk == 1:
                            eng.tensor_copy(bnd_my[:, mb * 3:(mb + 1) * 3],
                                            zst[zb][:, 509:512])

                    last = mm_group(p, ppool, mats, evac)
                    p.waitm(p.y, p.mark("v"))
                    zfree[zb] = p.dma(
                        p.y, zsf[zb],
                        out=xbc_dram[mb, :, sbk * 512: sbk * 512 + 512],
                        in_=zst[zb][:, :])
                wstx.set_free(p, slot, last)

            p.v.tensor_scalar_mul(bnd_my[:, :], bnd_my[:, :], flags[:, 0:1])
            p.waitm(p.y, p.mark("v"))
            p.dma(p.y, bndm, out=bnd_in, in_=bnd_my[:, :])
            p.wait(p.g, bndm)
            ccb = ccsem()
            p.inc_cc(p.g.collective_compute(
                "AllReduce", ALU.add, replica_groups=RG_PAIRS,
                ins=[bnd_in], outs=[bnd_out]), ccb)
            p.wait(p.y, ccb, 1)
            mb_ = p.dma(p.y, bndl, out=bnd_eff[:, :], in_=bnd_out)
            p.waitm(p.v, mb_)
            p.v.tensor_scalar_mul(bnd_eff[:, :], bnd_eff[:, :], flags[:, 1:2])

            for c in range(TC):
                sl = slice(c * 32, (c + 1) * 32)
                p.v.tensor_tensor(dt_raw[:, sl], dt_raw[:, sl], dtb_bc[:, :],
                                  ALU.add)
            p.waitm(p.s, p.mark("v"))
            p.s.activation(out=fA[:, 0:256], in_=dt_raw[:, :], func=AF.Abs)
            p.waitm(p.v, p.mark("s"))
            p.v.tensor_scalar_mul(fA[:, 256:512], fA[:, 0:256], -1.0)
            p.waitm(p.s, p.mark("v"))
            p.s.activation(out=fA[:, 0:256], in_=fA[:, 256:512], func=AF.Exp)
            p.s.activation(out=fA[:, 256:512], in_=fA[:, 0:256], func=AF.Ln,
                           bias=1.0)
            p.waitm(p.v, p.mark("s"))
            p.v.tensor_scalar_max(dt_tok[:, :], dt_raw[:, :], 0.0)
            p.v.tensor_tensor(dt_tok[:, :], dt_tok[:, :], fA[:, 256:512],
                              ALU.add)
            for c in range(TC):
                sl = slice(c * 32, (c + 1) * 32)
                p.v.tensor_tensor(adt_tok[:, sl], dt_tok[:, sl], A_bc[:, :],
                                  ALU.mult)

            p.waitm(p.v, p.mark("s"))
            p.wait(p.y, zsf[0])
            p.wait(p.y, zsf[1])
            cfree = None
            for mb in range(33):
                p.waitm(p.y, p.mark("v"))
                mw = p.dma(p.y, cvl, out=ws_cv[:, 3:1027],
                           in_=xbc_dram[mb, :, :])
                p.waitm(p.v, mw)
                p.v.tensor_copy(ws_cv[:, 0:3], bnd_eff[:, mb * 3:(mb + 1) * 3])
                p.waitm(p.v, cfree)
                p.v.tensor_scalar_mul(convf[:, :], ws_cv[:, 0:1024],
                                      wconv[:, mb * 4: mb * 4 + 1])
                for j in range(1, 4):
                    p.v.scalar_tensor_tensor(
                        convf[:, :], ws_cv[:, j:j + 1024],
                        wconv[:, mb * 4 + j: mb * 4 + j + 1], convf[:, :],
                        op0=ALU.mult, op1=ALU.add)
                p.waitm(p.s, p.mark("v"))
                p.s.activation(out=convf[:, :], in_=convf[:, :], func=AF.Silu,
                               bias=bconv[:, mb:mb + 1])
                msl = p.mark("s")
                if mb < 32:
                    p.waitm(p.y, msl)
                    cfree = p.dma(p.y, xps, out=xpost_dram[mb, :, :],
                                  in_=convf[:, :])
                else:
                    p.waitm(p.v, msl)
                    p.v.tensor_copy(Bc_fm[:, :], convf[:, :])
                    p.waitm(p.y, msl)
                    p.dma(p.y, xps, out=cbc_dram, in_=convf[:, :])
                    p.wait(p.g, xps)
                    mcf = p.dma(p.g, cfl, out=C_fm[:, :],
                                in_=cbc_dram[64:128, :])
                    p.waitm(p.t, mcf)
                    p.waitm(p.v, mcf)
                    cfree = None
            p.waitm(p.t, p.mark("v"))
            for c in range(TC):
                i2 = tpool.next()
                pt2 = tpool.tiles[i2]
                w2 = tpool.wrote[i2]
                p.waitm(p.t, tpool.freed[i2])
                p.inc(p.t.transpose(pt2[:, 0:64],
                                    Bc_fm[0:64, c * 128:(c + 1) * 128],
                                    com.ident[0:64, 0:64]), w2)
                p.wait(p.v, w2, w2.v)
                p.v.tensor_copy(B_tok[:, c * 64:(c + 1) * 64], pt2[:, 0:64])
                tpool.freed[i2] = p.mark("v")

            ylast = [None, None]
            for sc in range(4):
                p.waitm(p.t, p.mark("v"))
                i2 = tpoolf.next()
                pt2 = tpoolf.tiles[i2]
                w2 = tpoolf.wrote[i2]
                p.waitm(p.t, tpoolf.freed[i2])
                for j in range(2):
                    c = sc * 2 + j
                    inst = p.t.matmul(pt2[0:32, 0:256],
                                      adt_tok[:, c * 32:(c + 1) * 32],
                                      ucum[:, j * 256:(j + 1) * 256],
                                      start=(j == 0), stop=(j == 1))
                    if j == 1:
                        p.inc(inst, w2)
                p.wait(p.v, w2, w2.v)
                p.v.tensor_copy(acum_sc[:, :], pt2[0:32, 0:256])
                tpoolf.freed[i2] = p.mark("v")
                p.waitm(p.y, p.mark("v"))
                p.dma(p.y, acs, out=acum_dram[sc], in_=acum_sc[:, :])
                p.waitm(p.s, p.mark("v"))
                p.s.activation(out=sd_sc[:, :], in_=acum_sc[:, :], func=AF.Exp)
                msd = p.mark("s")
                for j in range(2):
                    i2 = tpoolf.next()
                    pt2 = tpoolf.tiles[i2]
                    w2 = tpoolf.wrote[i2]
                    p.waitm(p.t, tpoolf.freed[i2])
                    p.waitm(p.t, msd)
                    p.inc(p.t.transpose(pt2[:, 0:32],
                                        sd_sc[0:32, j * 128:(j + 1) * 128],
                                        com.identf[0:32, 0:32]), w2)
                    p.wait(p.v, w2, w2.v)
                    p.v.tensor_copy(sdo_tok[:, j * 32:(j + 1) * 32],
                                    pt2[:, 0:32])
                    tpoolf.freed[i2] = p.mark("v")
                p.waitm(p.s, p.mark("v"))
                p.s.activation(out=sd_sc[:, :], in_=acum_sc[:, :], func=AF.Exp,
                               bias=acum_sc[:, 255:256],
                               scale=com.cc(-1.0)[0:32, :])
                msd = p.mark("s")
                for j in range(2):
                    i2 = tpoolf.next()
                    pt2 = tpoolf.tiles[i2]
                    w2 = tpoolf.wrote[i2]
                    p.waitm(p.t, tpoolf.freed[i2])
                    p.waitm(p.t, msd)
                    p.inc(p.t.transpose(pt2[:, 0:32],
                                        sd_sc[0:32, j * 128:(j + 1) * 128],
                                        com.identf[0:32, 0:32]), w2)
                    p.wait(p.v, w2, w2.v)
                    p.v.tensor_copy(decay_tok[:, j * 32:(j + 1) * 32],
                                    pt2[:, 0:32])
                    tpoolf.freed[i2] = p.mark("v")
                for j in range(2):
                    i2 = tpoolf.next()
                    pt2 = tpoolf.tiles[i2]
                    w2 = tpoolf.wrote[i2]
                    p.waitm(p.t, tpoolf.freed[i2])
                    p.inc(p.t.transpose(pt2[:, 0:32],
                                        acum_sc[0:32, j * 128:(j + 1) * 128],
                                        com.identf[0:32, 0:32]), w2)
                    p.wait(p.v, w2, w2.v)
                    p.v.tensor_scalar_mul(nacum_tok[:, j * 32:(j + 1) * 32],
                                          pt2[:, 0:32], -1.0)
                    tpoolf.freed[i2] = p.mark("v")
                i2 = tpoolf.next()
                pt2 = tpoolf.tiles[i2]
                w2 = tpoolf.wrote[i2]
                p.waitm(p.t, tpoolf.freed[i2])
                p.inc(p.t.transpose(pt2[0:1, 0:32], acum_sc[0:32, 255:256],
                                    com.identf[0:32, 0:32]), w2)
                p.wait(p.s, w2, w2.v)
                p.s.activation(out=cd_row[:, :], in_=pt2[0:1, 0:32],
                               func=AF.Exp)
                tpoolf.freed[i2] = p.mark("s")
                p.waitm(p.v, p.mark("s"))
                p.v.tensor_scalar_mul(pif_row[:, :], pi_row[:, :],
                                      flags[0:1, 1:2])
                p.waitm(p.t, p.mark("v"))
                bcast_row(pif_row[:, :], 32, pibc[:, :])
                for j in range(2):
                    c = sc * 2 + j
                    p.v.tensor_tensor(pisdo_tok[:, j * 32:(j + 1) * 32],
                                      sdo_tok[:, j * 32:(j + 1) * 32],
                                      pibc[:, :], ALU.mult)
                    p.v.tensor_copy(pisdo_all[:, c * 32:(c + 1) * 32],
                                    pisdo_tok[:, j * 32:(j + 1) * 32])
                p.v.tensor_tensor(pi_row[:, :], pi_row[:, :], cd_row[:, :],
                                  ALU.mult)
                bcast_row(cd_row[:, :], 32, cd64[:, :], nrow=64)
                for j in range(2):
                    c = sc * 2 + j

                    def evg(eng, pt3, j=j):
                        eng.tensor_tensor(GT_sb[:, j * 256:(j + 1) * 256],
                                          pt3[:, 0:256],
                                          trim[:, j * 256:(j + 1) * 256],
                                          ALU.mult)

                    mm_group(p, ppool,
                             [(Bc_fm[0:64, c * 128:(c + 1) * 128],
                               C_fm[:, sc * 256:(sc + 1) * 256])],
                             evg, ncols=256)

                for h in range(32):
                    hcol = h * 128
                    ab_ = h % 2
                    p.wait(p.y, acs)
                    if h < 2:
                        p.wait(p.y, xps)
                    mar = p.dma(p.y, arl[ab_], out=arow[ab_][:, :],
                                in_=acum_dram[sc, h:h + 1, :])
                    mxl = p.dma(p.y, xll[ab_], out=xload[ab_][:, :],
                                in_=xpost_dram[h, :, sc * 256:(sc + 1) * 256])
                    p.waitm(p.t, mar)
                    i3 = ppool.next()
                    pt3 = ppool.tiles[i3]
                    p.waitm(p.t, ppool.freed[i3])
                    p.inc(p.t.matmul(pt3[:, 0:256], com.ones_row[:, :],
                                     arow[ab_][:, :], start=True, stop=True),
                          ppool.wrote[i3])
                    p.wait(p.v, ppool.wrote[i3], ppool.wrote[i3].v)
                    p.v.tensor_copy(rb_sb[:, :], pt3[:, 0:256])
                    ppool.freed[i3] = p.mark("v")
                    p.waitm(p.t, mxl)
                    for j in range(2):
                        i2 = tpoolf.next()
                        pt2 = tpoolf.tiles[i2]
                        w2 = tpoolf.wrote[i2]
                        p.waitm(p.t, tpoolf.freed[i2])
                        p.inc(p.t.transpose(pt2[:, 0:128],
                                            xload[ab_][:, j * 128:(j + 1) * 128],
                                            com.identf[:, :]), w2)
                        p.wait(p.v, w2, w2.v)
                        p.v.tensor_copy(X_bf[:, j * 128:(j + 1) * 128],
                                        pt2[:, 0:128])
                        tpoolf.freed[i2] = p.mark("v")
                    for j in range(2):
                        c = sc * 2 + j
                        p.v.tensor_scalar_mul(
                            xs_st[:, j * 128:(j + 1) * 128],
                            X_bf[:, j * 128:(j + 1) * 128],
                            dt_tok[:, c * 32 + h: c * 32 + h + 1])
                        p.v.tensor_scalar_mul(
                            xd_st[:, j * 128:(j + 1) * 128],
                            X_bf[:, j * 128:(j + 1) * 128],
                            D_bc[:, h:h + 1])
                    for j in range(2):
                        p.v.tensor_scalar(fA[:, 0:256], rb_sb[:, :],
                                          nacum_tok[:, j * 32 + h:
                                                    j * 32 + h + 1],
                                          0.0, op0=ALU.add, op1=ALU.min)
                        p.waitm(p.s, p.mark("v"))
                        p.s.activation(out=fA[:, 256:512], in_=fA[:, 0:256],
                                       func=AF.Exp)
                        p.waitm(p.v, p.mark("s"))
                        p.v.tensor_tensor(M_bf[:, j * 256:(j + 1) * 256],
                                          fA[:, 256:512],
                                          GT_sb[:, j * 256:(j + 1) * 256],
                                          ALU.mult)
                    p.v.tensor_copy(S_bf[:, :], S_run[:, hcol:hcol + 128])
                    m_pre = p.mark("v")
                    p.waitm(p.t, m_pre)
                    for lc in range(2):
                        c = sc * 2 + lc
                        id_ = ppool.next()
                        ptd = ppool.tiles[id_]
                        wd_ = ppool.wrote[id_]
                        p.waitm(p.t, ppool.freed[id_])
                        for j in range(2):
                            inst = p.t.matmul(
                                ptd[:, 0:128],
                                M_bf[:, j * 256 + lc * 128:
                                     j * 256 + lc * 128 + 128],
                                xs_st[:, j * 128:(j + 1) * 128],
                                start=(j == 0), stop=(j == 1))
                            if j == 1:
                                p.inc(inst, wd_)
                        io_ = ppool.next()
                        pto = ppool.tiles[io_]
                        wo_ = ppool.wrote[io_]
                        p.waitm(p.t, ppool.freed[io_])
                        p.inc(p.t.matmul(
                            pto[:, 0:128],
                            C_fm[:, c * 128:(c + 1) * 128],
                            S_bf[:, :], start=True, stop=True), wo_)
                        yb = lc % 2
                        p.wait(p.v, wo_, wo_.v)
                        p.wait(p.v, wd_, wd_.v)
                        p.waitm(p.v, ylast[yb])
                        p.v.scalar_tensor_tensor(
                            yst[yb][:, :], pto[:, 0:128],
                            sdo_tok[:, lc * 32 + h: lc * 32 + h + 1],
                            xd_st[:, lc * 128:(lc + 1) * 128],
                            op0=ALU.mult, op1=ALU.add)
                        p.v.tensor_tensor(yst[yb][:, :], yst[yb][:, :],
                                          ptd[:, 0:128], ALU.add)
                        ppool.freed[id_] = p.mark("v")
                        ppool.freed[io_] = ppool.freed[id_]
                        p.waitm(p.y, ppool.freed[id_])
                        ylast[yb] = p.dma(
                            p.y, ysf[yb],
                            out=y_dram[c * 128:(c + 1) * 128, hcol:hcol + 128],
                            in_=yst[yb][:, :])
                    for j in range(2):
                        c = sc * 2 + j
                        p.v.tensor_scalar_mul(
                            bd_st[:, j * 64:(j + 1) * 64],
                            B_tok[:, c * 64:(c + 1) * 64],
                            decay_tok[:, j * 32 + h: j * 32 + h + 1])
                    m_bd = p.mark("v")
                    p.waitm(p.t, m_bd)
                    iu = ppool.next()
                    ptu = ppool.tiles[iu]
                    wu_ = ppool.wrote[iu]
                    p.waitm(p.t, ppool.freed[iu])
                    for j in range(2):
                        inst = p.t.matmul(
                            ptu[0:64, 0:128], bd_st[:, j * 64:(j + 1) * 64],
                            xs_st[:, j * 128:(j + 1) * 128],
                            start=(j == 0), stop=(j == 1))
                        if j == 1:
                            p.inc(inst, wu_)
                    p.wait(p.v, wu_, wu_.v)
                    p.v.scalar_tensor_tensor(
                        S_run[:, hcol:hcol + 128], S_run[:, hcol:hcol + 128],
                        cd64[:, h:h + 1], ptu[0:64, 0:128],
                        op0=ALU.mult, op1=ALU.add)
                    ppool.freed[iu] = p.mark("v")

            p.v.tensor_scalar_mul(S_run[:, :], S_run[:, :], flags[0:64, 0:1])
            p.waitm(p.y, p.mark("v"))
            p.dma(p.y, stx, out=st_in, in_=S_run[:, :])
            p.wait(p.g, stx)
            ccst = ccsem()
            p.inc_cc(p.g.collective_compute(
                "AllReduce", ALU.add, replica_groups=RG_PAIRS,
                ins=[st_in], outs=[st_out]), ccst)
            p.wait(p.y, ccst, 1)
            msl2 = p.dma(p.y, stl, out=S_run[:, :], in_=st_out)
            p.waitm(p.v, msl2)
            p.v.tensor_copy(seed_bf[:, :], S_run[:, :])
            m_seed = p.mark("v")
            p.waitm(p.t, m_seed)

            p.wait(p.y, ysf[0])
            p.wait(p.y, ysf[1])
            csm = sem("pr_cst")
            if csm.v > 0:
                p.wait(p.y, csm)
            for c in range(TC):
                sc = c // 2
                p.waitm(p.y, p.mark("v"))
                myr = p.dma(p.y, yrl, out=yrow[:, :],
                            in_=y_dram[c * 128:(c + 1) * 128, :])
                mzr = p.dma(p.y, zrl, out=zrow[:, :],
                            in_=z_dram[c * 128:(c + 1) * 128, :])
                p.waitm(p.v, myr)
                for h in range(32):
                    ic = ppool.next()
                    ptc = ppool.tiles[ic]
                    wc_ = ppool.wrote[ic]
                    p.waitm(p.t, ppool.freed[ic])
                    p.inc(p.t.matmul(ptc[:, 0:128],
                                     C_fm[:, c * 128:(c + 1) * 128],
                                     seed_bf[:, h * 128:(h + 1) * 128],
                                     start=True, stop=True), wc_)
                    p.wait(p.v, wc_, wc_.v)
                    p.v.scalar_tensor_tensor(
                        yrow[:, h * 128:(h + 1) * 128], ptc[:, 0:128],
                        pisdo_all[:, c * 32 + h: c * 32 + h + 1],
                        yrow[:, h * 128:(h + 1) * 128],
                        op0=ALU.mult, op1=ALU.add)
                    ppool.freed[ic] = p.mark("v")
                p.waitm(p.s, mzr)
                p.waitm(p.s, p.mark("v"))
                p.s.activation(out=bfx[:, :], in_=zrow[:, :], func=AF.Silu)
                p.waitm(p.v, p.mark("s"))
                p.v.tensor_tensor(yrow[:, :], yrow[:, :], bfx[:, :], ALU.mult)
                p.waitm(p.s, p.mark("v"))
                p.s.activation(out=zrow[:, :], in_=yrow[:, :], func=AF.Square)
                p.waitm(p.v, p.mark("s"))
                p.v.reduce_sum(scr["m"][:, :], zrow[:, :], axis=AX.X)
                p.v.tensor_scalar_mul(scr["m"][:, :], scr["m"][:, :],
                                      1.0 / 4096.0)
                p.waitm(p.s, p.mark("v"))
                p.s.activation(out=scr["t1"][:, :], in_=scr["m"][:, :],
                               func=AF.Sqrt, bias=com.cc(1e-5))
                p.waitm(p.v, p.mark("s"))
                p.v.reciprocal(scr["r"][:, :], scr["t1"][:, :])
                p.v.tensor_scalar_mul(bfx[:, :], yrow[:, :], scr["r"][:, :])
                p.waitm(p.t, p.mark("v"))
                for fc in range(32):
                    i2 = tpool.next()
                    pt2 = tpool.tiles[i2]
                    w2 = tpool.wrote[i2]
                    p.waitm(p.t, tpool.freed[i2])
                    p.inc(p.t.transpose(pt2[:, 0:128],
                                        bfx[:, fc * 128:(fc + 1) * 128],
                                        com.ident[:, :]), w2)
                    p.wait(p.v, w2, w2.v)
                    p.v.tensor_scalar_mul(gnT_c[:, fc * 128:(fc + 1) * 128],
                                          pt2[:, 0:128],
                                          normw[:, fc:fc + 1])
                    tpool.freed[i2] = p.mark("v")
                if c > 0:
                    p.wait(p.y, osf)
                mxr2 = p.dma(p.y, xrl2, out=yacc_c[:, :],
                             in_=x_in[c * 128:(c + 1) * 128, :])
                p.waitm(p.v, mxr2)
                p.waitm(p.t, p.mark("v"))
                for db in range(4):
                    ib = ppool.next()
                    ptb = ppool.tiles[ib]
                    wb_ = ppool.wrote[ib]
                    p.waitm(p.t, ppool.freed[ib])
                    for os_ in range(4):
                        wsto.load_re(
                            p, 0,
                            wou_bf[os_ * 1024:(os_ + 1) * 1024,
                                   db * 512:(db + 1) * 512],
                            p.y, 8, 512)
                        wsto.wait_loaded(p, p.t, 0)
                        for k8 in range(8):
                            fc = os_ * 8 + k8
                            lastmm = (os_ == 3 and k8 == 7)
                            inst = p.t.matmul(
                                ptb[:, :],
                                gnT_c[:, fc * 128:(fc + 1) * 128],
                                wsto.bufs[0][:, k8 * 512:(k8 + 1) * 512],
                                start=(os_ == 0 and k8 == 0), stop=lastmm)
                            if lastmm:
                                p.inc(inst, wb_)
                            elif k8 == 7:
                                p.inc(inst, wofr)
                        if os_ < 3:
                            wsto.set_free(p, 0, (wofr, wofr.v))
                    wsto.set_free(p, 0, (wb_, wb_.v))
                    p.wait(p.v, wb_, wb_.v)
                    p.v.tensor_add(yacc_c[:, db * 512:(db + 1) * 512],
                                   yacc_c[:, db * 512:(db + 1) * 512],
                                   ptb[:, :])
                    ppool.freed[ib] = p.mark("v")
                p.waitm(p.y, p.mark("v"))
                p.dma(p.y, osf, out=x_out[c * 128:(c + 1) * 128, :],
                      in_=yacc_c[:, :])
            p.wait(p.y, osf)

        def emit_l1aa(p, x_in, x_out):
            xn_fm = la["xn_fm"]; qkv_h = la["qkv_h"]; o_tok = la["o_tok"]
            qs_fm = la["qs_fm"]; kv_run = la["kv_run"]
            qf = la["qf"]; kf = la["kf"]; qs_st = la["qs_st"]
            ks_st = la["ks_st"]; kvb = la["kvb"]; M_sb = la["M_sb"]
            ddTh = la["ddTh"]; sil = la["sil"]
            outst = la["outst"]; xres = la["xres"]
            negS = la["negS"]; qd_all = la["qd_all"]; kd_all = la["kd_all"]
            bd_all = la["bd_all"]; bdjs = la["bdjs"]; rstdc = la["rstdc"]
            kvs_bf = xn_scr
            og_tok = qs_fm
            og_fm = qkv_h
            wstr = WStream2(la["ws"], "laws")
            kvst = sem("lakvst"); kvld = sem("lakvld")
            osf = [sem("laosf0"), sem("laosf1")]
            xrl = [sem("laxrl0"), sem("laxrl1")]

            i = ppool.next()
            pt = ppool.tiles[i]
            p.waitm(p.t, ppool.freed[i])
            p.inc(p.t.matmul(pt[:, 0:16], com.ones_row[:, :], srow[:, :],
                             start=True, stop=True), ppool.wrote[i])
            p.wait(p.v, ppool.wrote[i], ppool.wrote[i].v)
            p.v.tensor_scalar_mul(negS[:, :], pt[:, 0:16], -1.0)
            ppool.freed[i] = p.mark("v")
            p.waitm(p.s, p.mark("v"))
            for h in range(16):
                nsc = negS[:, h:h + 1]
                for j in range(2):
                    p.s.activation(out=qd_all[:, h * 2 + j: h * 2 + j + 1],
                                   in_=iotaq[:, j:j + 1], func=AF.Exp,
                                   scale=nsc)
                    p.s.activation(out=kd_all[:, h * 2 + j: h * 2 + j + 1],
                                   in_=iotak[:, j:j + 1], func=AF.Exp,
                                   scale=nsc)
                p.s.activation(out=bd_all[:, h:h + 1], in_=com.cc(256.0),
                               func=AF.Exp, scale=nsc)
            for j in range(4):
                p.s.activation(out=bdjs[:, j * 16:(j + 1) * 16],
                               in_=negS[:, :], func=AF.Exp,
                               scale=com.cc(256.0 * j))
            p.waitm(p.v, p.mark("s"))
            for j in range(4):
                p.v.tensor_scalar_mul(bdjs[:, j * 16:(j + 1) * 16],
                                      bdjs[:, j * 16:(j + 1) * 16],
                                      flags[:, 1:2])
            p.v.memset(kv_run[:, :], 0.0)

            ln_transpose_x(p, x_in, xn_fm)

            for half in range(2):
                for nb in range(12):
                    wstr.load_plain(p, 0,
                                    la_wqkv[nb].rearrange("p k n -> p (k n)"),
                                    p.g)
                    wstr.wait_loaded(p, p.t, 0)
                    last = None
                    which = nb // 4
                    col = (nb % 4) * 512
                    for cl in range(4):
                        c = half * 4 + cl
                        mats = [(xn_fm[:, k * T + c * 128:
                                       k * T + (c + 1) * 128],
                                 wstr.bufs[0][:, k * 512:(k + 1) * 512])
                                for k in range(DC)]
                        dst = qkv_h[:, which * 8192 + cl * D + col:
                                    which * 8192 + cl * D + col + 512]

                        def evac(eng, pt, dst=dst):
                            eng.activation(out=dst, in_=pt[:, :], func=AF.Silu)

                        last = mm_group(p, ppool, mats, evac, evac_eng="s")
                    wstr.set_free(p, 0, last)
                m_qkv = p.mark("s")
                p.waitm(p.v, m_qkv)
                p.waitm(p.t, m_qkv)

                for h in range(16):
                    hcol = h * 128
                    nsc = negS[:, h:h + 1]
                    for mcx in range(2):
                        base = lmm[:, mcx * 256:(mcx + 1) * 256]
                        p.waitm(p.v, p.mark("s"))
                        p.v.tensor_scalar(fA[:, 0:256], base, nsc, 0.0,
                                          op0=ALU.mult, op1=ALU.min)
                        p.waitm(p.s, p.mark("v"))
                        p.s.activation(out=fA[:, 256:512], in_=fA[:, 0:256],
                                       func=AF.Exp)
                        p.waitm(p.v, p.mark("s"))
                        p.v.tensor_tensor(
                            ddTh[:, mcx * 256:(mcx + 1) * 256],
                            fA[:, 256:512],
                            trim[:, mcx * 256:(mcx + 1) * 256], ALU.mult)
                    for blh in range(2):
                        bl = half * 2 + blh
                        for j in range(2):
                            cl = blh * 2 + j
                            c = half * 4 + cl
                            src_q = qkv_h[:, cl * D + hcol: cl * D + hcol + 128]
                            src_k = qkv_h[:, 8192 + cl * D + hcol:
                                          8192 + cl * D + hcol + 128]
                            p.v.tensor_scalar_mul(
                                qs_st[:, j * 128:(j + 1) * 128], src_q,
                                qd_all[:, h * 2 + j: h * 2 + j + 1])
                            p.v.tensor_scalar_mul(
                                ks_st[:, j * 128:(j + 1) * 128], src_k,
                                kd_all[:, h * 2 + j: h * 2 + j + 1])
                            mkv = p.mark("v")
                            p.waitm(p.t, mkv)
                            transpose_tile(p, com, tpool, src_q,
                                           qf[:, j * 128:(j + 1) * 128])
                            transpose_tile(p, com, tpool, src_k,
                                           kf[:, j * 128:(j + 1) * 128])
                            transpose_tile(
                                p, com, tpool,
                                qs_st[:, j * 128:(j + 1) * 128],
                                qs_fm[:, h * T + c * 128:
                                      h * T + c * 128 + 128])
                            p.v.tensor_copy(kvb[:, :],
                                            kv_run[:, hcol:hcol + 128]) \
                                if j == 1 else None
                        m_pre = p.mark("v")
                        p.waitm(p.t, m_pre)
                        for mcx in range(2):
                            def evacm(eng, pt, mcx=mcx):
                                eng.tensor_tensor(
                                    M_sb[:, mcx * 256:(mcx + 1) * 256],
                                    pt[:, 0:256],
                                    ddTh[:, mcx * 256:(mcx + 1) * 256],
                                    ALU.mult)

                            mm_group(p, ppool,
                                     [(kf[:, mcx * 128:(mcx + 1) * 128],
                                       qf[:, 0:256])],
                                     evacm, ncols=256)
                        m_M = p.mark("v")
                        p.waitm(p.t, m_M)
                        for lc in range(2):
                            c = bl * 2 + lc
                            mats = [(M_sb[:, mcx * 256 + lc * 128:
                                          mcx * 256 + lc * 128 + 128],
                                     qkv_h[:, 16384 + (blh * 2 + mcx) * D + hcol:
                                           16384 + (blh * 2 + mcx) * D + hcol + 128])
                                    for mcx in range(2)]
                            mats.append((qs_fm[:, h * T + c * 128:
                                               h * T + c * 128 + 128],
                                         kvb[:, :]))
                            dst = o_tok[:, c * D + hcol: c * D + hcol + 128]

                            def evaco(eng, pt, dst=dst):
                                eng.tensor_copy(dst, pt[:, 0:128])

                            mm_group(p, ppool, mats, evaco, ncols=128)
                        mats = [(ks_st[:, mcx * 128:(mcx + 1) * 128],
                                 qkv_h[:, 16384 + (blh * 2 + mcx) * D + hcol:
                                       16384 + (blh * 2 + mcx) * D + hcol + 128])
                                for mcx in range(2)]

                        def evacu(eng, pt, hcol=hcol, h=h):
                            eng.scalar_tensor_tensor(
                                kv_run[:, hcol:hcol + 128],
                                kv_run[:, hcol:hcol + 128],
                                bd_all[:, h:h + 1], pt[:, 0:128],
                                op0=ALU.mult, op1=ALU.add)

                        mm_group(p, ppool, mats, evacu, ncols=128)

            p.v.tensor_scalar_mul(fB[:, :], kv_run[:, :], flags[:, 0:1])
            p.waitm(p.y, p.mark("v"))
            p.dma(p.y, kvst, out=kv_in, in_=fB[:, :])
            p.wait(p.g, kvst)
            cckv = ccsem()
            p.inc_cc(p.g.collective_compute(
                "AllReduce", ALU.add, replica_groups=RG_PAIRS,
                ins=[kv_in], outs=[kv_out]), cckv)
            p.wait(p.y, cckv, 1)
            mkv2 = p.dma(p.y, kvld, out=fB[:, :], in_=kv_out)
            p.waitm(p.v, mkv2)
            p.v.tensor_copy(kvs_bf[:, :], fB[:, :])
            m_seed = p.mark("v")
            p.waitm(p.t, m_seed)

            for bl in range(4):
                for h in range(16):
                    hcol = h * 128
                    for lc in range(2):
                        c = bl * 2 + lc
                        dst = o_tok[:, c * D + hcol: c * D + hcol + 128]

                        def evacc(eng, pt, dst=dst, bl=bl, h=h):
                            eng.scalar_tensor_tensor(
                                dst, pt[:, 0:128],
                                bdjs[:, bl * 16 + h: bl * 16 + h + 1], dst,
                                op0=ALU.mult, op1=ALU.add)

                        mm_group(p, ppool,
                                 [(qs_fm[:, h * T + c * 128:
                                         h * T + c * 128 + 128],
                                   kvs_bf[:, hcol:hcol + 128])],
                                 evacc, ncols=128)

            for c in range(TC):
                p.waitm(p.s, p.mark("v"))
                p.s.activation(out=fB[:, :], in_=o_tok[:, c * D:(c + 1) * D],
                               func=AF.Square)
                p.waitm(p.v, p.mark("s"))
                p.v.reduce_sum(scr["m"][:, :], fB[:, :], axis=AX.X)
                p.v.tensor_scalar_mul(scr["m"][:, :], scr["m"][:, :],
                                      1.0 / 2048.0)
                p.waitm(p.s, p.mark("v"))
                p.s.activation(out=scr["t1"][:, :], in_=scr["m"][:, :],
                               func=AF.Sqrt, bias=com.cc(1e-6))
                p.waitm(p.v, p.mark("s"))
                p.v.reciprocal(rstdc[:, c:c + 1], scr["t1"][:, :])

            for nb in range(4):
                wstr.load_plain(p, 0, la_wg[nb].rearrange("p k n -> p (k n)"),
                                p.g)
                wstr.wait_loaded(p, p.t, 0)
                last = None
                for c in range(TC):
                    mats = [(xn_fm[:, k * T + c * 128: k * T + (c + 1) * 128],
                             wstr.bufs[0][:, k * 512:(k + 1) * 512])
                            for k in range(DC)]

                    def evacg(eng, pt):
                        p.waitm(eng, p.mark("v"))
                        eng.activation(out=sil[:, :], in_=pt[:, :],
                                       func=AF.Sigmoid)

                    last = mm_group(p, ppool, mats, evacg, evac_eng="s")
                    p.waitm(p.v, p.mark("s"))
                    dst = og_tok[:, c * D + nb * 512: c * D + (nb + 1) * 512]
                    p.v.scalar_tensor_tensor(
                        dst, o_tok[:, c * D + nb * 512: c * D + (nb + 1) * 512],
                        rstdc[:, c:c + 1], sil[:, :],
                        op0=ALU.mult, op1=ALU.mult)
                wstr.set_free(p, 0, last)

            p.waitm(p.t, p.mark("v"))
            for c in range(TC):
                for fc in range(DC):
                    i2 = tpool.next()
                    pt2 = tpool.tiles[i2]
                    w2 = tpool.wrote[i2]
                    p.waitm(p.t, tpool.freed[i2])
                    p.inc(p.t.transpose(
                        pt2[:, 0:128],
                        og_tok[:, c * D + fc * 128: c * D + (fc + 1) * 128],
                        com.ident[:, :]), w2)
                    p.wait(p.v, w2, w2.v)
                    p.v.tensor_scalar_mul(
                        og_fm[:, fc * T + c * 128: fc * T + (c + 1) * 128],
                        pt2[:, 0:128], rmsw[:, fc:fc + 1])
                    tpool.freed[i2] = p.mark("v")
            p.waitm(p.t, p.mark("v"))

            free_markers = [None, None]
            xres_free = [None, None]
            for nb in range(4):
                wstr.load_plain(p, 0, la_wo[nb].rearrange("p k n -> p (k n)"),
                                p.g)
                wstr.wait_loaded(p, p.t, 0)
                last = None
                for c in range(TC):
                    mats = [(og_fm[:, k * T + c * 128: k * T + (c + 1) * 128],
                             wstr.bufs[0][:, k * 512:(k + 1) * 512])
                            for k in range(DC)]
                    ob = (nb * TC + c) % 2
                    p.waitm(p.y, xres_free[ob])
                    mxr = p.dma(
                        p.y, xrl[ob], out=xres[ob][:, :],
                        in_=x_in[c * 128:(c + 1) * 128,
                                 nb * 512:(nb + 1) * 512])

                    def evacf(eng, pt, ob=ob, mxr=mxr):
                        p.waitm(eng, free_markers[ob])
                        p.waitm(eng, mxr)
                        eng.tensor_add(outst[ob][:, :], xres[ob][:, :],
                                       pt[:, :])

                    last = mm_group(p, ppool, mats, evacf)
                    xres_free[ob] = p.mark("v")
                    p.waitm(p.y, xres_free[ob])
                    free_markers[ob] = p.dma(
                        p.y, osf[ob],
                        out=x_out[c * 128:(c + 1) * 128,
                                  nb * 512:(nb + 1) * 512],
                        in_=outst[ob][:, :])
                wstr.set_free(p, 0, last)
            p.wait(p.y, osf[0])
            p.wait(p.y, osf[1])

        def emit_l1ab(p, x_in, x_out):
            xn_fm = lb["xn_fm"]; q_fm = lb["q_fm"]; kg = lb["kg"]
            vg = lb["vg"]; o_tok = lb["o_tok"]; P_sb = lb["P_sb"]
            PT_sb = lb["PT_sb"]; mask_sb = lb["mask_sb"]
            st2 = lb["st2"]; outst = lb["outst"]; xres = lb["xres"]
            S_sb = fA
            o_fm = xn_fm
            SCALE = 1.0 / math.sqrt(128.0)
            wstr = WStream2(lb["ws"], "lbws")
            wstr_l = WStream2(lb["wl"], "lbwl")
            mld_s = sem("lbmld")
            kst = [sem("lbkst0"), sem("lbkst1")]
            kld = sem("lbkld"); vld = sem("lbvld")
            osf = [sem("lbosf0"), sem("lbosf1")]
            xrl = [sem("lbxrl0"), sem("lbxrl1")]
            ccs = [ccsem() for _ in range(6)]

            ln_transpose_x(p, x_in, xn_fm)

            for which, w_d in (("q", ab_wq), ("k", ab_wk)):
                for h in range(16):
                    slot = h % 2
                    wstr_l.load_plain(p, slot,
                                      w_d[h].rearrange("p k n -> p (k n)"),
                                      p.g)
                    wstr_l.wait_loaded(p, p.t, slot)
                    last = None
                    for sbk in range(2):
                        mats = [(wstr_l.bufs[slot][:, k * 128:(k + 1) * 128],
                                 xn_fm[:, k * T + sbk * 512:
                                       k * T + sbk * 512 + 512])
                                for k in range(DC)]
                        if which == "q":
                            dst = q_fm[:, h * T + sbk * 512:
                                       h * T + sbk * 512 + 512]

                            def evac(eng, pt, dst=dst):
                                eng.tensor_scalar_mul(dst, pt[:, :], SCALE)

                            last = mm_group(p, ppool, mats, evac)
                        else:
                            def evac(eng, pt, sbk=sbk):
                                p.waitm(eng, (kst[0], kst[0].v))
                                p.waitm(eng, (kst[1], kst[1].v))
                                eng.tensor_scalar_mul(st2[0][:, :], pt[:, :],
                                                      flags[:, 0:1])
                                eng.tensor_scalar_mul(st2[1][:, :], pt[:, :],
                                                      flags[:, 1:2])

                            last = mm_group(p, ppool, mats, evac)
                            p.waitm(p.y, p.mark("v"))
                            p.dma(p.y, kst[0],
                                  out=ck_in[h, :, sbk * 512: sbk * 512 + 512],
                                  in_=st2[0][:, :])
                            p.dma(p.y, kst[1],
                                  out=ck_in[h, :, 1024 + sbk * 512:
                                            1024 + sbk * 512 + 512],
                                  in_=st2[1][:, :])
                    wstr_l.set_free(p, slot, last)

            for nb in range(4):
                slot = 0
                wstr.load_plain(p, slot,
                                ab_wv[nb].rearrange("p k n -> p (k n)"), p.g)
                wstr.wait_loaded(p, p.t, slot)
                last = None
                for c in range(TC):
                    mats = [(xn_fm[:, k * T + c * 128: k * T + (c + 1) * 128],
                             wstr.bufs[slot][:, k * 512:(k + 1) * 512])
                            for k in range(DC)]

                    def evac(eng, pt):
                        p.waitm(eng, (kst[0], kst[0].v))
                        p.waitm(eng, (kst[1], kst[1].v))
                        eng.tensor_scalar_mul(st2[0][:, :], pt[:, :],
                                              flags[:, 0:1])
                        eng.tensor_scalar_mul(st2[1][:, :], pt[:, :],
                                              flags[:, 1:2])

                    last = mm_group(p, ppool, mats, evac)
                    p.waitm(p.y, p.mark("v"))
                    eh = nb // 2
                    ec = (nb % 2) * 512
                    p.dma(p.y, kst[0],
                          out=cv_in[eh, c * 128:(c + 1) * 128, ec:ec + 512],
                          in_=st2[0][:, :])
                    p.dma(p.y, kst[1],
                          out=cv_in[eh, 1024 + c * 128: 1024 + (c + 1) * 128,
                                    ec:ec + 512],
                          in_=st2[1][:, :])
                wstr.set_free(p, slot, last)

            p.wait(p.g, kst[0])
            p.wait(p.g, kst[1])
            for j in range(4):
                p.inc_cc(p.g.collective_compute(
                    "AllReduce", ALU.add, replica_groups=RG_PAIRS,
                    ins=[ck_in[j * 4:(j + 1) * 4].rearrange(
                        "h p s -> (h p) s")],
                    outs=[ck_out[j * 4:(j + 1) * 4].rearrange(
                        "h p s -> (h p) s")]), ccs[j])
            for j in range(2):
                p.inc_cc(p.g.collective_compute(
                    "AllReduce", ALU.add, replica_groups=RG_PAIRS,
                    ins=[cv_in[j]], outs=[cv_out[j]]), ccs[4 + j])

            for grp in range(4):
                p.wait(p.y, ccs[grp], 1)
                p.wait(p.y, ccs[4 + grp // 2], 1)
                if grp > 0:
                    for w_ in ppool.wrote:
                        if w_.v > 0:
                            p.wait(p.y, w_, w_.v)
                mkld = p.dma(p.y, kld,
                             out=kg[:, :].rearrange("p (h s) -> p h s", h=4),
                             in_=ck_out[grp * 4:(grp + 1) * 4].rearrange(
                                 "h p s -> p h s"))
                mvld = p.dma(p.y, vld,
                             out=vg[:, :].rearrange("p (kc e) -> p kc e",
                                                    kc=16),
                             in_=cv_out[grp // 2][:, (grp % 2) * 512:
                                                  (grp % 2) * 512 + 512].rearrange(
                                 "(kc p) e -> p kc e", p=128))
                p.waitm(p.t, mkld)
                p.waitm(p.t, mvld)
                for c in range(TC):
                    p.waitm(p.y, p.mark("v"))
                    mm = p.dma(p.y, mld_s, out=mask_sb[:, :],
                               in_=mask_d[c * 128:(c + 1) * 128, :])
                    p.waitm(p.v, mm)
                    for h8 in range(4):
                        h = grp * 4 + h8
                        sslots = []
                        for kb in range(4):
                            i3 = ppool.next()
                            pt3 = ppool.tiles[i3]
                            w3 = ppool.wrote[i3]
                            p.waitm(p.t, ppool.freed[i3])
                            p.inc(p.t.matmul(
                                pt3[:, :],
                                q_fm[:, h * T + c * 128:
                                     h * T + (c + 1) * 128],
                                kg[:, h8 * 2048 + kb * 512:
                                   h8 * 2048 + (kb + 1) * 512],
                                start=True, stop=True), w3)
                            sslots.append(i3)
                        for kb, i3 in enumerate(sslots):
                            p.wait(p.v, ppool.wrote[i3], ppool.wrote[i3].v)
                            p.v.tensor_tensor(
                                S_sb[:, kb * 512:(kb + 1) * 512],
                                ppool.tiles[i3][:, :],
                                mask_sb[:, kb * 512:(kb + 1) * 512], ALU.add)
                            ppool.freed[i3] = p.mark("v")
                        p.v.reduce_max(scr["m"][:, :], S_sb[:, :], axis=AX.X,
                                       negate=True)
                        p.waitm(p.s, p.mark("v"))
                        p.s.activation(out=P_sb[:, :], in_=S_sb[:, :],
                                       func=AF.Exp, bias=scr["m"][:, :])
                        p.waitm(p.v, p.mark("s"))
                        p.v.reduce_sum(scr["q"][:, :], P_sb[:, :], axis=AX.X)
                        p.v.reciprocal(scr["r"][:, :], scr["q"][:, :])
                        mkp = p.mark("v")
                        p.waitm(p.t, mkp)
                        for kc in range(16):
                            transpose_tile(
                                p, com, tpool,
                                P_sb[:, kc * 128:(kc + 1) * 128],
                                PT_sb[:, kc * 128:(kc + 1) * 128])
                        p.waitm(p.t, p.mark("v"))
                        mats = [(PT_sb[:, kc * 128:(kc + 1) * 128],
                                 vg[:, kc * 512 + h8 * 128:
                                    kc * 512 + (h8 + 1) * 128])
                                for kc in range(16)]
                        dst = o_tok[:, c * D + h * 128: c * D + (h + 1) * 128]

                        def evac(eng, pt2, dst=dst):
                            eng.tensor_scalar_mul(dst, pt2[:, 0:128],
                                                  scr["r"][:, :])

                        mm_group(p, ppool, mats, evac, ncols=128)

            m_o = p.mark("v")
            p.waitm(p.t, m_o)
            for c in range(TC):
                for fc in range(DC):
                    transpose_tile(
                        p, com, tpool,
                        o_tok[:, c * D + fc * 128: c * D + (fc + 1) * 128],
                        o_fm[:, fc * T + c * 128: fc * T + (c + 1) * 128])
            p.waitm(p.t, p.mark("v"))
            free_markers = [None, None]
            xres_free = [None, None]
            for nb in range(4):
                slot = 0
                wstr.load_plain(p, slot,
                                ab_wo[nb].rearrange("p k n -> p (k n)"), p.g)
                wstr.wait_loaded(p, p.t, slot)
                last = None
                for c in range(TC):
                    mats = [(o_fm[:, k * T + c * 128: k * T + (c + 1) * 128],
                             wstr.bufs[slot][:, k * 512:(k + 1) * 512])
                            for k in range(DC)]
                    ob = (nb * TC + c) % 2
                    p.waitm(p.y, xres_free[ob])
                    mxr = p.dma(
                        p.y, xrl[ob], out=xres[ob][:, :],
                        in_=x_in[c * 128:(c + 1) * 128,
                                 nb * 512:(nb + 1) * 512])

                    def evacf(eng, pt, ob=ob, mxr=mxr):
                        p.waitm(eng, free_markers[ob])
                        p.waitm(eng, mxr)
                        eng.tensor_add(outst[ob][:, :], xres[ob][:, :],
                                       pt[:, :])

                    last = mm_group(p, ppool, mats, evacf)
                    xres_free[ob] = p.mark("v")
                    p.waitm(p.y, xres_free[ob])
                    free_markers[ob] = p.dma(
                        p.y, osf[ob],
                        out=x_out[c * 128:(c + 1) * 128,
                                  nb * 512:(nb + 1) * 512],
                        in_=outst[ob][:, :])
                wstr.set_free(p, slot, last)
            p.wait(p.y, osf[0])
            p.wait(p.y, osf[1])

        run_walk(nc, block, sems, walk,
                 resettables=[ppool, tpool, tpoolf])
    return nc



def _shard_x(x):
    out = []
    for b in range(4):
        for h in range(2):
            out.append(np.ascontiguousarray(x[b, h * T:(h + 1) * T, :]))
    return out


def _unshard_x(shards):
    x = np.empty((4, 2048, 2048), np.float32)
    for b in range(4):
        for h in range(2):
            x[b, h * T:(h + 1) * T, :] = shards[2 * b + h]
    return x


def _run(nc, in_maps, out_key):
    res = run_bass_kernel_spmd(nc, in_maps, core_ids=list(range(NCORES)))
    return [np.asarray(res.results[i][out_key]) for i in range(NCORES)]


def run_l0aa(x_shards, Wq, Wk, Wv, Wo, stop_after=None):
    nc = build_l0aa(stop_after=stop_after)
    wq_p = rhs_pack(Wq)
    wk_p = rhs_pack(Wk)
    wv_p = lhsT_pack(Wv)
    wo_p = rhs_pack(Wo)
    in_maps = [dict(x=x_shards[i], wq_p=wq_p, wk_p=wk_p, wv_p=wv_p, wo_p=wo_p)
               for i in range(NCORES)]
    return _run(nc, in_maps, "xo")


def kernel_fused(**inputs):
    f32 = lambda a: np.ascontiguousarray(np.asarray(a, np.float32))
    x = f32(inputs["L0_attn_a_x"])
    shards = _shard_x(x)
    nc = build_fused()
    iotaq, iotak, lmm, trim = l1aa_consts()
    common = dict(
        aa_wq=rhs_pack(f32(inputs["L0_attn_a_Wq"])),
        aa_wk=rhs_pack(f32(inputs["L0_attn_a_Wk"])),
        aa_wv=lhsT_pack(f32(inputs["L0_attn_a_Wv"])),
        aa_wo=rhs_pack(f32(inputs["L0_attn_a_Wo"])),
        wfcT_f=np.ascontiguousarray(f32(inputs["L0_ffn_w_fc"]).T),
        wprT_f=np.ascontiguousarray(f32(inputs["L0_ffn_w_proj"]).T),
        winT_f=np.ascontiguousarray(f32(inputs["L0_attn_b_W_in"]).T),
        wouT_f=np.ascontiguousarray(f32(inputs["L0_attn_b_W_out"]).T),
        wgaT_f=np.ascontiguousarray(f32(inputs["L1_ffn_w_gate"]).T),
        wupT_f=np.ascontiguousarray(f32(inputs["L1_ffn_w_up"]).T),
        wdoT_f=np.ascontiguousarray(f32(inputs["L1_ffn_w_down"]).T),
        wconv_p=np.ascontiguousarray(
            f32(inputs["L0_attn_b_W_conv"])[:, 0, :].reshape(33, 128, 4)
            .transpose(1, 0, 2).reshape(128, 132)),
        bconv_p=np.ascontiguousarray(
            f32(inputs["L0_attn_b_b_conv"]).reshape(33, 128).T),
        dtb_r=f32(inputs["L0_attn_b_dt_bias"]).reshape(1, 32),
        alog_r=f32(inputs["L0_attn_b_A_log"]).reshape(1, 32),
        dco_r=f32(inputs["L0_attn_b_D"]).reshape(1, 32),
        normw_p=col_pack(inputs["L0_attn_b_norm_w"]),
        ucum=l0ab_consts(),
        la_wqkv=rhs_pack(f32(inputs["L1_attn_a_w_qkv"])),
        la_wg=rhs_pack(f32(inputs["L1_attn_a_w_gate"])),
        la_wo=rhs_pack(f32(inputs["L1_attn_a_w_out"])),
        slope_r=f32(inputs["L1_attn_a_slope_rate"]).reshape(1, 16),
        rms_p=col_pack(inputs["L1_attn_a_rms_weight"]),
        iotaq=iotaq, iotak=iotak, lmm=lmm, trim=trim,
        ab_wq=lhsT_pack(f32(inputs["L1_attn_b_Wq"])),
        ab_wk=lhsT_pack(f32(inputs["L1_attn_b_Wk"])),
        ab_wv=rhs_pack(f32(inputs["L1_attn_b_Wv"])),
        ab_wo=rhs_pack(f32(inputs["L1_attn_b_Wo"])),
    )
    in_maps = []
    for i in range(NCORES):
        m = dict(common)
        m["x"] = shards[i]
        m["flags"] = make_flags(i % 2)
        m["mask"] = make_l1ab_mask(i % 2)
        m["wfc_sl"] = np.ascontiguousarray(
            common["wfcT_f"].T[i * 1024:(i + 1) * 1024, :])
        m["wpr_sl"] = np.ascontiguousarray(
            common["wprT_f"].T[i * 256:(i + 1) * 256, :])
        in_maps.append(m)
    out = _run(nc, in_maps, "xo")
    return _unshard_x(out)


def kernel(**inputs):
    return kernel_fused(**inputs)


def kernel_unfused(**inputs):
    f32 = lambda a: np.ascontiguousarray(np.asarray(a, np.float32))
    x = f32(inputs["L0_attn_a_x"])
    shards = _shard_x(x)
    prep = run_prep(inputs)
    shards = run_l0aa(shards,
                      f32(inputs["L0_attn_a_Wq"]), f32(inputs["L0_attn_a_Wk"]),
                      f32(inputs["L0_attn_a_Wv"]), f32(inputs["L0_attn_a_Wo"]))
    shards = run_l0ffn(shards, prep["wfcT"], prep["wprT"], prep["gam"])
    shards = run_l0ab(shards, prep["winT"], prep["wouT"],
                      f32(inputs["L0_attn_b_W_conv"]),
                      f32(inputs["L0_attn_b_b_conv"]),
                      f32(inputs["L0_attn_b_dt_bias"]),
                      f32(inputs["L0_attn_b_A_log"]),
                      f32(inputs["L0_attn_b_D"]))
    shards = run_l0ffn(shards, prep["wfcT"], prep["wprT"], prep["gam"])
    shards = run_l1aa(shards, f32(inputs["L1_attn_a_slope_rate"]),
                      f32(inputs["L1_attn_a_w_qkv"]),
                      f32(inputs["L1_attn_a_w_gate"]),
                      f32(inputs["L1_attn_a_w_out"]),
                      f32(inputs["L1_attn_a_rms_weight"]))
    shards = run_l1ffn(shards, prep["wgaT"], prep["wupT"], prep["wdoT"])
    shards = run_l1ab(shards,
                      f32(inputs["L1_attn_b_Wq"]), f32(inputs["L1_attn_b_Wk"]),
                      f32(inputs["L1_attn_b_Wv"]), f32(inputs["L1_attn_b_Wo"]))
    shards = run_l1ffn(shards, prep["wgaT"], prep["wupT"], prep["wdoT"])
    return _unshard_x(shards)
